# version 08
# baseline (speedup 1.0000x reference)
import ctypes
import numpy as np
from contextlib import ExitStack

import concourse.bass as bass
import concourse.mybir as mybir
from concourse.bass_utils import run_bass_kernel_spmd
from concourse.aot_env import allow_ici_all_to_all

F32 = mybir.dt.float32
AF = mybir.ActivationFunctionType
ALU = mybir.AluOpType
AX = mybir.AxisListType

B, T, C, H, D, II = 4, 4096, 2048, 16, 128, 8192
TT = 512
NTT = 4
HLOC = 8
NKC = C // 128
NMC = II // 128
NMG = NMC // 4
EPS = 1e-5
SCALE = float(np.float32(np.sqrt(np.float64(D))))
RSCALE = 1.0 / SCALE
NEG = -1.0e30


class Sem:
    __slots__ = ("name", "h", "n")

    def __init__(self, name):
        self.name, self.h, self.n = name, None, 0


class Op:
    __slots__ = ("eng", "waits", "emit", "marks")

    def __init__(self, eng, waits, emit, marks):
        self.eng, self.waits, self.emit, self.marks = eng, waits, emit, marks


class Prog:
    def __init__(self):
        self.ops = []
        self.sems = []

    def sem(self, name):
        s = Sem(name)
        self.sems.append(s)
        return s

    def do(self, eng, emit, waits=(), marks=()):
        evs = []
        for s, inc in marks:
            s.n += inc
            evs.append((s, s.n))
        self.ops.append(Op(eng, tuple(w for w in waits if w is not None),
                           emit, tuple(marks)))
        return evs[0] if evs else None

    def snap(self, *sems):
        return [(s, s.n) for s in sems if s.n > 0]


def _rope_tables():
    libm = ctypes.CDLL("libm.so.6")
    libm.cosf.restype = ctypes.c_float
    libm.cosf.argtypes = [ctypes.c_float]
    libm.sinf.restype = ctypes.c_float
    libm.sinf.argtypes = [ctypes.c_float]
    libm.powf.restype = ctypes.c_float
    libm.powf.argtypes = [ctypes.c_float, ctypes.c_float]
    expo = (np.arange(0, D, 2, dtype=np.float32) / np.float32(D)).astype(np.float32)
    inv = np.array([np.float32(1.0) / np.float32(libm.powf(ctypes.c_float(10000.0),
                                                           ctypes.c_float(float(e))))
                    for e in expo], dtype=np.float32)
    freqs = (np.arange(T, dtype=np.float32)[:, None] * inv[None, :]).astype(np.float32)
    cos = np.array([libm.cosf(ctypes.c_float(float(v))) for v in freqs.ravel()],
                   dtype=np.float32).reshape(freqs.shape)
    sin = np.array([libm.sinf(ctypes.c_float(float(v))) for v in freqs.ravel()],
                   dtype=np.float32).reshape(freqs.shape)
    return cos, sin


def _perm_I():
    new = np.arange(II)
    p = new % 128
    ch = new // 128
    a, k = ch // 4, ch % 4
    return 4 * (16 * p + a) + k


_XSQB = [None, None]


def build_nc(debug=False):
    nc = bass.Bass("TRN2", target_bir_lowering=False)
    P = Prog()
    PE, VE, SE, CC = (P.sem(n) for n in ["PE", "VE", "SE", "CC"])
    S_WG = [P.sem(f"sWG{i}") for i in range(2)]
    S_WU = [P.sem(f"sWU{i}") for i in range(2)]
    S_WD = [P.sem(f"sWD{i}") for i in range(2)]
    S_QV = [P.sem(f"sQV{i}") for i in range(2)]
    S_ST = [P.sem(f"sST{i}") for i in range(2)]
    S_GB = [P.sem(f"sGB{i}") for i in range(2)]
    S_OT = [P.sem(f"sOT{i}") for i in range(2)]
    S_XC = [P.sem(f"sXC{i}") for i in range(2)]
    S_XL = P.sem("sXL")
    S_SW = P.sem("sSW")
    S_GS = P.sem("sGS")
    S_CM = P.sem("sCM")
    S_QK = P.sem("sQK")
    S_VK = P.sem("sVK")
    S_B = P.sem("sB")
    S_C = P.sem("sC")
    S_CW = P.sem("sCW")
    DMAS = (S_WG + S_WU + S_WD + S_QV + S_ST + S_GB + S_OT + S_XC +
            [S_XL, S_SW, S_GS, S_CM, S_QK, S_VK, S_B, S_C, S_CW])

    def dram_in(name, shape):
        return nc.dram_tensor(name, list(shape), F32, kind="ExternalInput").ap()

    xin = dram_in("xin", (NTT, 128, NKC * TT))
    wgt = dram_in("wgt", (NMC, 128, NKC * 128))
    wut = dram_in("wut", (NMC, 128, NKC * 128))
    wdt = dram_in("wdt", (NMG, 4, 128, 2048))
    wqt = dram_in("wqt", (HLOC, 128, NKC * 128))
    wkt = dram_in("wkt", (HLOC, 128, NKC * 128))
    wvt = dram_in("wvt", (HLOC, 128, NKC * 128))
    wsmt = dram_in("wsmt", (128, NKC * 32))
    wft = dram_in("wft", (128, NKC * 8))
    wot = dram_in("wot", (NKC, 128, NKC * 128))
    nbf_d = dram_in("nbf", (8, 1))
    cost_d = dram_in("costab", (128, T))
    sint_d = dram_in("sintab", (128, T))
    caus_d = dram_in("causal_d", (128, 128))
    selm_d = dram_in("selm", (128, 2))
    iden_d = dram_in("ident_d", (128, 128))

    x1m = nc.dram_tensor("x1m", [NTT, 128, NKC * TT], F32).ap()
    x1f = nc.dram_tensor("x1f", [2, NTT, 128, NKC * TT], F32).ap()
    x2m = nc.dram_tensor("x2m", [NTT, 128, NKC * TT], F32).ap()
    qTs = nc.dram_tensor("qTs", [HLOC, 128, T], F32).ap()
    kTs = nc.dram_tensor("kTs", [HLOC, 128, T], F32).ap()
    vKM = nc.dram_tensor("vKM", [HLOC, T, 128], F32).ap()
    cums = nc.dram_tensor("cums", [HLOC, T], F32).ap()
    gsc = nc.dram_tensor("gsc", [32, TT], F32).ap()
    oTm = nc.dram_tensor("oTm", [2, C, T // 2], F32).ap()
    oTr = nc.dram_tensor("oTr", [C, T // 2], F32).ap()
    outx = nc.dram_tensor("outx", [NTT, 128, NKC * TT], F32, kind="ExternalOutput").ap()
    d_u1 = nc.dram_tensor("d_u1", [128, NKC * TT], F32, kind="ExternalOutput").ap() if debug else None
    d_h4 = nc.dram_tensor("d_h4", [128, 4 * TT], F32, kind="ExternalOutput").ap() if debug else None
    d_hpre = nc.dram_tensor("d_hpre", [128, 4 * TT], F32, kind="ExternalOutput").ap() if debug else None
    d_acc = nc.dram_tensor("d_acc", [128, NKC * TT], F32, kind="ExternalOutput").ap() if debug else None
    d_prow = nc.dram_tensor("d_prow", [4, 128, 512], F32, kind="ExternalOutput").ap() if debug else None
    d_pTg = nc.dram_tensor("d_pTg", [128, 4 * 512], F32, kind="ExternalOutput").ap() if debug else None
    dbg = {}
    if debug:
        for nm, src in [("d_x1m", x1m), ("d_qTs", qTs), ("d_kTs", kTs),
                        ("d_vKM", vKM), ("d_cums", cums), ("d_oTm", oTm),
                        ("d_x2m", x2m), ("d_x1f", x1f), ("d_oTr", oTr)]:
            dbg[nm] = (nc.dram_tensor(nm, list(src.tensor.shape), F32,
                                      kind="ExternalOutput").ap(), src)

    st = ExitStack()
    sbuf = lambda name, shape: st.enter_context(nc.sbuf_tensor(name, list(shape), F32))
    psum = lambda name, shape: st.enter_context(nc.psum_tensor(name, list(shape), F32))

    with st:
        ones1 = sbuf("ones1", (1, 128))
        onesK = sbuf("onesK", (128, 1))
        ident = sbuf("ident", (128, 128))
        causal = sbuf("causal", (128, 128))
        nbf_t = sbuf("nbf_t", (8, 1))
        selm = sbuf("selm_t", (128, 2))
        srow = sbuf("srow", (1, 5 * TT))
        epsb = sbuf("epsb", (1, 1))

        ps = [psum(f"ps{i}", (128, 512)) for i in range(6)]
        psT = psum("psT", (64, 512))

        for s in P.sems:
            s.h = st.enter_context(nc.semaphore(s.name))

        ev_ones = P.do("ve", lambda e: e.memset(ones1[:], 1.0), marks=[(VE, 1)])
        ev_eps = P.do("ve", lambda e: e.memset(epsb[:], EPS), marks=[(VE, 1)])
        ev_onesK = P.do("ve", lambda e: e.memset(onesK[:], 1.0), marks=[(VE, 1)])
        for dst, src in [(ident, iden_d), (causal, caus_d), (nbf_t, nbf_d),
                         (selm, selm_d)]:
            P.do("ge", lambda e, d=dst, s=src: e.dma_start(out=d[:], in_=s[:, :]),
                 marks=[(S_C, 16)])
        ev_const = P.snap(S_C, VE)

        def emit_ln(x_src_ap, xt, ut, w0):
            xq0, xq1 = _XSQB[0], _XSQB[1]
            xsqb = (xq0, xq1)
            ev_x = P.do("ge", lambda e: e.dma_start(out=xt[:], in_=x_src_ap),
                        waits=list(w0) + ev_const, marks=[(S_XL, 16)])
            ev_sum = None
            for kc in range(NKC):
                sl = slice(kc * TT, (kc + 1) * TT)
                ev_sum = P.do("te", lambda e, sl=sl, kc=kc: e.matmul(
                    psT[0:1, :], onesK[:, 0:1], xt[:, sl],
                    start=(kc == 0), stop=(kc == NKC - 1)),
                    waits=(list(w0) + [ev_x, ev_onesK] if kc == 0 else []),
                    marks=([(PE, 1)] if kc == NKC - 1 else []))
            prev_mm = None
            ev_sq = None
            for kc in range(NKC):
                sl = slice(kc * TT, (kc + 1) * TT)
                par = kc % 2
                es = P.do("se", lambda e, sl=sl, xq=xsqb[par]: e.activation(
                    xq[:], xt[:, sl], AF.Square),
                    waits=(list(w0) + [ev_x] if kc < 2 else [prev_mm]),
                    marks=[(SE, 1)])
                prev_mm = P.do("te", lambda e, kc=kc, xq=xsqb[par]: e.matmul(
                    psT[32:33, :], onesK[:, 0:1], xq[:],
                    start=(kc == 0), stop=(kc == NKC - 1)),
                    waits=[es], marks=[(PE, 1)])
                if kc == NKC - 1:
                    ev_sq = prev_mm
            e1 = P.do("ve", lambda e: e.tensor_scalar_mul(
                srow[0:1, 0:TT], psT[0:1, :], 1.0 / C),
                waits=list(w0) + [ev_sum], marks=[(VE, 1)])
            e2 = P.do("ve", lambda e: e.tensor_mul(
                srow[0:1, 2 * TT:3 * TT], srow[0:1, 0:TT], srow[0:1, 0:TT]),
                waits=[e1], marks=[(VE, 1)])
            e3 = P.do("ve", lambda e: e.scalar_tensor_tensor(
                srow[0:1, 3 * TT:4 * TT], psT[32:33, :], 1.0 / C,
                srow[0:1, 2 * TT:3 * TT], op0=ALU.mult, op1=ALU.subtract),
                waits=[ev_sq, e2], marks=[(VE, 1)])
            e4 = P.do("se", lambda e: e.activation(
                srow[0:1, 4 * TT:5 * TT], srow[0:1, 3 * TT:4 * TT], AF.Sqrt,
                bias=epsb[0:1, 0:1]), waits=[e3, ev_eps], marks=[(SE, 1)])
            eb1 = P.do("te", lambda e: e.matmul(
                ps[4][:], ones1[0:1, :], srow[0:1, 0:TT], start=True, stop=True),
                waits=[e1, ev_ones], marks=[(PE, 1)])
            e4r = P.do("ve", lambda e: e.reciprocal(
                srow[0:1, TT:2 * TT], srow[0:1, 4 * TT:5 * TT]),
                waits=[e4], marks=[(VE, 1)])
            eb2 = P.do("te", lambda e: e.matmul(
                ps[5][:], ones1[0:1, :], srow[0:1, TT:2 * TT],
                start=True, stop=True), waits=[e4r], marks=[(PE, 1)])
            ev_u = None
            for kc in range(NKC):
                sl = slice(kc * TT, (kc + 1) * TT)
                eu1 = P.do("ve", lambda e, sl=sl: e.tensor_sub(
                    ut[:, sl], xt[:, sl], ps[4][:]),
                    waits=([eb1] if kc == 0 else []), marks=[(VE, 1)])
                ev_u = P.do("ve", lambda e, sl=sl: e.tensor_mul(
                    ut[:, sl], ut[:, sl], ps[5][:]),
                    waits=([eb2, eu1] if kc == 0 else [eu1]), marks=[(VE, 1)])
            return ev_x, ev_u, (e1, e3)

        def emit_ffn(x_src, dst, phase_w0, pfx):
            es = ExitStack()
            sb2 = lambda name, shape: es.enter_context(
                nc.sbuf_tensor(pfx + name, list(shape), F32))
            with es:
                xt = sb2("xt", (128, NKC * TT))
                ut = sb2("ut", (128, NKC * TT))
                acc = sb2("acc", (128, NKC * TT))
                h4 = sb2("h4", (128, 4 * TT))
                habs = sb2("habs", (128, 4 * TT))
                cmp6 = sb2("cmp6", (128, 6 * TT))
                tsc = sb2("tsc", (128, TT))
                _XSQB[0] = sb2("xsq0", (128, TT))
                _XSQB[1] = sb2("xsq1", (128, TT))
                wgb = [sb2(f"wgb{i}", (128, NKC * 128)) for i in range(2)]
                wub = [sb2(f"wub{i}", (128, NKC * 128)) for i in range(2)]
                wdb = [sb2(f"wdb{i}", (128, 2048)) for i in range(2)]
                x1t = [sb2(f"x1t{i}", (128, TT)) for i in range(2)]

                for tt in range(NTT):
                    w0 = list(phase_w0) if tt == 0 else P.snap(PE, VE, SE, *DMAS)
                    ev_x, ev_u, _ = emit_ln(x_src[tt], xt, ut, w0)
                    ev_w = {}
                    ev_wd = {}
                    ev_g_done = {}
                    ev_up_done = {}
                    ev_sil = {}
                    ev_mul = {}
                    ev_dn_done = {}
                    ev_acc_w = {}
                    ev_ps_free = {}
                    ev_x1_st = {}

                    def pf_gu(mc, w=()):
                        par = mc % 2
                        ev_w[mc] = (
                            P.do("sy", lambda e, mc=mc, par=par: e.dma_start(
                                out=wgb[par][:], in_=wgt[mc]),
                                waits=list(w) + [ev_g_done.get(mc - 2)],
                                marks=[(S_WG[par], 16)]),
                            P.do("sy", lambda e, mc=mc, par=par: e.dma_start(
                                out=wub[par][:], in_=wut[mc]),
                                waits=list(w) + [ev_up_done.get(mc - 2)],
                                marks=[(S_WU[par], 16)]))

                    def pf_dn(k, w=()):
                        mg, coq = k // 4, k % 4
                        par = k % 2
                        ev_wd[k] = P.do("sy", lambda e, mg=mg, coq=coq, par=par:
                                        e.dma_start(out=wdb[par][:], in_=wdt[mg, coq]),
                                        waits=list(w) + [ev_dn_done.get(k - 2)],
                                        marks=[(S_WD[par], 16)])

                    pf_gu(0, w0)
                    pf_gu(1, w0)
                    pf_dn(0, w0)
                    pf_dn(1, w0)
                    prev_em3 = None
                    pe_after_down = None
                    for mg in range(NMG):
                        ev_h = {}
                        for i in range(4):
                            mc = mg * 4 + i
                            par = mc % 2
                            last = None
                            for kc in range(NKC):
                                sl = slice(kc * TT, (kc + 1) * TT)
                                wsl = slice(kc * 128, (kc + 1) * 128)
                                last = P.do("te", lambda e, par=par, sl=sl, wsl=wsl,
                                            kc=kc: e.matmul(
                                    ps[par][:], wgb[par][:, wsl], ut[:, sl],
                                    start=(kc == 0), stop=(kc == NKC - 1)),
                                    waits=([ev_w[mc][0], ev_u,
                                            ev_sil.get(mc - 2)] if kc == 0 else []),
                                    marks=([(PE, 1)] if kc == NKC - 1 else []))
                            ev_g = last
                            ev_g_done[mc] = ev_g
                            for kc in range(NKC):
                                sl = slice(kc * TT, (kc + 1) * TT)
                                wsl = slice(kc * 128, (kc + 1) * 128)
                                last = P.do("te", lambda e, par=par, sl=sl, wsl=wsl,
                                            kc=kc: e.matmul(
                                    ps[2 + par][:], wub[par][:, wsl], ut[:, sl],
                                    start=(kc == 0), stop=(kc == NKC - 1)),
                                    waits=([ev_w[mc][1],
                                            ev_mul.get(mc - 2)] if kc == 0 else []),
                                    marks=([(PE, 1)] if kc == NKC - 1 else []))
                            ev_up = last
                            ev_up_done[mc] = ev_up
                            if mc + 2 < NMC:
                                pf_gu(mc + 2)
                            hsl = slice(i * TT, (i + 1) * TT)
                            ev_sil[mc] = P.do(
                                "se", lambda e, hsl=hsl, par=par: e.activation(
                                    h4[:, hsl], ps[par][:], AF.Silu),
                                waits=[ev_g] + ([pe_after_down, prev_em3]
                                                if i == 0 else []),
                                marks=[(SE, 1)])
                            ev_mul[mc] = P.do(
                                "ve", lambda e, hsl=hsl, par=par: e.tensor_tensor(
                                    h4[:, hsl], h4[:, hsl], ps[2 + par][:],
                                    op=ALU.mult),
                                waits=[ev_sil[mc], ev_up], marks=[(VE, 1)])
                            ev_h[i] = ev_mul[mc]
                        A = [h4[:, i * TT:(i + 1) * TT] for i in range(4)]
                        Ab = [habs[:, i * TT:(i + 1) * TT] for i in range(4)]
                        Cp = [cmp6[:, i * TT:(i + 1) * TT] for i in range(6)]
                        eabs = P.do("se", lambda e: e.activation(habs[:], h4[:], AF.Abs),
                                    waits=[ev_h[0], ev_h[1], ev_h[2], ev_h[3],
                                           prev_em3], marks=[(SE, 1)])
                        pairs = [(0, 1), (0, 2), (0, 3), (1, 2), (1, 3), (2, 3)]
                        ecmp = None
                        for ci, (a, b) in enumerate(pairs):
                            ecmp = P.do("ve", lambda e, ci=ci, a=a, b=b:
                                        e.tensor_tensor(Cp[ci], Ab[a], Ab[b],
                                                        op=ALU.is_ge),
                                        waits=([eabs] if ci == 0 else []),
                                        marks=[(VE, 1)])

                        def keep(i, terms, thr, op, w):
                            e = P.do("ve", lambda e, t=terms: e.tensor_tensor(
                                tsc[:], t[0][1], t[1][1],
                                op=(ALU.add if t[1][0] > 0 else ALU.subtract)),
                                waits=w, marks=[(VE, 1)])
                            for sgn, cc in terms[2:]:
                                e = P.do("ve", lambda e, cc=cc, sgn=sgn:
                                         e.tensor_tensor(
                                             tsc[:], tsc[:], cc,
                                             op=(ALU.add if sgn > 0 else ALU.subtract)),
                                         waits=[e], marks=[(VE, 1)])
                            ek = P.do("ve", lambda e, i=i, thr=thr, op=op:
                                      e.tensor_single_scalar(Ab[i], tsc[:], thr, op=op),
                                      waits=[e], marks=[(VE, 1)])
                            return P.do("ve", lambda e, i=i: e.tensor_mul(
                                A[i], A[i], Ab[i]), waits=[ek], marks=[(VE, 1)])

                        em0 = keep(0, [(1, Cp[0]), (1, Cp[1]), (1, Cp[2])], 2.0,
                                   ALU.is_ge, [ecmp])
                        em1 = keep(1, [(1, Cp[3]), (1, Cp[4]), (-1, Cp[0])], 1.0,
                                   ALU.is_ge, [em0])
                        em2 = keep(2, [(1, Cp[5]), (-1, Cp[1]), (-1, Cp[3])], 0.0,
                                   ALU.is_ge, [em1])
                        em3 = keep(3, [(1, Cp[2]), (1, Cp[4]), (1, Cp[5])], 1.0,
                                   ALU.is_le, [em2])
                        ev_hs = [em0, em1, em2, em3]
                        if debug and pfx == "f1" and tt == 0 and mg == 0:
                            P.do("ge", lambda e: e.dma_start(out=d_u1[:, :], in_=ut[:]),
                                 waits=[ev_u], marks=[(S_C, 16)])
                            P.do("ge", lambda e: e.dma_start(out=d_h4[:, :], in_=h4[:]),
                                 waits=[em3], marks=[(S_C, 16)])
                            P.do("ge", lambda e: e.dma_start(out=d_hpre[:, :], in_=habs[:]),
                                 waits=[em3], marks=[(S_C, 16)])
                        prev_em3 = em3
                        for co in range(NKC):
                            coq, col = co // 4, co % 4
                            k = mg * 4 + coq
                            par = k % 2
                            pd = 4 + (co % 2)
                            last = None
                            for ic in range(4):
                                wsl = slice((col * 4 + ic) * 128,
                                            (col * 4 + ic + 1) * 128)
                                last = P.do("te", lambda e, par=par, wsl=wsl, ic=ic,
                                            pd=pd: e.matmul(
                                    ps[pd][:], wdb[par][:, wsl],
                                    h4[:, ic * TT:(ic + 1) * TT],
                                    start=(ic == 0), stop=(ic == 3)),
                                    waits=(([ev_wd[k]] + ev_hs +
                                            [ev_ps_free.get(pd), ev_u])
                                           if ic == 0 else []),
                                    marks=([(PE, 1)] if ic == 3 else []))
                            ev_dn_done[k] = last
                            csl = slice(co * TT, (co + 1) * TT)
                            if mg == 0:
                                ea = P.do("ve", lambda e, csl=csl, pd=pd:
                                          e.tensor_copy(acc[:, csl], ps[pd][:]),
                                          waits=[last], marks=[(VE, 1)])
                            else:
                                ea = P.do("ve", lambda e, csl=csl, pd=pd:
                                          e.tensor_add(acc[:, csl], acc[:, csl],
                                                       ps[pd][:]),
                                          waits=[last, ev_acc_w.get(co)],
                                          marks=[(VE, 1)])
                            ev_acc_w[co] = ea
                            ev_ps_free[pd] = ea
                            if col == 3 and k + 2 < NMG * 4:
                                pf_dn(k + 2)
                        pe_after_down = (PE, PE.n)
                    if debug and pfx == "f1" and tt == 0:
                        P.do("ge", lambda e: e.dma_start(out=d_acc[:, :], in_=acc[:]),
                             waits=[ev_acc_w[k2] for k2 in range(NKC)],
                             marks=[(S_C, 16)])
                    for co in range(NKC):
                        csl = slice(co * TT, (co + 1) * TT)
                        par = co % 2
                        er = P.do("ve", lambda e, csl=csl, par=par:
                                  e.scalar_tensor_tensor(
                                      x1t[par][:], acc[:, csl], 0.5, xt[:, csl],
                                      op0=ALU.mult, op1=ALU.add),
                                  waits=[ev_acc_w[co], ev_x1_st.get(co - 2)],
                                  marks=[(VE, 1)])
                        ev_x1_st[co] = P.do("ge", lambda e, csl=csl, par=par, tt=tt:
                                            e.dma_start(out=dst[tt][:, csl],
                                                        in_=x1t[par][:]),
                                            waits=[er], marks=[(S_ST[par], 16)])

        emit_ffn(xin, x1m, ev_const, "f1")

        ev_p1 = P.snap(*S_ST)
        with allow_ici_all_to_all():
            ev_ag = P.do("ge", lambda e: e.collective_compute(
                "AllGather", ALU.bypass,
                replica_groups=[[0, 1], [2, 3], [4, 5], [6, 7]],
                ins=[x1m], outs=[x1f]),
                waits=ev_p1, marks=[(CC, 1)])

        es2 = ExitStack()
        sb2 = lambda name, shape: es2.enter_context(
            nc.sbuf_tensor(name, list(shape), F32))
        with es2:
            xt2 = sb2("a_xt", (128, NKC * TT))
            ut2 = sb2("a_ut", (128, NKC * TT))
            _XSQB[0] = sb2("a_xsq0", (128, TT))
            _XSQB[1] = sb2("a_xsq1", (128, TT))
            qraw = sb2("a_qraw", (128, HLOC * TT))
            kraw = sb2("a_kraw", (128, HLOC * TT))
            vraw = sb2("a_vraw", (128, HLOC * TT))
            costb = sb2("a_cost", (128, T))
            sintb = sb2("a_sint", (128, T))
            wqb = [sb2(f"a_wq{i}", (128, NKC * 128)) for i in range(2)]
            wsmb = sb2("a_wsm", (128, NKC * 32))
            wfb = sb2("a_wf", (128, NKC * 8))
            sp8 = sb2("a_sp8", (8, TT))
            g24 = sb2("a_g24", (32, TT))
            ncumt = sb2("a_ncum", (8, TT))
            zeros8 = sb2("a_zeros8", (8, TT))
            car = sb2("a_car", (128, 24))
            carcum = sb2("a_carcum", (8, 1))
            sh1 = sb2("a_sh1", (128, TT))
            rt1 = sb2("a_rt1", (128, TT))
            rt2 = sb2("a_rt2", (128, TT))
            swapb = sb2("a_swap", (128, TT))
            vkt = sb2("a_vkt", (128, 4 * 128))
            gateB = [sb2(f"a_gb{i}", (128, TT)) for i in range(2)]

            w_ph = P.snap(PE, VE, SE, *DMAS)
            ev_tbl0 = P.do("ge", lambda e: e.dma_start(out=costb[:], in_=cost_d[:, :]),
                           waits=w_ph, marks=[(S_C, 16)])
            ev_tbl1 = P.do("ge", lambda e: e.dma_start(out=sintb[:], in_=sint_d[:, :]),
                           marks=[(S_C, 16)])
            ev_tbl2 = P.do("sy", lambda e: e.dma_start(out=wsmb[:], in_=wsmt[:, :]),
                           waits=w_ph, marks=[(S_CW, 16)])
            ev_tbl3 = P.do("sy", lambda e: e.dma_start(out=wfb[:], in_=wft[:, :]),
                           marks=[(S_CW, 16)])
            ev_tbl = [ev_tbl1, ev_tbl1, ev_tbl3, ev_tbl3]
            ev_z = [P.do("ve", lambda e: e.memset(zeros8[:], 0.0), waits=w_ph,
                         marks=[(VE, 1)]),
                    P.do("ve", lambda e: e.memset(car[:], 0.0), marks=[(VE, 1)]),
                    P.do("ve", lambda e: e.memset(carcum[:], 0.0), marks=[(VE, 1)])]

            ev_cum_rd = None
            e_sp_prev = None
            ev_vk_st = None
            for g in range(2 * NTT):
                w0 = (w_ph + [ev_ag]) if g == 0 else P.snap(PE, VE, SE, *DMAS)
                ev_x, ev_u, psT_rd = emit_ln(x1f[g // NTT, g % NTT], xt2, ut2, w0)
                ev_raw = {}
                ev_wq = {}
                for im, (wsrc, rawt) in enumerate([(wqt, qraw), (wkt, kraw),
                                                   (wvt, vraw)]):
                    for mh in range(HLOC):
                        idx = im * HLOC + mh
                        par = idx % 2
                        ev_wq[idx] = P.do(
                            "sy", lambda e, wsrc=wsrc, mh=mh, par=par: e.dma_start(
                                out=wqb[par][:], in_=wsrc[mh]),
                            waits=(w0 if idx < 2 else []) +
                            [ev_raw.get(idx - 2)], marks=[(S_QV[par], 16)])
                        last = None
                        for kc in range(NKC):
                            sl = slice(kc * TT, (kc + 1) * TT)
                            wsl = slice(kc * 128, (kc + 1) * 128)
                            last = P.do("te", lambda e, par=par, sl=sl, wsl=wsl,
                                        kc=kc, q=idx % 2: e.matmul(
                                ps[q][:], wqb[par][:, wsl], ut2[:, sl],
                                start=(kc == 0), stop=(kc == NKC - 1)),
                                waits=([ev_wq[idx], ev_u, ev_raw.get(idx - 2)]
                                       if kc == 0 else []),
                                marks=([(PE, 1)] if kc == NKC - 1 else []))
                        hs = slice(mh * TT, (mh + 1) * TT)
                        ev_raw[idx] = P.do(
                            "ve", lambda e, rawt=rawt, hs=hs, q=idx % 2:
                            e.tensor_copy(rawt[:, hs], ps[q][:]),
                            waits=[last], marks=[(VE, 1)])
                last = None
                for kc in range(NKC):
                    sl = slice(kc * TT, (kc + 1) * TT)
                    wsl = slice(kc * 32, (kc + 1) * 32)
                    last = P.do("te", lambda e, sl=sl, wsl=wsl, kc=kc: e.matmul(
                        psT[0:32, :], wsmb[:, wsl], ut2[:, sl],
                        start=(kc == 0), stop=(kc == NKC - 1)),
                        waits=([ev_tbl[2], ev_u, psT_rd[0], psT_rd[1]]
                               if kc == 0 else []),
                        marks=([(PE, 1)] if kc == NKC - 1 else []))
                lastf = None
                for kc in range(NKC):
                    sl = slice(kc * TT, (kc + 1) * TT)
                    wsl = slice(kc * 8, (kc + 1) * 8)
                    lastf = P.do("te", lambda e, sl=sl, wsl=wsl, kc=kc: e.matmul(
                        ps[2][0:8, :], wfb[:, wsl], ut2[:, sl],
                        start=(kc == 0), stop=(kc == NKC - 1)),
                        waits=([ev_tbl[3], ev_u, e_sp_prev] if kc == 0 else []),
                        marks=([(PE, 1)] if kc == NKC - 1 else []))
                e_sg = P.do("se", lambda e: e.activation(
                    g24[0:32, :], psT[0:32, :], AF.Sigmoid),
                    waits=[last], marks=[(SE, 1)])
                e_spa = P.do("se", lambda e: e.activation(
                    sp8[:, :], ps[2][0:8, :], AF.Exp,
                    bias=nbf_t[0:8, 0:1], scale=-1.0),
                    waits=[lastf], marks=[(SE, 1)])
                e_spb = P.do("ve", lambda e: e.tensor_scalar_add(
                    sp8[:, :], sp8[:, :], 1.0), waits=[e_spa], marks=[(VE, 1)])
                e_sp = P.do("se", lambda e: e.activation(
                    sp8[:, :], sp8[:, :], AF.Ln), waits=[e_spb], marks=[(SE, 1)])
                e_sp_prev = e_spa
                e_scan = P.do("ve", lambda e: e.tensor_tensor_scan(
                    ncumt[:], sp8[:, :], zeros8[:], carcum[:, 0:1],
                    op0=ALU.add, op1=ALU.add),
                    waits=[e_sp, ev_z[0], ev_z[2], ev_cum_rd], marks=[(VE, 1)])
                e_carc = P.do("ve", lambda e: e.tensor_copy(
                    carcum[:, 0:1], ncumt[:, TT - 1:TT]), waits=[e_scan],
                    marks=[(VE, 1)])
                ev_cum_rd = P.do("ge", lambda e, g=g: e.dma_start(
                    out=cums[:, g * TT:(g + 1) * TT], in_=ncumt[:]),
                    waits=[e_carc], marks=[(S_CM, 16)])
                e_g2d = P.do("ge", lambda e: e.dma_start(out=gsc[:, :], in_=g24[:]),
                             waits=[e_sg, e_sp], marks=[(S_GS, 16)])
                gate_reads = []
                for mh in range(HLOC):
                    hs = slice(mh * TT, (mh + 1) * TT)
                    for im, rawt in enumerate([qraw, kraw, vraw]):
                        r = im * 8 + mh
                        ci = slice(r, r + 1)
                        ev_y = ev_raw[im * HLOC + mh]
                        ec1 = P.do("ve", lambda e, rawt=rawt, hs=hs: e.tensor_copy(
                            sh1[:, 1:TT], rawt[:, hs.start:hs.stop - 1]),
                            waits=[ev_y] +
                            ([gate_reads[-1]] if gate_reads else []),
                            marks=[(VE, 1)])
                        ec2 = P.do("ve", lambda e, ci=ci: e.tensor_copy(
                            sh1[:, 0:1], car[:, ci]),
                            waits=[ec1, ev_z[1]], marks=[(VE, 1)])
                        ec3 = P.do("ve", lambda e, ci=ci, rawt=rawt, hs=hs:
                                   e.tensor_copy(car[:, ci],
                                                 rawt[:, hs.stop - 1:hs.stop]),
                                   waits=[ev_y, ec2], marks=[(VE, 1)])
                        ec4 = P.do("ve", lambda e, rawt=rawt, hs=hs: e.tensor_sub(
                            sh1[:], sh1[:], rawt[:, hs]), waits=[ec3],
                            marks=[(VE, 1)])
                        pq = r % 2
                        eb = P.do("ge", lambda e, r=r, pq=pq: e.dma_start(
                            out=gateB[pq][:],
                            in_=bass.AP(tensor=gsc.tensor, offset=r * TT,
                                        ap=[[0, 128], [1, TT]])),
                            waits=[e_g2d] +
                            ([gate_reads[-2]] if len(gate_reads) >= 2 else []),
                            marks=[(S_GB[pq], 16)])
                        ec5 = P.do("ve", lambda e, pq=pq: e.tensor_mul(
                            sh1[:], sh1[:], gateB[pq][:]), waits=[ec4, eb],
                            marks=[(VE, 1)])
                        gate_reads.append(ec5)
                        ec6 = P.do("ve", lambda e, rawt=rawt, hs=hs: e.tensor_add(
                            rawt[:, hs], rawt[:, hs], sh1[:]), waits=[ec5],
                            marks=[(VE, 1)])
                        if im < 2:
                            es1 = P.do("ge", lambda e, rawt=rawt, hs=hs: e.dma_start(
                                out=swapb[0:64, :], in_=rawt[64:128, hs]),
                                waits=[ec6], marks=[(S_SW, 16)])
                            es2_ = P.do("ge", lambda e, rawt=rawt, hs=hs: e.dma_start(
                                out=swapb[64:128, :], in_=rawt[0:64, hs]),
                                waits=[ec6], marks=[(S_SW, 16)])
                            csl = slice(g * TT, (g + 1) * TT)
                            er1 = P.do("ve", lambda e, rawt=rawt, hs=hs, csl=csl:
                                       e.tensor_mul(rt1[:], rawt[:, hs],
                                                    costb[:, csl]),
                                       waits=[ec6] +
                                       ([ev_tbl[0]] if g == 0 else []),
                                       marks=[(VE, 1)])
                            er2 = P.do("ve", lambda e, csl=csl: e.tensor_mul(
                                rt2[:], swapb[:], sintb[:, csl]),
                                waits=[es1, es2_] +
                                ([ev_tbl[1]] if g == 0 else []),
                                marks=[(VE, 1)])
                            er3 = P.do("ve", lambda e, rawt=rawt, hs=hs:
                                       e.tensor_add(rawt[:, hs], rt1[:], rt2[:]),
                                       waits=[er1, er2], marks=[(VE, 1)])
                            dstd = qTs if im == 0 else kTs
                            P.do("ge", lambda e, d=dstd, mh=mh, hs=hs, g=g,
                                 rawt=rawt: e.dma_start(
                                out=d[mh][:, g * TT:(g + 1) * TT],
                                in_=rawt[:, hs]),
                                waits=[er3], marks=[(S_QK, 16)])
                        else:
                            evt = None
                            for blk in range(4):
                                evt = P.do("te", lambda e, hs=hs, blk=blk:
                                           e.transpose(
                                    ps[5][:, blk * 128:(blk + 1) * 128],
                                    vraw[:, hs.start + blk * 128:
                                         hs.start + (blk + 1) * 128],
                                    ident[:]),
                                    waits=([ec6, ev_u] if blk == 0 else []),
                                    marks=([(PE, 1)] if blk == 3 else []))
                            evk = P.do("ve", lambda e: e.tensor_copy(
                                vkt[:], ps[5][:]),
                                waits=[evt, ev_vk_st], marks=[(VE, 1)])
                            ev_vk_st = P.do("ge", lambda e, mh=mh, g=g: e.dma_start(
                                out=bass.AP(tensor=vKM.tensor,
                                            offset=(mh * T + g * TT) * 128,
                                            ap=[[128, 128], [128 * 128, 4],
                                                [1, 128]]),
                                in_=bass.AP(tensor=vkt.ap().tensor, offset=0,
                                            ap=[[512, 128], [128, 4], [1, 128]])),
                                waits=[evk], marks=[(S_VK, 16)])

        es3 = ExitStack()
        sb3 = lambda name, shape: es3.enter_context(
            nc.sbuf_tensor(name, list(shape), F32))
        with es3:
            qTt = sb3("b_qT", (128, T))
            kTt = sb3("b_kT", (128, T))
            vKt = sb3("b_vK", (128, T))
            cumB = sb3("b_cumB", (128, T))
            cumr = sb3("b_cumr", (1, T))
            prow = sb3("b_prow", (128, T))
            pTg = sb3("b_pTg", (128, 32 * 512))
            mrow = sb3("b_mrow", (128, 2))
            oTt = [sb3(f"b_oT{i}", (128, 512)) for i in range(2)]

            w_ph2 = P.snap(PE, VE, SE, *DMAS)
            ev_exp_done = None
            for h in range(HLOC):
                w0 = w_ph2 if h == 0 else P.snap(PE, VE, SE, *DMAS)
                eq = P.do("sy", lambda e, h=h: e.dma_start(out=qTt[:], in_=qTs[h]),
                          waits=w0, marks=[(S_B, 16)])
                ek = P.do("sy", lambda e, h=h: e.dma_start(out=kTt[:], in_=kTs[h]),
                          marks=[(S_B, 16)])
                evv = P.do("sy", lambda e, h=h: e.dma_start(
                    out=bass.AP(tensor=vKt.ap().tensor, offset=0,
                                ap=[[T, 128], [128, 32], [1, 128]]),
                    in_=bass.AP(tensor=vKM.tensor, offset=h * T * 128,
                                ap=[[128, 128], [128 * 128, 32], [1, 128]])),
                    marks=[(S_B, 16)])
                ecr = P.do("sy", lambda e, h=h: e.dma_start(
                    out=cumr[0:1, :], in_=cums[h:h + 1, :]), marks=[(S_B, 16)])
                eq = ek = evv = ecr
                ev_cb = []
                for kci in range(8):
                    sl = slice(kci * 512, (kci + 1) * 512)
                    pq = kci % 2
                    ebm = P.do("te", lambda e, sl=sl, pq=pq: e.matmul(
                        ps[pq][:], ones1[0:1, :], cumr[0:1, sl],
                        start=True, stop=True),
                        waits=(w0 if kci == 0 else []) +
                        [ecr] + ([ev_cb[kci - 2]] if kci >= 2 else []),
                        marks=[(PE, 1)])
                    ev_cb.append(P.do("ve", lambda e, sl=sl, pq=pq: e.tensor_copy(
                        cumB[:, sl], ps[pq][:]),
                        waits=[ebm] + (w0 if kci == 0 else []), marks=[(VE, 1)]))
                ev_ps_s = {0: ev_cb[6], 1: ev_cb[7], 2: None, 3: None}
                ev_pT_rd = None
                ev_oT_st = {}
                ev_oT_ev = {}
                ev_tr_free = None
                ev_prow_free = None
                ev_pT_w = []
                first_mm = True
                for qg in range(8):
                    nkc_g = 4 * qg + 4
                    ev_pad = []
                    for i in range(3):
                        qb = qg * 4 + i
                        cnt = 3 - i
                        ev_pad.append(P.do("ve", lambda e, qb=qb, i=i, cnt=cnt:
                                           e.memset(bass.AP(
                                               tensor=pTg.ap().tensor,
                                               offset=(qb + 1) * 512 + i * 128,
                                               ap=[[32 * 512, 128], [512, cnt],
                                                   [1, 128]]),
                                               0.0),
                                           waits=([ev_pT_rd] if i == 0 else []),
                                           marks=[(VE, 1)]))
                    for i in range(4):
                        qb = qg * 4 + i
                        klen = (qb + 1) * 128
                        nk5 = (klen + 511) // 512
                        ev_evac = None
                        for kci in range(nk5):
                            n = min(512, klen - kci * 512)
                            pq = kci % 4
                            sl = slice(kci * 512, kci * 512 + n)
                            em = P.do("te", lambda e, qb=qb, sl=sl, pq=pq, n=n:
                                      e.matmul(ps[pq][:, 0:n],
                                               qTt[:, qb * 128:(qb + 1) * 128],
                                               kTt[:, sl], start=True, stop=True),
                                      waits=([eq, ek] if first_mm else []) +
                                      [ev_ps_s.get(pq)],
                                      marks=[(PE, 1)])
                            first_mm = False
                            ev_evac = P.do("ve", lambda e, sl=sl, pq=pq, n=n:
                                           e.scalar_tensor_tensor(
                                               prow[:, sl], ps[pq][:, 0:n], RSCALE,
                                               cumB[:, sl], op0=ALU.mult,
                                               op1=ALU.add),
                                           waits=[em, ev_cb[7], ev_exp_done,
                                                  ev_prow_free if kci == 0 else None],
                                           marks=[(VE, 1)])
                            ev_ps_s[pq] = ev_evac
                        edg = P.do("ve", lambda e, klen=klen: e.tensor_add(
                            prow[:, klen - 128:klen], prow[:, klen - 128:klen],
                            causal[:]), waits=[ev_evac], marks=[(VE, 1)])
                        enm = P.do("ve", lambda e, klen=klen: e.reduce_max(
                            mrow[:, 0:1], prow[:, 0:klen], axis=AX.X, negate=True),
                            waits=[edg], marks=[(VE, 1)])
                        eex = P.do("se", lambda e, klen=klen: e.activation(
                            prow[:, 0:klen], prow[:, 0:klen], AF.Exp,
                            bias=mrow[:, 0:1]), waits=[enm], marks=[(SE, 1)])
                        ev_exp_done = eex
                        esm = P.do("ve", lambda e, klen=klen: e.reduce_sum(
                            mrow[:, 1:2], prow[:, 0:klen], axis=AX.X),
                            waits=[eex], marks=[(VE, 1)])
                        erc = P.do("ve", lambda e: e.reciprocal(
                            mrow[:, 1:2], mrow[:, 1:2]), waits=[esm],
                            marks=[(VE, 1)])
                        edv = P.do("ve", lambda e, klen=klen: e.tensor_single_scalar(
                            prow[:, 0:klen], prow[:, 0:klen], mrow[:, 1:2],
                            op=ALU.mult), waits=[erc], marks=[(VE, 1)])
                        if debug and h == 0 and qg == 0:
                            P.do("ge", lambda e, i=i, klen=klen: e.dma_start(
                                out=d_prow[i][:, 0:klen], in_=prow[:, 0:klen]),
                                waits=[edv], marks=[(S_C, 16)])
                        nkc = klen // 128
                        ev_tr_last = None
                        for kc0 in range(0, nkc, 4):
                            nblk = min(4, nkc - kc0)
                            evt = None
                            for bb in range(nblk):
                                kc = kc0 + bb
                                evt = P.do("te", lambda e, kc=kc, bb=bb: e.transpose(
                                    ps[4][:, bb * 128:(bb + 1) * 128],
                                    prow[:, kc * 128:(kc + 1) * 128], ident[:]),
                                    waits=([edv, ev_tr_free, ev_pT_rd]
                                           if bb == 0 else []),
                                    marks=([(PE, 1)] if bb == nblk - 1 else []))
                            etv = P.do("ve", lambda e, kc0=kc0, i=i, nblk=nblk:
                                       e.tensor_copy(
                                           bass.AP(tensor=pTg.ap().tensor,
                                                   offset=kc0 * 512 + i * 128,
                                                   ap=[[32 * 512, 128],
                                                       [512, nblk], [1, 128]]),
                                           ps[4][:, 0:nblk * 128]),
                                       waits=[evt] + (ev_pad if i == 0 else []),
                                       marks=[(VE, 1)])
                            ev_tr_free = etv
                            ev_tr_last = evt
                            ev_pT_w.append(etv)
                        ev_prow_free = ev_tr_last
                    if debug and h == 0 and qg == 0:
                        P.do("ge", lambda e: e.dma_start(
                            out=d_pTg[:, :], in_=pTg[:, 0:4 * 512]),
                            waits=ev_pT_w + ev_pad, marks=[(S_C, 16)])
                    wv = [evv, ev_oT_ev.get(qg - 1)] + ev_pad + ev_pT_w
                    last = None
                    for kc in range(nkc_g):
                        last = P.do("te", lambda e, kc=kc, n=nkc_g: e.matmul(
                            ps[5][:], vKt[:, kc * 128:(kc + 1) * 128],
                            pTg[:, kc * 512:(kc + 1) * 512],
                            start=(kc == 0), stop=(kc == n - 1)),
                            waits=(wv if kc == 0 else []),
                            marks=([(PE, 1)] if kc == nkc_g - 1 else []))
                    ev_pT_rd = last
                    ev_pT_w = []
                    par = qg % 2
                    shard = qg // 4
                    colr = (qg * 512) % 2048
                    eo0 = P.do("ve", lambda e, par=par: e.tensor_single_scalar(
                        oTt[par][:], ps[5][:], selm[:, 0:1], op=ALU.mult),
                        waits=[last, ev_oT_st.get(qg - 2)], marks=[(VE, 1)])
                    es0 = P.do("ge", lambda e, h=h, shard=shard, colr=colr,
                               par=par: e.dma_start(
                        out=oTm[shard][h * 128:(h + 1) * 128, colr:colr + 512],
                        in_=oTt[par][:]), waits=[eo0], marks=[(S_OT[par], 16)])
                    eo1 = P.do("ve", lambda e, par=par: e.tensor_single_scalar(
                        oTt[par][:], ps[5][:], selm[:, 1:2], op=ALU.mult),
                        waits=[es0], marks=[(VE, 1)])
                    ev_oT_ev[qg] = eo1
                    ev_oT_st[qg] = P.do("ge", lambda e, h=h, shard=shard, colr=colr,
                                        par=par: e.dma_start(
                        out=oTm[shard][1024 + h * 128:1024 + (h + 1) * 128,
                                       colr:colr + 512],
                        in_=oTt[par][:]), waits=[eo1], marks=[(S_OT[par], 16)])

        ev_p2b = P.snap(*S_OT, S_VK, S_QK, S_CM)
        ev_a2a = P.do("ge", lambda e: e.collective_compute(
            "ReduceScatter", ALU.add,
            replica_groups=[[0, 1], [2, 3], [4, 5], [6, 7]],
            ins=[oTm], outs=[oTr]),
            waits=ev_p2b, marks=[(CC, 1)])

        es4 = ExitStack()
        sb4 = lambda name, shape: es4.enter_context(
            nc.sbuf_tensor(name, list(shape), F32))
        with es4:
            ot = sb4("c_ot", (128, NKC * TT))
            wob = [sb4(f"c_wo{i}", (128, NKC * 128)) for i in range(2)]
            x1c = [sb4(f"c_x1{i}", (128, TT)) for i in range(2)]
            x2t = [sb4(f"c_x2{i}", (128, TT)) for i in range(2)]

            w_ph3 = P.snap(PE, VE, SE, *DMAS)
            ev_x2w = {}
            ev_x2st = {}
            for tt in range(NTT):
                w0 = (w_ph3 + [ev_a2a]) if tt == 0 else P.snap(PE, VE, SE, *DMAS)
                ev_ot = [P.do("ge", lambda e, tt=tt: e.dma_start(
                    out=bass.AP(tensor=ot.ap().tensor, offset=0,
                                ap=[[NKC * TT, 128], [TT, 16], [1, TT]]),
                    in_=bass.AP(tensor=oTr.tensor, offset=tt * TT,
                                ap=[[T // 2, 128], [128 * (T // 2), 16],
                                    [1, TT]])),
                    waits=list(w0) + [ev_a2a], marks=[(S_XL, 16)])]
                ev_ot.append(ev_ot[0])
                ev_wo = {}
                ev_mm = {}
                for mc in range(NKC):
                    par = mc % 2
                    ev_wo[mc] = P.do("sy", lambda e, mc=mc, par=par: e.dma_start(
                        out=wob[par][:], in_=wot[mc]),
                        waits=(list(w0) if mc < 2 else []) + [ev_mm.get(mc - 2)],
                        marks=[(S_QV[par], 16)])
                    exc = P.do("ge", lambda e, mc=mc, par=par, tt=tt: e.dma_start(
                        out=x1c[par][:], in_=x1m[tt][:, mc * TT:(mc + 1) * TT]),
                        waits=(list(w0) if mc < 2 else []) + [ev_x2w.get(mc - 2)],
                        marks=[(S_XC[par], 16)])
                    last = None
                    for kc in range(NKC):
                        wsl = slice(kc * 128, (kc + 1) * 128)
                        last = P.do("te", lambda e, par=par, wsl=wsl, kc=kc, mc=mc:
                                    e.matmul(ps[mc % 2][:], wob[par][:, wsl],
                                             ot[:, kc * TT:(kc + 1) * TT],
                                             start=(kc == 0), stop=(kc == NKC - 1)),
                                    waits=([ev_wo[mc], ev_ot[0], ev_ot[1],
                                            ev_x2w.get(mc - 2)] if kc == 0 else []),
                                    marks=([(PE, 1)] if kc == NKC - 1 else []))
                    ev_mm[mc] = last
                    ev_x2w[mc] = P.do("ve", lambda e, par=par, mc=mc: e.tensor_add(
                        x2t[par][:], ps[mc % 2][:], x1c[par][:]),
                        waits=[last, exc, ev_x2st.get(mc - 2)], marks=[(VE, 1)])
                    ev_x2st[mc] = P.do("ge", lambda e, par=par, mc=mc, tt=tt:
                                       e.dma_start(
                        out=x2m[tt][:, mc * TT:(mc + 1) * TT], in_=x2t[par][:]),
                        waits=[ev_x2w[mc]], marks=[(S_ST[par], 16)])

        emit_ffn(x2m, outx, P.snap(PE, VE, SE, *DMAS), "f3")

        if debug:
            wdbg = P.snap(VE, PE, SE, *DMAS)
            for nm, (dap, src) in dbg.items():
                P.do("ge", lambda e, d=dap, s=src: e.dma_start(out=d, in_=s),
                     waits=wdbg, marks=[(S_C, 16)])

        P.do("ge", lambda e: None, waits=P.snap(CC, PE, VE, SE, *DMAS))

        with nc.Block() as block:
            for eng_name, starter in [("te", block.tensor), ("ve", block.vector),
                                      ("se", block.scalar), ("ge", block.gpsimd),
                                      ("sy", block.sync)]:
                def mk(name):
                    def f(eng):
                        done = {}
                        for o in P.ops:
                            if o.eng != name:
                                continue
                            byname = {}
                            for s, v in o.waits:
                                if v > byname.get(s.name, 0):
                                    byname[s.name] = v
                                    byname[s.name + "_s"] = s
                            for snm in list(byname):
                                if snm.endswith("_s"):
                                    continue
                                v = byname[snm]
                                if done.get(snm, -1) >= v:
                                    continue
                                eng.wait_ge(byname[snm + "_s"].h, v)
                                done[snm] = v
                            inst = o.emit(eng)
                            if inst is not None:
                                for s, inc in o.marks:
                                    inst.then_inc(s.h, inc)
                    return f
                starter(mk(eng_name))
    return nc


def _tile_feat(xt):
    ntok = xt.shape[1]
    n = ntok // TT
    return np.ascontiguousarray(
        xt.reshape(NKC, 128, n, TT).transpose(2, 1, 0, 3).reshape(n, 128, NKC * TT))


def _tile_lhsT(wT, nm):
    K, M = wT.shape
    nk = K // 128
    return np.ascontiguousarray(
        wT.reshape(nk, 128, nm, 128).transpose(2, 1, 0, 3).reshape(nm, 128, nk * 128))


def _host_inputs(L0_attn_x, L0_attn_Wq, L0_attn_Wsq, L0_attn_Wk, L0_attn_Wsk,
                 L0_attn_Wv, L0_attn_Wsv, L0_attn_Wf, L0_attn_bf, L0_attn_Wo,
                 L0_ffn_w_gate, L0_ffn_w_up, L0_ffn_w_down):
    x = np.ascontiguousarray(L0_attn_x, dtype=np.float32)
    perm = _perm_I()
    wg_p = np.asarray(L0_ffn_w_gate, np.float32)[:, perm]
    wu_p = np.asarray(L0_ffn_w_up, np.float32)[:, perm]
    wd_p = np.asarray(L0_ffn_w_down, np.float32)[perm, :]

    wgt = _tile_lhsT(wg_p, NMC)
    wut = _tile_lhsT(wu_p, NMC)
    wdt = np.ascontiguousarray(
        wd_p.reshape(NMG, 4, 128, 4, 4, 128).transpose(0, 3, 2, 4, 1, 5)
        .reshape(NMG, 4, 128, 2048))
    wot_full = _tile_lhsT(
        np.ascontiguousarray(np.asarray(L0_attn_Wo, np.float32).T), NKC)

    cos, sin = _rope_tables()
    costab = np.ascontiguousarray(np.concatenate([cos.T, cos.T], axis=0))
    sintab = np.ascontiguousarray(np.concatenate([sin.T, -sin.T], axis=0))
    qi, ki = np.meshgrid(np.arange(128), np.arange(128), indexing="ij")
    causal = np.where(ki <= qi, 0.0, NEG).astype(np.float32)
    ident = np.eye(128, dtype=np.float32)

    in_maps = []
    for c in range(8):
        b, j = c // 2, c % 2
        hsl = slice(j * HLOC, (j + 1) * HLOC)
        rsl = slice(j * HLOC * D, (j + 1) * HLOC * D)
        wq = np.ascontiguousarray(np.asarray(L0_attn_Wq, np.float32)[rsl].T)
        wk = np.ascontiguousarray(np.asarray(L0_attn_Wk, np.float32)[rsl].T)
        wv = np.ascontiguousarray(np.asarray(L0_attn_Wv, np.float32)[rsl].T)
        S = np.zeros((32, C), np.float32)
        S[0:8] = np.asarray(L0_attn_Wsq, np.float32)[hsl]
        S[8:16] = np.asarray(L0_attn_Wsk, np.float32)[hsl]
        S[16:24] = np.asarray(L0_attn_Wsv, np.float32)[hsl]
        wsmt = np.ascontiguousarray(
            S.T.reshape(NKC, 128, 32).transpose(1, 0, 2).reshape(128, NKC * 32))
        Sf = np.asarray(L0_attn_Wf, np.float32)[hsl]
        wft = np.ascontiguousarray(
            Sf.T.reshape(NKC, 128, 8).transpose(1, 0, 2).reshape(128, NKC * 8))
        nbf = -np.asarray(L0_attn_bf, np.float32)[hsl].reshape(8, 1)
        selmh = np.zeros((128, 2), np.float32)
        selmh[:, j] = 1.0
        xmine = _tile_feat(np.ascontiguousarray(x[b, j * 2048:(j + 1) * 2048, :].T))
        in_maps.append({
            "xin": xmine, "wgt": wgt, "wut": wut, "wdt": wdt,
            "wqt": _tile_lhsT(wq, HLOC), "wkt": _tile_lhsT(wk, HLOC),
            "wvt": _tile_lhsT(wv, HLOC), "wsmt": wsmt, "wft": wft, "wot": wot_full,
            "nbf": nbf, "costab": costab, "sintab": sintab,
            "causal_d": causal, "ident_d": ident, "selm": selmh,
        })
    return in_maps


def kernel(**inputs):
    in_maps = _host_inputs(**inputs)
    nc = build_nc()
    res = run_bass_kernel_spmd(nc, in_maps, core_ids=list(range(8)))
    out = np.empty((B, T, C), np.float32)
    for c in range(8):
        b, j = c // 2, c % 2
        o = np.asarray(res.results[c]["outx"])
        o = o.reshape(NTT, 128, NKC, TT).transpose(0, 3, 2, 1).reshape(2048, C)
        out[b, j * 2048:(j + 1) * 2048, :] = o
    return out


# version 09
# speedup vs baseline: 1.0148x; 1.0148x over previous
import ctypes
import numpy as np
from contextlib import ExitStack

import concourse.bass as bass
import concourse.mybir as mybir
from concourse.bass_utils import run_bass_kernel_spmd
from concourse.aot_env import allow_ici_all_to_all

F32 = mybir.dt.float32
AF = mybir.ActivationFunctionType
ALU = mybir.AluOpType
AX = mybir.AxisListType

B, T, C, H, D, II = 4, 4096, 2048, 16, 128, 8192
TT = 512
NTT = 4
HLOC = 8
NKC = C // 128
NMC = II // 128
NMG = NMC // 4
EPS = 1e-5
SCALE = float(np.float32(np.sqrt(np.float64(D))))
RSCALE = 1.0 / SCALE
NEG = -1.0e30


class Sem:
    __slots__ = ("name", "h", "n")

    def __init__(self, name):
        self.name, self.h, self.n = name, None, 0


class Op:
    __slots__ = ("eng", "waits", "emit", "marks")

    def __init__(self, eng, waits, emit, marks):
        self.eng, self.waits, self.emit, self.marks = eng, waits, emit, marks


class Prog:
    def __init__(self):
        self.ops = []
        self.sems = []

    def sem(self, name):
        s = Sem(name)
        self.sems.append(s)
        return s

    def do(self, eng, emit, waits=(), marks=()):
        evs = []
        for s, inc in marks:
            s.n += inc
            evs.append((s, s.n))
        self.ops.append(Op(eng, tuple(w for w in waits if w is not None),
                           emit, tuple(marks)))
        return evs[0] if evs else None

    def snap(self, *sems):
        return [(s, s.n) for s in sems if s.n > 0]


def _rope_tables():
    libm = ctypes.CDLL("libm.so.6")
    libm.cosf.restype = ctypes.c_float
    libm.cosf.argtypes = [ctypes.c_float]
    libm.sinf.restype = ctypes.c_float
    libm.sinf.argtypes = [ctypes.c_float]
    libm.powf.restype = ctypes.c_float
    libm.powf.argtypes = [ctypes.c_float, ctypes.c_float]
    expo = (np.arange(0, D, 2, dtype=np.float32) / np.float32(D)).astype(np.float32)
    inv = np.array([np.float32(1.0) / np.float32(libm.powf(ctypes.c_float(10000.0),
                                                           ctypes.c_float(float(e))))
                    for e in expo], dtype=np.float32)
    freqs = (np.arange(T, dtype=np.float32)[:, None] * inv[None, :]).astype(np.float32)
    cos = np.array([libm.cosf(ctypes.c_float(float(v))) for v in freqs.ravel()],
                   dtype=np.float32).reshape(freqs.shape)
    sin = np.array([libm.sinf(ctypes.c_float(float(v))) for v in freqs.ravel()],
                   dtype=np.float32).reshape(freqs.shape)
    return cos, sin


def _perm_I():
    new = np.arange(II)
    p = new % 128
    ch = new // 128
    a, k = ch // 4, ch % 4
    return 4 * (16 * p + a) + k


_XSQB = [None, None]


def build_nc(debug=False):
    nc = bass.Bass("TRN2", target_bir_lowering=False)
    P = Prog()
    PE, VE, SE, CC = (P.sem(n) for n in ["PE", "VE", "SE", "CC"])
    S_WG = [P.sem(f"sWG{i}") for i in range(2)]
    S_WU = [P.sem(f"sWU{i}") for i in range(2)]
    S_WD = [P.sem(f"sWD{i}") for i in range(2)]
    S_QV = [P.sem(f"sQV{i}") for i in range(2)]
    S_ST = [P.sem(f"sST{i}") for i in range(2)]
    S_GB = [P.sem(f"sGB{i}") for i in range(2)]
    S_OT = [P.sem(f"sOT{i}") for i in range(2)]
    S_XC = [P.sem(f"sXC{i}") for i in range(2)]
    S_XL = P.sem("sXL")
    S_SW = P.sem("sSW")
    S_GS = P.sem("sGS")
    S_CM = P.sem("sCM")
    S_QK = P.sem("sQK")
    S_VK = P.sem("sVK")
    S_B = P.sem("sB")
    S_C = P.sem("sC")
    S_CW = P.sem("sCW")
    DMAS = (S_WG + S_WU + S_WD + S_QV + S_ST + S_GB + S_OT + S_XC +
            [S_XL, S_SW, S_GS, S_CM, S_QK, S_VK, S_B, S_C, S_CW])

    def dram_in(name, shape):
        return nc.dram_tensor(name, list(shape), F32, kind="ExternalInput").ap()

    xin = dram_in("xin", (NTT, 128, NKC * TT))
    wgt = dram_in("wgt", (NMC, 128, NKC * 128))
    wut = dram_in("wut", (NMC, 128, NKC * 128))
    wdt = dram_in("wdt", (NMG, 4, 128, 2048))
    wqt = dram_in("wqt", (HLOC, 128, NKC * 128))
    wkt = dram_in("wkt", (HLOC, 128, NKC * 128))
    wvt = dram_in("wvt", (HLOC, 128, NKC * 128))
    wsmt = dram_in("wsmt", (128, NKC * 32))
    wft = dram_in("wft", (128, NKC * 8))
    wot = dram_in("wot", (NKC, 128, NKC * 128))
    nbf_d = dram_in("nbf", (8, 1))
    cost_d = dram_in("costab", (128, T))
    sint_d = dram_in("sintab", (128, T))
    caus_d = dram_in("causal_d", (128, 128))
    selm_d = dram_in("selm", (128, 2))
    iden_d = dram_in("ident_d", (128, 128))

    x1m = nc.dram_tensor("x1m", [NTT, 128, NKC * TT], F32).ap()
    x1f = nc.dram_tensor("x1f", [2, NTT, 128, NKC * TT], F32).ap()
    x2m = nc.dram_tensor("x2m", [NTT, 128, NKC * TT], F32).ap()
    qTs = nc.dram_tensor("qTs", [HLOC, 128, T], F32).ap()
    kTs = nc.dram_tensor("kTs", [HLOC, 128, T], F32).ap()
    vKM = nc.dram_tensor("vKM", [HLOC, T, 128], F32).ap()
    cums = nc.dram_tensor("cums", [HLOC, T], F32).ap()
    gsc = nc.dram_tensor("gsc", [32, TT], F32).ap()
    oTm = nc.dram_tensor("oTm", [2, C, T // 2], F32).ap()
    oTr = nc.dram_tensor("oTr", [C, T // 2], F32).ap()
    outx = nc.dram_tensor("outx", [NTT, 128, NKC * TT], F32, kind="ExternalOutput").ap()
    d_u1 = nc.dram_tensor("d_u1", [128, NKC * TT], F32, kind="ExternalOutput").ap() if debug else None
    d_h4 = nc.dram_tensor("d_h4", [128, 4 * TT], F32, kind="ExternalOutput").ap() if debug else None
    d_hpre = nc.dram_tensor("d_hpre", [128, 4 * TT], F32, kind="ExternalOutput").ap() if debug else None
    d_acc = nc.dram_tensor("d_acc", [128, NKC * TT], F32, kind="ExternalOutput").ap() if debug else None
    d_prow = nc.dram_tensor("d_prow", [4, 128, 512], F32, kind="ExternalOutput").ap() if debug else None
    d_pTg = nc.dram_tensor("d_pTg", [128, 4 * 512], F32, kind="ExternalOutput").ap() if debug else None
    dbg = {}
    if debug:
        for nm, src in [("d_x1m", x1m), ("d_qTs", qTs), ("d_kTs", kTs),
                        ("d_vKM", vKM), ("d_cums", cums), ("d_oTm", oTm),
                        ("d_x2m", x2m), ("d_x1f", x1f), ("d_oTr", oTr)]:
            dbg[nm] = (nc.dram_tensor(nm, list(src.tensor.shape), F32,
                                      kind="ExternalOutput").ap(), src)

    st = ExitStack()
    sbuf = lambda name, shape: st.enter_context(nc.sbuf_tensor(name, list(shape), F32))
    psum = lambda name, shape: st.enter_context(nc.psum_tensor(name, list(shape), F32))

    with st:
        ones1 = sbuf("ones1", (1, 128))
        onesK = sbuf("onesK", (128, 1))
        ident = sbuf("ident", (128, 128))
        causal = sbuf("causal", (128, 128))
        nbf_t = sbuf("nbf_t", (8, 1))
        selm = sbuf("selm_t", (128, 2))
        srow = sbuf("srow", (1, 5 * TT))
        epsb = sbuf("epsb", (1, 1))

        ps = [psum(f"ps{i}", (128, 512)) for i in range(6)]
        psT = psum("psT", (64, 512))

        for s in P.sems:
            s.h = st.enter_context(nc.semaphore(s.name))

        ev_ones = P.do("ve", lambda e: e.memset(ones1[:], 1.0), marks=[(VE, 1)])
        ev_eps = P.do("ve", lambda e: e.memset(epsb[:], EPS), marks=[(VE, 1)])
        ev_onesK = P.do("ve", lambda e: e.memset(onesK[:], 1.0), marks=[(VE, 1)])
        for dst, src in [(ident, iden_d), (causal, caus_d), (nbf_t, nbf_d),
                         (selm, selm_d)]:
            P.do("ge", lambda e, d=dst, s=src: e.dma_start(out=d[:], in_=s[:, :]),
                 marks=[(S_C, 16)])
        ev_const = P.snap(S_C, VE)

        def emit_ln(x_src_ap, xt, ut, w0):
            xq0, xq1 = _XSQB[0], _XSQB[1]
            xsqb = (xq0, xq1)
            ev_x = P.do("ge", lambda e: e.dma_start(out=xt[:], in_=x_src_ap),
                        waits=list(w0) + ev_const, marks=[(S_XL, 16)])
            ev_sum = None
            for kc in range(NKC):
                sl = slice(kc * TT, (kc + 1) * TT)
                ev_sum = P.do("te", lambda e, sl=sl, kc=kc: e.matmul(
                    psT[0:1, :], onesK[:, 0:1], xt[:, sl],
                    start=(kc == 0), stop=(kc == NKC - 1)),
                    waits=(list(w0) + [ev_x, ev_onesK] if kc == 0 else []),
                    marks=([(PE, 1)] if kc == NKC - 1 else []))
            prev_mm = None
            ev_sq = None
            for kc in range(NKC):
                sl = slice(kc * TT, (kc + 1) * TT)
                par = kc % 2
                es = P.do("se", lambda e, sl=sl, xq=xsqb[par]: e.activation(
                    xq[:], xt[:, sl], AF.Square),
                    waits=(list(w0) + [ev_x] if kc < 2 else [prev_mm]),
                    marks=[(SE, 1)])
                prev_mm = P.do("te", lambda e, kc=kc, xq=xsqb[par]: e.matmul(
                    psT[32:33, :], onesK[:, 0:1], xq[:],
                    start=(kc == 0), stop=(kc == NKC - 1)),
                    waits=[es], marks=[(PE, 1)])
                if kc == NKC - 1:
                    ev_sq = prev_mm
            e1 = P.do("ve", lambda e: e.tensor_scalar_mul(
                srow[0:1, 0:TT], psT[0:1, :], 1.0 / C),
                waits=list(w0) + [ev_sum], marks=[(VE, 1)])
            e2 = P.do("ve", lambda e: e.tensor_mul(
                srow[0:1, 2 * TT:3 * TT], srow[0:1, 0:TT], srow[0:1, 0:TT]),
                waits=[e1], marks=[(VE, 1)])
            e3 = P.do("ve", lambda e: e.scalar_tensor_tensor(
                srow[0:1, 3 * TT:4 * TT], psT[32:33, :], 1.0 / C,
                srow[0:1, 2 * TT:3 * TT], op0=ALU.mult, op1=ALU.subtract),
                waits=[ev_sq, e2], marks=[(VE, 1)])
            e4 = P.do("se", lambda e: e.activation(
                srow[0:1, 4 * TT:5 * TT], srow[0:1, 3 * TT:4 * TT], AF.Sqrt,
                bias=epsb[0:1, 0:1]), waits=[e3, ev_eps], marks=[(SE, 1)])
            eb1 = P.do("te", lambda e: e.matmul(
                ps[4][:], ones1[0:1, :], srow[0:1, 0:TT], start=True, stop=True),
                waits=[e1, ev_ones], marks=[(PE, 1)])
            e4r = P.do("ve", lambda e: e.reciprocal(
                srow[0:1, TT:2 * TT], srow[0:1, 4 * TT:5 * TT]),
                waits=[e4], marks=[(VE, 1)])
            eb2 = P.do("te", lambda e: e.matmul(
                ps[5][:], ones1[0:1, :], srow[0:1, TT:2 * TT],
                start=True, stop=True), waits=[e4r], marks=[(PE, 1)])
            ev_u = None
            for kc in range(NKC):
                sl = slice(kc * TT, (kc + 1) * TT)
                eu1 = P.do("ve", lambda e, sl=sl: e.tensor_sub(
                    ut[:, sl], xt[:, sl], ps[4][:]),
                    waits=([eb1] if kc == 0 else []), marks=[(VE, 1)])
                ev_u = P.do("ve", lambda e, sl=sl: e.tensor_mul(
                    ut[:, sl], ut[:, sl], ps[5][:]),
                    waits=([eb2, eu1] if kc == 0 else [eu1]), marks=[(VE, 1)])
            return ev_x, ev_u, (e1, e3)

        def emit_ffn(x_src, dst, phase_w0, pfx):
            es = ExitStack()
            sb2 = lambda name, shape: es.enter_context(
                nc.sbuf_tensor(pfx + name, list(shape), F32))
            with es:
                xt = sb2("xt", (128, NKC * TT))
                ut = sb2("ut", (128, NKC * TT))
                acc = sb2("acc", (128, NKC * TT))
                h4b = [sb2(f"h4_{i}", (128, 4 * TT)) for i in range(2)]
                habs = sb2("habs", (128, 4 * TT))
                cmp6 = sb2("cmp6", (128, 6 * TT))
                tsc = sb2("tsc", (128, TT))
                _XSQB[0] = sb2("xsq0", (128, TT))
                _XSQB[1] = sb2("xsq1", (128, TT))
                wgb = [sb2(f"wgb{i}", (128, NKC * 128)) for i in range(2)]
                wub = [sb2(f"wub{i}", (128, NKC * 128)) for i in range(2)]
                wdb = [sb2(f"wdb{i}", (128, 2048)) for i in range(2)]
                x1t = [sb2(f"x1t{i}", (128, TT)) for i in range(2)]

                for tt in range(NTT):
                    w0 = list(phase_w0) if tt == 0 else P.snap(PE, VE, SE, *DMAS)
                    ev_x, ev_u, _ = emit_ln(x_src[tt], xt, ut, w0)
                    ev_w = {}
                    ev_wd = {}
                    ev_g_done = {}
                    ev_up_done = {}
                    ev_sil = {}
                    ev_mul = {}
                    ev_dn_done = {}
                    ev_acc_w = {}
                    ev_ps_free = {}
                    ev_x1_st = {}

                    def pf_gu(mc, w=()):
                        par = mc % 2
                        ev_w[mc] = (
                            P.do("sy", lambda e, mc=mc, par=par: e.dma_start(
                                out=wgb[par][:], in_=wgt[mc]),
                                waits=list(w) + [ev_g_done.get(mc - 2)],
                                marks=[(S_WG[par], 16)]),
                            P.do("sy", lambda e, mc=mc, par=par: e.dma_start(
                                out=wub[par][:], in_=wut[mc]),
                                waits=list(w) + [ev_up_done.get(mc - 2)],
                                marks=[(S_WU[par], 16)]))

                    def pf_dn(k, w=()):
                        mg, coq = k // 4, k % 4
                        par = k % 2
                        ev_wd[k] = P.do("sy", lambda e, mg=mg, coq=coq, par=par:
                                        e.dma_start(out=wdb[par][:], in_=wdt[mg, coq]),
                                        waits=list(w) + [ev_dn_done.get(k - 2)],
                                        marks=[(S_WD[par], 16)])

                    pf_gu(0, w0)
                    pf_gu(1, w0)
                    pf_dn(0, w0)
                    pf_dn(1, w0)
                    prev_em3 = None
                    pe_down = {}
                    ev_hs_by = {}

                    def emit_gusp(mg):
                        h4 = h4b[mg % 2]
                        ev_h = {}
                        for i in range(4):
                            mc = mg * 4 + i
                            par = mc % 2
                            last = None
                            for kc in range(NKC):
                                sl = slice(kc * TT, (kc + 1) * TT)
                                wsl = slice(kc * 128, (kc + 1) * 128)
                                last = P.do("te", lambda e, par=par, sl=sl, wsl=wsl,
                                            kc=kc: e.matmul(
                                    ps[par][:], wgb[par][:, wsl], ut[:, sl],
                                    start=(kc == 0), stop=(kc == NKC - 1)),
                                    waits=([ev_w[mc][0], ev_u,
                                            ev_sil.get(mc - 2)] if kc == 0 else []),
                                    marks=([(PE, 1)] if kc == NKC - 1 else []))
                            ev_g = last
                            ev_g_done[mc] = ev_g
                            for kc in range(NKC):
                                sl = slice(kc * TT, (kc + 1) * TT)
                                wsl = slice(kc * 128, (kc + 1) * 128)
                                last = P.do("te", lambda e, par=par, sl=sl, wsl=wsl,
                                            kc=kc: e.matmul(
                                    ps[2 + par][:], wub[par][:, wsl], ut[:, sl],
                                    start=(kc == 0), stop=(kc == NKC - 1)),
                                    waits=([ev_w[mc][1],
                                            ev_mul.get(mc - 2)] if kc == 0 else []),
                                    marks=([(PE, 1)] if kc == NKC - 1 else []))
                            ev_up = last
                            ev_up_done[mc] = ev_up
                            if mc + 2 < NMC:
                                pf_gu(mc + 2)
                            hsl = slice(i * TT, (i + 1) * TT)
                            ev_sil[mc] = P.do(
                                "se", lambda e, hsl=hsl, par=par, h4=h4: e.activation(
                                    h4[:, hsl], ps[par][:], AF.Silu),
                                waits=[ev_g] + ([pe_down.get(mg - 2), prev_em3]
                                                if i == 0 else []),
                                marks=[(SE, 1)])
                            ev_mul[mc] = P.do(
                                "ve", lambda e, hsl=hsl, par=par, h4=h4: e.tensor_tensor(
                                    h4[:, hsl], h4[:, hsl], ps[2 + par][:],
                                    op=ALU.mult),
                                waits=[ev_sil[mc], ev_up], marks=[(VE, 1)])
                            ev_h[i] = ev_mul[mc]
                        A = [h4[:, i * TT:(i + 1) * TT] for i in range(4)]
                        Ab = [habs[:, i * TT:(i + 1) * TT] for i in range(4)]
                        Cp = [cmp6[:, i * TT:(i + 1) * TT] for i in range(6)]
                        eabs = P.do("se", lambda e, h4=h4: e.activation(habs[:], h4[:], AF.Abs),
                                    waits=[ev_h[0], ev_h[1], ev_h[2], ev_h[3],
                                           prev_em3], marks=[(SE, 1)])
                        pairs = [(0, 1), (0, 2), (0, 3), (1, 2), (1, 3), (2, 3)]
                        ecmp = None
                        for ci, (a, b) in enumerate(pairs):
                            ecmp = P.do("ve", lambda e, ci=ci, a=a, b=b:
                                        e.tensor_tensor(Cp[ci], Ab[a], Ab[b],
                                                        op=ALU.is_ge),
                                        waits=([eabs] if ci == 0 else []),
                                        marks=[(VE, 1)])

                        def keep(i, terms, thr, op, w):
                            e = P.do("ve", lambda e, t=terms: e.tensor_tensor(
                                tsc[:], t[0][1], t[1][1],
                                op=(ALU.add if t[1][0] > 0 else ALU.subtract)),
                                waits=w, marks=[(VE, 1)])
                            for sgn, cc in terms[2:]:
                                e = P.do("ve", lambda e, cc=cc, sgn=sgn:
                                         e.tensor_tensor(
                                             tsc[:], tsc[:], cc,
                                             op=(ALU.add if sgn > 0 else ALU.subtract)),
                                         waits=[e], marks=[(VE, 1)])
                            ek = P.do("ve", lambda e, i=i, thr=thr, op=op:
                                      e.tensor_single_scalar(Ab[i], tsc[:], thr, op=op),
                                      waits=[e], marks=[(VE, 1)])
                            return P.do("ve", lambda e, i=i: e.tensor_mul(
                                A[i], A[i], Ab[i]), waits=[ek], marks=[(VE, 1)])

                        em0 = keep(0, [(1, Cp[0]), (1, Cp[1]), (1, Cp[2])], 2.0,
                                   ALU.is_ge, [ecmp])
                        em1 = keep(1, [(1, Cp[3]), (1, Cp[4]), (-1, Cp[0])], 1.0,
                                   ALU.is_ge, [em0])
                        em2 = keep(2, [(1, Cp[5]), (-1, Cp[1]), (-1, Cp[3])], 0.0,
                                   ALU.is_ge, [em1])
                        em3 = keep(3, [(1, Cp[2]), (1, Cp[4]), (1, Cp[5])], 1.0,
                                   ALU.is_le, [em2])
                        ev_hs_by[mg] = [em0, em1, em2, em3]
                        if debug and pfx == "f1" and tt == 0 and mg == 0:
                            P.do("ge", lambda e, h4=h4: e.dma_start(out=d_u1[:, :], in_=ut[:]),
                                 waits=[ev_u], marks=[(S_C, 16)])
                            P.do("ge", lambda e, h4=h4: e.dma_start(out=d_h4[:, :], in_=h4[:]),
                                 waits=[em3], marks=[(S_C, 16)])
                        return em3

                    def emit_down(mg):
                        h4 = h4b[mg % 2]
                        ev_hs = ev_hs_by[mg]
                        for co in range(NKC):
                            coq, col = co // 4, co % 4
                            k = mg * 4 + coq
                            par = k % 2
                            pd = 4 + (co % 2)
                            last = None
                            for ic in range(4):
                                wsl = slice((col * 4 + ic) * 128,
                                            (col * 4 + ic + 1) * 128)
                                last = P.do("te", lambda e, par=par, wsl=wsl, ic=ic,
                                            pd=pd, h4=h4: e.matmul(
                                    ps[pd][:], wdb[par][:, wsl],
                                    h4[:, ic * TT:(ic + 1) * TT],
                                    start=(ic == 0), stop=(ic == 3)),
                                    waits=(([ev_wd[k]] + ev_hs +
                                            [ev_ps_free.get(pd), ev_u])
                                           if ic == 0 else []),
                                    marks=([(PE, 1)] if ic == 3 else []))
                            ev_dn_done[k] = last
                            csl = slice(co * TT, (co + 1) * TT)
                            if mg == 0:
                                ea = P.do("ve", lambda e, csl=csl, pd=pd:
                                          e.tensor_copy(acc[:, csl], ps[pd][:]),
                                          waits=[last], marks=[(VE, 1)])
                            else:
                                ea = P.do("ve", lambda e, csl=csl, pd=pd:
                                          e.tensor_add(acc[:, csl], acc[:, csl],
                                                       ps[pd][:]),
                                          waits=[last, ev_acc_w.get(co)],
                                          marks=[(VE, 1)])
                            ev_acc_w[co] = ea
                            ev_ps_free[pd] = ea
                            if col == 3 and k + 2 < NMG * 4:
                                pf_dn(k + 2)
                        pe_down[mg] = (PE, PE.n)

                    for mg in range(NMG):
                        prev_em3 = emit_gusp(mg)
                        if mg > 0:
                            emit_down(mg - 1)
                    emit_down(NMG - 1)
                    if debug and pfx == "f1" and tt == 0:
                        P.do("ge", lambda e: e.dma_start(out=d_acc[:, :], in_=acc[:]),
                             waits=[ev_acc_w[k2] for k2 in range(NKC)],
                             marks=[(S_C, 16)])
                    for co in range(NKC):
                        csl = slice(co * TT, (co + 1) * TT)
                        par = co % 2
                        er = P.do("ve", lambda e, csl=csl, par=par:
                                  e.scalar_tensor_tensor(
                                      x1t[par][:], acc[:, csl], 0.5, xt[:, csl],
                                      op0=ALU.mult, op1=ALU.add),
                                  waits=[ev_acc_w[co], ev_x1_st.get(co - 2)],
                                  marks=[(VE, 1)])
                        ev_x1_st[co] = P.do("ge", lambda e, csl=csl, par=par, tt=tt:
                                            e.dma_start(out=dst[tt][:, csl],
                                                        in_=x1t[par][:]),
                                            waits=[er], marks=[(S_ST[par], 16)])

        emit_ffn(xin, x1m, ev_const, "f1")

        ev_p1 = P.snap(*S_ST)
        with allow_ici_all_to_all():
            ev_ag = P.do("ge", lambda e: e.collective_compute(
                "AllGather", ALU.bypass,
                replica_groups=[[0, 1], [2, 3], [4, 5], [6, 7]],
                ins=[x1m], outs=[x1f]),
                waits=ev_p1, marks=[(CC, 1)])

        es2 = ExitStack()
        sb2 = lambda name, shape: es2.enter_context(
            nc.sbuf_tensor(name, list(shape), F32))
        with es2:
            xt2 = sb2("a_xt", (128, NKC * TT))
            ut2 = sb2("a_ut", (128, NKC * TT))
            _XSQB[0] = sb2("a_xsq0", (128, TT))
            _XSQB[1] = sb2("a_xsq1", (128, TT))
            qraw = sb2("a_qraw", (128, HLOC * TT))
            kraw = sb2("a_kraw", (128, HLOC * TT))
            vraw = sb2("a_vraw", (128, HLOC * TT))
            costb = sb2("a_cost", (128, T))
            sintb = sb2("a_sint", (128, T))
            wqb = [sb2(f"a_wq{i}", (128, NKC * 128)) for i in range(2)]
            wsmb = sb2("a_wsm", (128, NKC * 32))
            wfb = sb2("a_wf", (128, NKC * 8))
            sp8 = sb2("a_sp8", (8, TT))
            g24 = sb2("a_g24", (32, TT))
            ncumt = sb2("a_ncum", (8, TT))
            zeros8 = sb2("a_zeros8", (8, TT))
            car = sb2("a_car", (128, 24))
            carcum = sb2("a_carcum", (8, 1))
            sh1 = sb2("a_sh1", (128, TT))
            rt1 = sb2("a_rt1", (128, TT))
            rt2 = sb2("a_rt2", (128, TT))
            swapb = sb2("a_swap", (128, TT))
            vkt = sb2("a_vkt", (128, 4 * 128))
            gateB = [sb2(f"a_gb{i}", (128, TT)) for i in range(2)]

            w_ph = P.snap(PE, VE, SE, *DMAS)
            ev_tbl0 = P.do("ge", lambda e: e.dma_start(out=costb[:], in_=cost_d[:, :]),
                           waits=w_ph, marks=[(S_C, 16)])
            ev_tbl1 = P.do("ge", lambda e: e.dma_start(out=sintb[:], in_=sint_d[:, :]),
                           marks=[(S_C, 16)])
            ev_tbl2 = P.do("sy", lambda e: e.dma_start(out=wsmb[:], in_=wsmt[:, :]),
                           waits=w_ph, marks=[(S_CW, 16)])
            ev_tbl3 = P.do("sy", lambda e: e.dma_start(out=wfb[:], in_=wft[:, :]),
                           marks=[(S_CW, 16)])
            ev_tbl = [ev_tbl1, ev_tbl1, ev_tbl3, ev_tbl3]
            ev_z = [P.do("ve", lambda e: e.memset(zeros8[:], 0.0), waits=w_ph,
                         marks=[(VE, 1)]),
                    P.do("ve", lambda e: e.memset(car[:], 0.0), marks=[(VE, 1)]),
                    P.do("ve", lambda e: e.memset(carcum[:], 0.0), marks=[(VE, 1)])]

            ev_cum_rd = None
            e_sp_prev = None
            ev_vk_st = None
            for g in range(2 * NTT):
                w0 = (w_ph + [ev_ag]) if g == 0 else P.snap(PE, VE, SE, *DMAS)
                ev_x, ev_u, psT_rd = emit_ln(x1f[g // NTT, g % NTT], xt2, ut2, w0)
                ev_raw = {}
                ev_wq = {}
                for im, (wsrc, rawt) in enumerate([(wqt, qraw), (wkt, kraw),
                                                   (wvt, vraw)]):
                    for mh in range(HLOC):
                        idx = im * HLOC + mh
                        par = idx % 2
                        ev_wq[idx] = P.do(
                            "sy", lambda e, wsrc=wsrc, mh=mh, par=par: e.dma_start(
                                out=wqb[par][:], in_=wsrc[mh]),
                            waits=(w0 if idx < 2 else []) +
                            [ev_raw.get(idx - 2)], marks=[(S_QV[par], 16)])
                        last = None
                        for kc in range(NKC):
                            sl = slice(kc * TT, (kc + 1) * TT)
                            wsl = slice(kc * 128, (kc + 1) * 128)
                            last = P.do("te", lambda e, par=par, sl=sl, wsl=wsl,
                                        kc=kc, q=idx % 2: e.matmul(
                                ps[q][:], wqb[par][:, wsl], ut2[:, sl],
                                start=(kc == 0), stop=(kc == NKC - 1)),
                                waits=([ev_wq[idx], ev_u, ev_raw.get(idx - 2)]
                                       if kc == 0 else []),
                                marks=([(PE, 1)] if kc == NKC - 1 else []))
                        hs = slice(mh * TT, (mh + 1) * TT)
                        ev_raw[idx] = P.do(
                            "ve", lambda e, rawt=rawt, hs=hs, q=idx % 2:
                            e.tensor_copy(rawt[:, hs], ps[q][:]),
                            waits=[last], marks=[(VE, 1)])
                last = None
                for kc in range(NKC):
                    sl = slice(kc * TT, (kc + 1) * TT)
                    wsl = slice(kc * 32, (kc + 1) * 32)
                    last = P.do("te", lambda e, sl=sl, wsl=wsl, kc=kc: e.matmul(
                        psT[0:32, :], wsmb[:, wsl], ut2[:, sl],
                        start=(kc == 0), stop=(kc == NKC - 1)),
                        waits=([ev_tbl[2], ev_u, psT_rd[0], psT_rd[1]]
                               if kc == 0 else []),
                        marks=([(PE, 1)] if kc == NKC - 1 else []))
                lastf = None
                for kc in range(NKC):
                    sl = slice(kc * TT, (kc + 1) * TT)
                    wsl = slice(kc * 8, (kc + 1) * 8)
                    lastf = P.do("te", lambda e, sl=sl, wsl=wsl, kc=kc: e.matmul(
                        ps[2][0:8, :], wfb[:, wsl], ut2[:, sl],
                        start=(kc == 0), stop=(kc == NKC - 1)),
                        waits=([ev_tbl[3], ev_u, e_sp_prev] if kc == 0 else []),
                        marks=([(PE, 1)] if kc == NKC - 1 else []))
                e_sg = P.do("se", lambda e: e.activation(
                    g24[0:32, :], psT[0:32, :], AF.Sigmoid),
                    waits=[last], marks=[(SE, 1)])
                e_spa = P.do("se", lambda e: e.activation(
                    sp8[:, :], ps[2][0:8, :], AF.Exp,
                    bias=nbf_t[0:8, 0:1], scale=-1.0),
                    waits=[lastf], marks=[(SE, 1)])
                e_spb = P.do("ve", lambda e: e.tensor_scalar_add(
                    sp8[:, :], sp8[:, :], 1.0), waits=[e_spa], marks=[(VE, 1)])
                e_sp = P.do("se", lambda e: e.activation(
                    sp8[:, :], sp8[:, :], AF.Ln), waits=[e_spb], marks=[(SE, 1)])
                e_sp_prev = e_spa
                e_scan = P.do("ve", lambda e: e.tensor_tensor_scan(
                    ncumt[:], sp8[:, :], zeros8[:], carcum[:, 0:1],
                    op0=ALU.add, op1=ALU.add),
                    waits=[e_sp, ev_z[0], ev_z[2], ev_cum_rd], marks=[(VE, 1)])
                e_carc = P.do("ve", lambda e: e.tensor_copy(
                    carcum[:, 0:1], ncumt[:, TT - 1:TT]), waits=[e_scan],
                    marks=[(VE, 1)])
                ev_cum_rd = P.do("ge", lambda e, g=g: e.dma_start(
                    out=cums[:, g * TT:(g + 1) * TT], in_=ncumt[:]),
                    waits=[e_carc], marks=[(S_CM, 16)])
                e_g2d = P.do("ge", lambda e: e.dma_start(out=gsc[:, :], in_=g24[:]),
                             waits=[e_sg, e_sp], marks=[(S_GS, 16)])
                gate_reads = []
                for mh in range(HLOC):
                    hs = slice(mh * TT, (mh + 1) * TT)
                    for im, rawt in enumerate([qraw, kraw, vraw]):
                        r = im * 8 + mh
                        ci = slice(r, r + 1)
                        ev_y = ev_raw[im * HLOC + mh]
                        ec1 = P.do("ve", lambda e, rawt=rawt, hs=hs: e.tensor_copy(
                            sh1[:, 1:TT], rawt[:, hs.start:hs.stop - 1]),
                            waits=[ev_y] +
                            ([gate_reads[-1]] if gate_reads else []),
                            marks=[(VE, 1)])
                        ec2 = P.do("ve", lambda e, ci=ci: e.tensor_copy(
                            sh1[:, 0:1], car[:, ci]),
                            waits=[ec1, ev_z[1]], marks=[(VE, 1)])
                        ec3 = P.do("ve", lambda e, ci=ci, rawt=rawt, hs=hs:
                                   e.tensor_copy(car[:, ci],
                                                 rawt[:, hs.stop - 1:hs.stop]),
                                   waits=[ev_y, ec2], marks=[(VE, 1)])
                        ec4 = P.do("ve", lambda e, rawt=rawt, hs=hs: e.tensor_sub(
                            sh1[:], sh1[:], rawt[:, hs]), waits=[ec3],
                            marks=[(VE, 1)])
                        pq = r % 2
                        eb = P.do("ge", lambda e, r=r, pq=pq: e.dma_start(
                            out=gateB[pq][:],
                            in_=bass.AP(tensor=gsc.tensor, offset=r * TT,
                                        ap=[[0, 128], [1, TT]])),
                            waits=[e_g2d] +
                            ([gate_reads[-2]] if len(gate_reads) >= 2 else []),
                            marks=[(S_GB[pq], 16)])
                        ec5 = P.do("ve", lambda e, pq=pq: e.tensor_mul(
                            sh1[:], sh1[:], gateB[pq][:]), waits=[ec4, eb],
                            marks=[(VE, 1)])
                        gate_reads.append(ec5)
                        ec6 = P.do("ve", lambda e, rawt=rawt, hs=hs: e.tensor_add(
                            rawt[:, hs], rawt[:, hs], sh1[:]), waits=[ec5],
                            marks=[(VE, 1)])
                        if im < 2:
                            es1 = P.do("ge", lambda e, rawt=rawt, hs=hs: e.dma_start(
                                out=swapb[0:64, :], in_=rawt[64:128, hs]),
                                waits=[ec6], marks=[(S_SW, 16)])
                            es2_ = P.do("ge", lambda e, rawt=rawt, hs=hs: e.dma_start(
                                out=swapb[64:128, :], in_=rawt[0:64, hs]),
                                waits=[ec6], marks=[(S_SW, 16)])
                            csl = slice(g * TT, (g + 1) * TT)
                            er1 = P.do("ve", lambda e, rawt=rawt, hs=hs, csl=csl:
                                       e.tensor_mul(rt1[:], rawt[:, hs],
                                                    costb[:, csl]),
                                       waits=[ec6] +
                                       ([ev_tbl[0]] if g == 0 else []),
                                       marks=[(VE, 1)])
                            er2 = P.do("ve", lambda e, csl=csl: e.tensor_mul(
                                rt2[:], swapb[:], sintb[:, csl]),
                                waits=[es1, es2_] +
                                ([ev_tbl[1]] if g == 0 else []),
                                marks=[(VE, 1)])
                            er3 = P.do("ve", lambda e, rawt=rawt, hs=hs:
                                       e.tensor_add(rawt[:, hs], rt1[:], rt2[:]),
                                       waits=[er1, er2], marks=[(VE, 1)])
                            dstd = qTs if im == 0 else kTs
                            P.do("ge", lambda e, d=dstd, mh=mh, hs=hs, g=g,
                                 rawt=rawt: e.dma_start(
                                out=d[mh][:, g * TT:(g + 1) * TT],
                                in_=rawt[:, hs]),
                                waits=[er3], marks=[(S_QK, 16)])
                        else:
                            evt = None
                            for blk in range(4):
                                evt = P.do("te", lambda e, hs=hs, blk=blk:
                                           e.transpose(
                                    ps[5][:, blk * 128:(blk + 1) * 128],
                                    vraw[:, hs.start + blk * 128:
                                         hs.start + (blk + 1) * 128],
                                    ident[:]),
                                    waits=([ec6, ev_u] if blk == 0 else []),
                                    marks=([(PE, 1)] if blk == 3 else []))
                            evk = P.do("ve", lambda e: e.tensor_copy(
                                vkt[:], ps[5][:]),
                                waits=[evt, ev_vk_st], marks=[(VE, 1)])
                            ev_vk_st = P.do("ge", lambda e, mh=mh, g=g: e.dma_start(
                                out=bass.AP(tensor=vKM.tensor,
                                            offset=(mh * T + g * TT) * 128,
                                            ap=[[128, 128], [128 * 128, 4],
                                                [1, 128]]),
                                in_=bass.AP(tensor=vkt.ap().tensor, offset=0,
                                            ap=[[512, 128], [128, 4], [1, 128]])),
                                waits=[evk], marks=[(S_VK, 16)])

        es3 = ExitStack()
        sb3 = lambda name, shape: es3.enter_context(
            nc.sbuf_tensor(name, list(shape), F32))
        with es3:
            qTt = sb3("b_qT", (128, T))
            kTt = sb3("b_kT", (128, T))
            vKt = sb3("b_vK", (128, T))
            cumB = sb3("b_cumB", (128, T))
            cumr = sb3("b_cumr", (1, T))
            prow = sb3("b_prow", (128, T))
            pTg = sb3("b_pTg", (128, 32 * 512))
            mrow = sb3("b_mrow", (128, 2))
            oTt = [sb3(f"b_oT{i}", (128, 512)) for i in range(2)]

            w_ph2 = P.snap(PE, VE, SE, *DMAS)
            ev_exp_done = None
            for h in range(HLOC):
                w0 = w_ph2 if h == 0 else P.snap(PE, VE, SE, *DMAS)
                eq = P.do("sy", lambda e, h=h: e.dma_start(out=qTt[:], in_=qTs[h]),
                          waits=w0, marks=[(S_B, 16)])
                ek = P.do("sy", lambda e, h=h: e.dma_start(out=kTt[:], in_=kTs[h]),
                          marks=[(S_B, 16)])
                evv = P.do("sy", lambda e, h=h: e.dma_start(
                    out=bass.AP(tensor=vKt.ap().tensor, offset=0,
                                ap=[[T, 128], [128, 32], [1, 128]]),
                    in_=bass.AP(tensor=vKM.tensor, offset=h * T * 128,
                                ap=[[128, 128], [128 * 128, 32], [1, 128]])),
                    marks=[(S_B, 16)])
                ecr = P.do("sy", lambda e, h=h: e.dma_start(
                    out=cumr[0:1, :], in_=cums[h:h + 1, :]), marks=[(S_B, 16)])
                eq = ek = evv = ecr
                ev_cb = []
                for kci in range(8):
                    sl = slice(kci * 512, (kci + 1) * 512)
                    pq = kci % 2
                    ebm = P.do("te", lambda e, sl=sl, pq=pq: e.matmul(
                        ps[pq][:], ones1[0:1, :], cumr[0:1, sl],
                        start=True, stop=True),
                        waits=(w0 if kci == 0 else []) +
                        [ecr] + ([ev_cb[kci - 2]] if kci >= 2 else []),
                        marks=[(PE, 1)])
                    ev_cb.append(P.do("ve", lambda e, sl=sl, pq=pq: e.tensor_copy(
                        cumB[:, sl], ps[pq][:]),
                        waits=[ebm] + (w0 if kci == 0 else []), marks=[(VE, 1)]))
                ev_ps_s = {0: ev_cb[6], 1: ev_cb[7], 2: None, 3: None}
                ev_pT_rd = None
                ev_oT_st = {}
                ev_oT_ev = {}
                ev_tr_free = None
                ev_prow_free = None
                ev_pT_w = []
                first_mm = True
                for qg in range(8):
                    nkc_g = 4 * qg + 4
                    ev_pad = []
                    for i in range(3):
                        qb = qg * 4 + i
                        cnt = 3 - i
                        ev_pad.append(P.do("ve", lambda e, qb=qb, i=i, cnt=cnt:
                                           e.memset(bass.AP(
                                               tensor=pTg.ap().tensor,
                                               offset=(qb + 1) * 512 + i * 128,
                                               ap=[[32 * 512, 128], [512, cnt],
                                                   [1, 128]]),
                                               0.0),
                                           waits=([ev_pT_rd] if i == 0 else []),
                                           marks=[(VE, 1)]))
                    for i in range(4):
                        qb = qg * 4 + i
                        klen = (qb + 1) * 128
                        nk5 = (klen + 511) // 512
                        ev_evac = None
                        for kci in range(nk5):
                            n = min(512, klen - kci * 512)
                            pq = kci % 4
                            sl = slice(kci * 512, kci * 512 + n)
                            em = P.do("te", lambda e, qb=qb, sl=sl, pq=pq, n=n:
                                      e.matmul(ps[pq][:, 0:n],
                                               qTt[:, qb * 128:(qb + 1) * 128],
                                               kTt[:, sl], start=True, stop=True),
                                      waits=([eq, ek] if first_mm else []) +
                                      [ev_ps_s.get(pq)],
                                      marks=[(PE, 1)])
                            first_mm = False
                            ev_evac = P.do("ve", lambda e, sl=sl, pq=pq, n=n:
                                           e.scalar_tensor_tensor(
                                               prow[:, sl], ps[pq][:, 0:n], RSCALE,
                                               cumB[:, sl], op0=ALU.mult,
                                               op1=ALU.add),
                                           waits=[em, ev_cb[7], ev_exp_done,
                                                  ev_prow_free if kci == 0 else None],
                                           marks=[(VE, 1)])
                            ev_ps_s[pq] = ev_evac
                        edg = P.do("ve", lambda e, klen=klen: e.tensor_add(
                            prow[:, klen - 128:klen], prow[:, klen - 128:klen],
                            causal[:]), waits=[ev_evac], marks=[(VE, 1)])
                        enm = P.do("ve", lambda e, klen=klen: e.reduce_max(
                            mrow[:, 0:1], prow[:, 0:klen], axis=AX.X, negate=True),
                            waits=[edg], marks=[(VE, 1)])
                        eex = P.do("se", lambda e, klen=klen: e.activation(
                            prow[:, 0:klen], prow[:, 0:klen], AF.Exp,
                            bias=mrow[:, 0:1]), waits=[enm], marks=[(SE, 1)])
                        ev_exp_done = eex
                        esm = P.do("ve", lambda e, klen=klen: e.reduce_sum(
                            mrow[:, 1:2], prow[:, 0:klen], axis=AX.X),
                            waits=[eex], marks=[(VE, 1)])
                        erc = P.do("ve", lambda e: e.reciprocal(
                            mrow[:, 1:2], mrow[:, 1:2]), waits=[esm],
                            marks=[(VE, 1)])
                        edv = P.do("ve", lambda e, klen=klen: e.tensor_single_scalar(
                            prow[:, 0:klen], prow[:, 0:klen], mrow[:, 1:2],
                            op=ALU.mult), waits=[erc], marks=[(VE, 1)])
                        if debug and h == 0 and qg == 0:
                            P.do("ge", lambda e, i=i, klen=klen: e.dma_start(
                                out=d_prow[i][:, 0:klen], in_=prow[:, 0:klen]),
                                waits=[edv], marks=[(S_C, 16)])
                        nkc = klen // 128
                        ev_tr_last = None
                        for kc0 in range(0, nkc, 4):
                            nblk = min(4, nkc - kc0)
                            evt = None
                            for bb in range(nblk):
                                kc = kc0 + bb
                                evt = P.do("te", lambda e, kc=kc, bb=bb: e.transpose(
                                    ps[4][:, bb * 128:(bb + 1) * 128],
                                    prow[:, kc * 128:(kc + 1) * 128], ident[:]),
                                    waits=([edv, ev_tr_free, ev_pT_rd]
                                           if bb == 0 else []),
                                    marks=([(PE, 1)] if bb == nblk - 1 else []))
                            etv = P.do("ve", lambda e, kc0=kc0, i=i, nblk=nblk:
                                       e.tensor_copy(
                                           bass.AP(tensor=pTg.ap().tensor,
                                                   offset=kc0 * 512 + i * 128,
                                                   ap=[[32 * 512, 128],
                                                       [512, nblk], [1, 128]]),
                                           ps[4][:, 0:nblk * 128]),
                                       waits=[evt] + (ev_pad if i == 0 else []),
                                       marks=[(VE, 1)])
                            ev_tr_free = etv
                            ev_tr_last = evt
                            ev_pT_w.append(etv)
                        ev_prow_free = ev_tr_last
                    if debug and h == 0 and qg == 0:
                        P.do("ge", lambda e: e.dma_start(
                            out=d_pTg[:, :], in_=pTg[:, 0:4 * 512]),
                            waits=ev_pT_w + ev_pad, marks=[(S_C, 16)])
                    wv = [evv, ev_oT_ev.get(qg - 1)] + ev_pad + ev_pT_w
                    last = None
                    for kc in range(nkc_g):
                        last = P.do("te", lambda e, kc=kc, n=nkc_g: e.matmul(
                            ps[5][:], vKt[:, kc * 128:(kc + 1) * 128],
                            pTg[:, kc * 512:(kc + 1) * 512],
                            start=(kc == 0), stop=(kc == n - 1)),
                            waits=(wv if kc == 0 else []),
                            marks=([(PE, 1)] if kc == nkc_g - 1 else []))
                    ev_pT_rd = last
                    ev_pT_w = []
                    par = qg % 2
                    shard = qg // 4
                    colr = (qg * 512) % 2048
                    eo0 = P.do("ve", lambda e, par=par: e.tensor_single_scalar(
                        oTt[par][:], ps[5][:], selm[:, 0:1], op=ALU.mult),
                        waits=[last, ev_oT_st.get(qg - 2)], marks=[(VE, 1)])
                    es0 = P.do("ge", lambda e, h=h, shard=shard, colr=colr,
                               par=par: e.dma_start(
                        out=oTm[shard][h * 128:(h + 1) * 128, colr:colr + 512],
                        in_=oTt[par][:]), waits=[eo0], marks=[(S_OT[par], 16)])
                    eo1 = P.do("ve", lambda e, par=par: e.tensor_single_scalar(
                        oTt[par][:], ps[5][:], selm[:, 1:2], op=ALU.mult),
                        waits=[es0], marks=[(VE, 1)])
                    ev_oT_ev[qg] = eo1
                    ev_oT_st[qg] = P.do("ge", lambda e, h=h, shard=shard, colr=colr,
                                        par=par: e.dma_start(
                        out=oTm[shard][1024 + h * 128:1024 + (h + 1) * 128,
                                       colr:colr + 512],
                        in_=oTt[par][:]), waits=[eo1], marks=[(S_OT[par], 16)])

        ev_p2b = P.snap(*S_OT, S_VK, S_QK, S_CM)
        ev_a2a = P.do("ge", lambda e: e.collective_compute(
            "ReduceScatter", ALU.add,
            replica_groups=[[0, 1], [2, 3], [4, 5], [6, 7]],
            ins=[oTm], outs=[oTr]),
            waits=ev_p2b, marks=[(CC, 1)])

        es4 = ExitStack()
        sb4 = lambda name, shape: es4.enter_context(
            nc.sbuf_tensor(name, list(shape), F32))
        with es4:
            ot = sb4("c_ot", (128, NKC * TT))
            wob = [sb4(f"c_wo{i}", (128, NKC * 128)) for i in range(2)]
            x1c = [sb4(f"c_x1{i}", (128, TT)) for i in range(2)]
            x2t = [sb4(f"c_x2{i}", (128, TT)) for i in range(2)]

            w_ph3 = P.snap(PE, VE, SE, *DMAS)
            ev_x2w = {}
            ev_x2st = {}
            for tt in range(NTT):
                w0 = (w_ph3 + [ev_a2a]) if tt == 0 else P.snap(PE, VE, SE, *DMAS)
                ev_ot = [P.do("ge", lambda e, tt=tt: e.dma_start(
                    out=bass.AP(tensor=ot.ap().tensor, offset=0,
                                ap=[[NKC * TT, 128], [TT, 16], [1, TT]]),
                    in_=bass.AP(tensor=oTr.tensor, offset=tt * TT,
                                ap=[[T // 2, 128], [128 * (T // 2), 16],
                                    [1, TT]])),
                    waits=list(w0) + [ev_a2a], marks=[(S_XL, 16)])]
                ev_ot.append(ev_ot[0])
                ev_wo = {}
                ev_mm = {}
                for mc in range(NKC):
                    par = mc % 2
                    ev_wo[mc] = P.do("sy", lambda e, mc=mc, par=par: e.dma_start(
                        out=wob[par][:], in_=wot[mc]),
                        waits=(list(w0) if mc < 2 else []) + [ev_mm.get(mc - 2)],
                        marks=[(S_QV[par], 16)])
                    exc = P.do("ge", lambda e, mc=mc, par=par, tt=tt: e.dma_start(
                        out=x1c[par][:], in_=x1m[tt][:, mc * TT:(mc + 1) * TT]),
                        waits=(list(w0) if mc < 2 else []) + [ev_x2w.get(mc - 2)],
                        marks=[(S_XC[par], 16)])
                    last = None
                    for kc in range(NKC):
                        wsl = slice(kc * 128, (kc + 1) * 128)
                        last = P.do("te", lambda e, par=par, wsl=wsl, kc=kc, mc=mc:
                                    e.matmul(ps[mc % 2][:], wob[par][:, wsl],
                                             ot[:, kc * TT:(kc + 1) * TT],
                                             start=(kc == 0), stop=(kc == NKC - 1)),
                                    waits=([ev_wo[mc], ev_ot[0], ev_ot[1],
                                            ev_x2w.get(mc - 2)] if kc == 0 else []),
                                    marks=([(PE, 1)] if kc == NKC - 1 else []))
                    ev_mm[mc] = last
                    ev_x2w[mc] = P.do("ve", lambda e, par=par, mc=mc: e.tensor_add(
                        x2t[par][:], ps[mc % 2][:], x1c[par][:]),
                        waits=[last, exc, ev_x2st.get(mc - 2)], marks=[(VE, 1)])
                    ev_x2st[mc] = P.do("ge", lambda e, par=par, mc=mc, tt=tt:
                                       e.dma_start(
                        out=x2m[tt][:, mc * TT:(mc + 1) * TT], in_=x2t[par][:]),
                        waits=[ev_x2w[mc]], marks=[(S_ST[par], 16)])

        emit_ffn(x2m, outx, P.snap(PE, VE, SE, *DMAS), "f3")

        if debug:
            wdbg = P.snap(VE, PE, SE, *DMAS)
            for nm, (dap, src) in dbg.items():
                P.do("ge", lambda e, d=dap, s=src: e.dma_start(out=d, in_=s),
                     waits=wdbg, marks=[(S_C, 16)])

        P.do("ge", lambda e: None, waits=P.snap(CC, PE, VE, SE, *DMAS))

        with nc.Block() as block:
            for eng_name, starter in [("te", block.tensor), ("ve", block.vector),
                                      ("se", block.scalar), ("ge", block.gpsimd),
                                      ("sy", block.sync)]:
                def mk(name):
                    def f(eng):
                        done = {}
                        for o in P.ops:
                            if o.eng != name:
                                continue
                            byname = {}
                            for s, v in o.waits:
                                if v > byname.get(s.name, 0):
                                    byname[s.name] = v
                                    byname[s.name + "_s"] = s
                            for snm in list(byname):
                                if snm.endswith("_s"):
                                    continue
                                v = byname[snm]
                                if done.get(snm, -1) >= v:
                                    continue
                                eng.wait_ge(byname[snm + "_s"].h, v)
                                done[snm] = v
                            inst = o.emit(eng)
                            if inst is not None:
                                for s, inc in o.marks:
                                    inst.then_inc(s.h, inc)
                    return f
                starter(mk(eng_name))
    return nc


def _tile_feat(xt):
    ntok = xt.shape[1]
    n = ntok // TT
    return np.ascontiguousarray(
        xt.reshape(NKC, 128, n, TT).transpose(2, 1, 0, 3).reshape(n, 128, NKC * TT))


def _tile_lhsT(wT, nm):
    K, M = wT.shape
    nk = K // 128
    return np.ascontiguousarray(
        wT.reshape(nk, 128, nm, 128).transpose(2, 1, 0, 3).reshape(nm, 128, nk * 128))


def _host_inputs(L0_attn_x, L0_attn_Wq, L0_attn_Wsq, L0_attn_Wk, L0_attn_Wsk,
                 L0_attn_Wv, L0_attn_Wsv, L0_attn_Wf, L0_attn_bf, L0_attn_Wo,
                 L0_ffn_w_gate, L0_ffn_w_up, L0_ffn_w_down):
    x = np.ascontiguousarray(L0_attn_x, dtype=np.float32)
    perm = _perm_I()
    wg_p = np.asarray(L0_ffn_w_gate, np.float32)[:, perm]
    wu_p = np.asarray(L0_ffn_w_up, np.float32)[:, perm]
    wd_p = np.asarray(L0_ffn_w_down, np.float32)[perm, :]

    wgt = _tile_lhsT(wg_p, NMC)
    wut = _tile_lhsT(wu_p, NMC)
    wdt = np.ascontiguousarray(
        wd_p.reshape(NMG, 4, 128, 4, 4, 128).transpose(0, 3, 2, 4, 1, 5)
        .reshape(NMG, 4, 128, 2048))
    wot_full = _tile_lhsT(
        np.ascontiguousarray(np.asarray(L0_attn_Wo, np.float32).T), NKC)

    cos, sin = _rope_tables()
    costab = np.ascontiguousarray(np.concatenate([cos.T, cos.T], axis=0))
    sintab = np.ascontiguousarray(np.concatenate([sin.T, -sin.T], axis=0))
    qi, ki = np.meshgrid(np.arange(128), np.arange(128), indexing="ij")
    causal = np.where(ki <= qi, 0.0, NEG).astype(np.float32)
    ident = np.eye(128, dtype=np.float32)

    in_maps = []
    for c in range(8):
        b, j = c // 2, c % 2
        hsl = slice(j * HLOC, (j + 1) * HLOC)
        rsl = slice(j * HLOC * D, (j + 1) * HLOC * D)
        wq = np.ascontiguousarray(np.asarray(L0_attn_Wq, np.float32)[rsl].T)
        wk = np.ascontiguousarray(np.asarray(L0_attn_Wk, np.float32)[rsl].T)
        wv = np.ascontiguousarray(np.asarray(L0_attn_Wv, np.float32)[rsl].T)
        S = np.zeros((32, C), np.float32)
        S[0:8] = np.asarray(L0_attn_Wsq, np.float32)[hsl]
        S[8:16] = np.asarray(L0_attn_Wsk, np.float32)[hsl]
        S[16:24] = np.asarray(L0_attn_Wsv, np.float32)[hsl]
        wsmt = np.ascontiguousarray(
            S.T.reshape(NKC, 128, 32).transpose(1, 0, 2).reshape(128, NKC * 32))
        Sf = np.asarray(L0_attn_Wf, np.float32)[hsl]
        wft = np.ascontiguousarray(
            Sf.T.reshape(NKC, 128, 8).transpose(1, 0, 2).reshape(128, NKC * 8))
        nbf = -np.asarray(L0_attn_bf, np.float32)[hsl].reshape(8, 1)
        selmh = np.zeros((128, 2), np.float32)
        selmh[:, j] = 1.0
        xmine = _tile_feat(np.ascontiguousarray(x[b, j * 2048:(j + 1) * 2048, :].T))
        in_maps.append({
            "xin": xmine, "wgt": wgt, "wut": wut, "wdt": wdt,
            "wqt": _tile_lhsT(wq, HLOC), "wkt": _tile_lhsT(wk, HLOC),
            "wvt": _tile_lhsT(wv, HLOC), "wsmt": wsmt, "wft": wft, "wot": wot_full,
            "nbf": nbf, "costab": costab, "sintab": sintab,
            "causal_d": causal, "ident_d": ident, "selm": selmh,
        })
    return in_maps


def kernel(**inputs):
    in_maps = _host_inputs(**inputs)
    nc = build_nc()
    res = run_bass_kernel_spmd(nc, in_maps, core_ids=list(range(8)))
    out = np.empty((B, T, C), np.float32)
    for c in range(8):
        b, j = c // 2, c % 2
        o = np.asarray(res.results[c]["outx"])
        o = o.reshape(NTT, 128, NKC, TT).transpose(0, 3, 2, 1).reshape(2048, C)
        out[b, j * 2048:(j + 1) * 2048, :] = o
    return out


# version 10
# speedup vs baseline: 1.0277x; 1.0127x over previous
import ctypes
import numpy as np
from contextlib import ExitStack

import concourse.bass as bass
import concourse.mybir as mybir
from concourse.bass_utils import run_bass_kernel_spmd
from concourse.aot_env import allow_ici_all_to_all

F32 = mybir.dt.float32
AF = mybir.ActivationFunctionType
ALU = mybir.AluOpType
AX = mybir.AxisListType

B, T, C, H, D, II = 4, 4096, 2048, 16, 128, 8192
TT = 512
NTT = 4
HLOC = 8
NKC = C // 128
NMC = II // 128
NMG = NMC // 4
EPS = 1e-5
SCALE = float(np.float32(np.sqrt(np.float64(D))))
RSCALE = 1.0 / SCALE
NEG = -1.0e30


class Sem:
    __slots__ = ("name", "h", "n")

    def __init__(self, name):
        self.name, self.h, self.n = name, None, 0


class Op:
    __slots__ = ("eng", "waits", "emit", "marks")

    def __init__(self, eng, waits, emit, marks):
        self.eng, self.waits, self.emit, self.marks = eng, waits, emit, marks


class Prog:
    def __init__(self):
        self.ops = []
        self.sems = []

    def sem(self, name):
        s = Sem(name)
        self.sems.append(s)
        return s

    def do(self, eng, emit, waits=(), marks=()):
        evs = []
        for s, inc in marks:
            s.n += inc
            evs.append((s, s.n))
        self.ops.append(Op(eng, tuple(w for w in waits if w is not None),
                           emit, tuple(marks)))
        return evs[0] if evs else None

    def snap(self, *sems):
        return [(s, s.n) for s in sems if s.n > 0]


def _rope_tables():
    libm = ctypes.CDLL("libm.so.6")
    libm.cosf.restype = ctypes.c_float
    libm.cosf.argtypes = [ctypes.c_float]
    libm.sinf.restype = ctypes.c_float
    libm.sinf.argtypes = [ctypes.c_float]
    libm.powf.restype = ctypes.c_float
    libm.powf.argtypes = [ctypes.c_float, ctypes.c_float]
    expo = (np.arange(0, D, 2, dtype=np.float32) / np.float32(D)).astype(np.float32)
    inv = np.array([np.float32(1.0) / np.float32(libm.powf(ctypes.c_float(10000.0),
                                                           ctypes.c_float(float(e))))
                    for e in expo], dtype=np.float32)
    freqs = (np.arange(T, dtype=np.float32)[:, None] * inv[None, :]).astype(np.float32)
    cos = np.array([libm.cosf(ctypes.c_float(float(v))) for v in freqs.ravel()],
                   dtype=np.float32).reshape(freqs.shape)
    sin = np.array([libm.sinf(ctypes.c_float(float(v))) for v in freqs.ravel()],
                   dtype=np.float32).reshape(freqs.shape)
    return cos, sin


def _perm_I():
    new = np.arange(II)
    p = new % 128
    ch = new // 128
    a, k = ch // 4, ch % 4
    return 4 * (16 * p + a) + k


_XSQB = [None, None]


def build_nc(debug=False):
    nc = bass.Bass("TRN2", target_bir_lowering=False)
    P = Prog()
    PE, VE, SE, CC = (P.sem(n) for n in ["PE", "VE", "SE", "CC"])
    S_WG = [P.sem(f"sWG{i}") for i in range(2)]
    S_WU = [P.sem(f"sWU{i}") for i in range(2)]
    S_WD = [P.sem(f"sWD{i}") for i in range(2)]
    S_QV = [P.sem(f"sQV{i}") for i in range(2)]
    S_ST = [P.sem(f"sST{i}") for i in range(2)]
    S_GB = [P.sem(f"sGB{i}") for i in range(2)]
    S_OT = [P.sem(f"sOT{i}") for i in range(2)]
    S_XC = [P.sem(f"sXC{i}") for i in range(2)]
    S_XL = P.sem("sXL")
    S_SW = P.sem("sSW")
    S_GS = P.sem("sGS")
    S_CM = P.sem("sCM")
    S_QK = P.sem("sQK")
    S_VK = P.sem("sVK")
    S_B = P.sem("sB")
    S_C = P.sem("sC")
    S_CW = P.sem("sCW")
    DMAS = (S_WG + S_WU + S_WD + S_QV + S_ST + S_GB + S_OT + S_XC +
            [S_XL, S_SW, S_GS, S_CM, S_QK, S_VK, S_B, S_C, S_CW])

    def dram_in(name, shape):
        return nc.dram_tensor(name, list(shape), F32, kind="ExternalInput").ap()

    xin = dram_in("xin", (NTT, 128, NKC * TT))
    wgt = dram_in("wgt", (NMC, 128, NKC * 128))
    wut = dram_in("wut", (NMC, 128, NKC * 128))
    wdt = dram_in("wdt", (NMG, 4, 128, 2048))
    wqt = dram_in("wqt", (HLOC, 128, NKC * 128))
    wkt = dram_in("wkt", (HLOC, 128, NKC * 128))
    wvt = dram_in("wvt", (HLOC, 128, NKC * 128))
    wsmt = dram_in("wsmt", (128, NKC * 32))
    wft = dram_in("wft", (128, NKC * 8))
    wot = dram_in("wot", (NKC, 128, NKC * 128))
    nbf_d = dram_in("nbf", (8, 1))
    cost_d = dram_in("costab", (128, T))
    sint_d = dram_in("sintab", (128, T))
    caus_d = dram_in("causal_d", (128, 128))
    selm_d = dram_in("selm", (128, 2))
    iden_d = dram_in("ident_d", (128, 128))

    x1m = nc.dram_tensor("x1m", [NTT, 128, NKC * TT], F32).ap()
    x1f = nc.dram_tensor("x1f", [2, NTT, 128, NKC * TT], F32).ap()
    x2m = nc.dram_tensor("x2m", [NTT, 128, NKC * TT], F32).ap()
    qTs = nc.dram_tensor("qTs", [HLOC, 128, T], F32).ap()
    kTs = nc.dram_tensor("kTs", [HLOC, 128, T], F32).ap()
    vKM = nc.dram_tensor("vKM", [HLOC, T, 128], F32).ap()
    cums = nc.dram_tensor("cums", [HLOC, T], F32).ap()
    gsc = nc.dram_tensor("gsc", [32, TT], F32).ap()
    oTm = nc.dram_tensor("oTm", [2, C, T // 2], F32).ap()
    oTr = nc.dram_tensor("oTr", [C, T // 2], F32).ap()
    outx = nc.dram_tensor("outx", [NTT, 128, NKC * TT], F32, kind="ExternalOutput").ap()
    d_u1 = nc.dram_tensor("d_u1", [128, NKC * TT], F32, kind="ExternalOutput").ap() if debug else None
    d_h4 = nc.dram_tensor("d_h4", [128, 4 * TT], F32, kind="ExternalOutput").ap() if debug else None
    d_hpre = nc.dram_tensor("d_hpre", [128, 4 * TT], F32, kind="ExternalOutput").ap() if debug else None
    d_acc = nc.dram_tensor("d_acc", [128, NKC * TT], F32, kind="ExternalOutput").ap() if debug else None
    d_prow = nc.dram_tensor("d_prow", [4, 128, 512], F32, kind="ExternalOutput").ap() if debug else None
    d_pTg = nc.dram_tensor("d_pTg", [128, 4 * 512], F32, kind="ExternalOutput").ap() if debug else None
    dbg = {}
    if debug:
        for nm, src in [("d_x1m", x1m), ("d_qTs", qTs), ("d_kTs", kTs),
                        ("d_vKM", vKM), ("d_cums", cums), ("d_oTm", oTm),
                        ("d_x2m", x2m), ("d_x1f", x1f), ("d_oTr", oTr)]:
            dbg[nm] = (nc.dram_tensor(nm, list(src.tensor.shape), F32,
                                      kind="ExternalOutput").ap(), src)

    st = ExitStack()
    sbuf = lambda name, shape: st.enter_context(nc.sbuf_tensor(name, list(shape), F32))
    psum = lambda name, shape: st.enter_context(nc.psum_tensor(name, list(shape), F32))

    with st:
        ones1 = sbuf("ones1", (1, 128))
        onesK = sbuf("onesK", (128, 1))
        ident = sbuf("ident", (128, 128))
        causal = sbuf("causal", (128, 128))
        nbf_t = sbuf("nbf_t", (8, 1))
        selm = sbuf("selm_t", (128, 2))
        srow = sbuf("srow", (1, 5 * TT))
        epsb = sbuf("epsb", (1, 1))

        ps = [psum(f"ps{i}", (128, 512)) for i in range(6)]
        psT = psum("psT", (64, 512))

        for s in P.sems:
            s.h = st.enter_context(nc.semaphore(s.name))

        ev_ones = P.do("ve", lambda e: e.memset(ones1[:], 1.0), marks=[(VE, 1)])
        ev_eps = P.do("ve", lambda e: e.memset(epsb[:], EPS), marks=[(VE, 1)])
        ev_onesK = P.do("ve", lambda e: e.memset(onesK[:], 1.0), marks=[(VE, 1)])
        for dst, src in [(ident, iden_d), (causal, caus_d), (nbf_t, nbf_d),
                         (selm, selm_d)]:
            P.do("ge", lambda e, d=dst, s=src: e.dma_start(out=d[:], in_=s[:, :]),
                 marks=[(S_C, 16)])
        ev_const = P.snap(S_C, VE)

        def emit_ln(x_src_ap, xt, ut, w0):
            xq0, xq1 = _XSQB[0], _XSQB[1]
            xsqb = (xq0, xq1)
            ev_x = P.do("ge", lambda e: e.dma_start(out=xt[:], in_=x_src_ap),
                        waits=list(w0) + ev_const, marks=[(S_XL, 16)])
            ev_sum = None
            for kc in range(NKC):
                sl = slice(kc * TT, (kc + 1) * TT)
                ev_sum = P.do("te", lambda e, sl=sl, kc=kc: e.matmul(
                    psT[0:1, :], onesK[:, 0:1], xt[:, sl],
                    start=(kc == 0), stop=(kc == NKC - 1)),
                    waits=(list(w0) + [ev_x, ev_onesK] if kc == 0 else []),
                    marks=([(PE, 1)] if kc == NKC - 1 else []))
            prev_mm = None
            ev_sq = None
            for kc in range(NKC):
                sl = slice(kc * TT, (kc + 1) * TT)
                par = kc % 2
                es = P.do("se", lambda e, sl=sl, xq=xsqb[par]: e.activation(
                    xq[:], xt[:, sl], AF.Square),
                    waits=(list(w0) + [ev_x] if kc < 2 else [prev_mm]),
                    marks=[(SE, 1)])
                prev_mm = P.do("te", lambda e, kc=kc, xq=xsqb[par]: e.matmul(
                    psT[32:33, :], onesK[:, 0:1], xq[:],
                    start=(kc == 0), stop=(kc == NKC - 1)),
                    waits=[es], marks=[(PE, 1)])
                if kc == NKC - 1:
                    ev_sq = prev_mm
            e1 = P.do("ve", lambda e: e.tensor_scalar_mul(
                srow[0:1, 0:TT], psT[0:1, :], 1.0 / C),
                waits=list(w0) + [ev_sum], marks=[(VE, 1)])
            e2 = P.do("ve", lambda e: e.tensor_mul(
                srow[0:1, 2 * TT:3 * TT], srow[0:1, 0:TT], srow[0:1, 0:TT]),
                waits=[e1], marks=[(VE, 1)])
            e3 = P.do("ve", lambda e: e.scalar_tensor_tensor(
                srow[0:1, 3 * TT:4 * TT], psT[32:33, :], 1.0 / C,
                srow[0:1, 2 * TT:3 * TT], op0=ALU.mult, op1=ALU.subtract),
                waits=[ev_sq, e2], marks=[(VE, 1)])
            e4 = P.do("se", lambda e: e.activation(
                srow[0:1, 4 * TT:5 * TT], srow[0:1, 3 * TT:4 * TT], AF.Sqrt,
                bias=epsb[0:1, 0:1]), waits=[e3, ev_eps], marks=[(SE, 1)])
            eb1 = P.do("te", lambda e: e.matmul(
                ps[4][:], ones1[0:1, :], srow[0:1, 0:TT], start=True, stop=True),
                waits=[e1, ev_ones], marks=[(PE, 1)])
            e4r = P.do("ve", lambda e: e.reciprocal(
                srow[0:1, TT:2 * TT], srow[0:1, 4 * TT:5 * TT]),
                waits=[e4], marks=[(VE, 1)])
            eb2 = P.do("te", lambda e: e.matmul(
                ps[5][:], ones1[0:1, :], srow[0:1, TT:2 * TT],
                start=True, stop=True), waits=[e4r], marks=[(PE, 1)])
            ev_u = None
            for kc in range(NKC):
                sl = slice(kc * TT, (kc + 1) * TT)
                eu1 = P.do("ve", lambda e, sl=sl: e.tensor_sub(
                    ut[:, sl], xt[:, sl], ps[4][:]),
                    waits=([eb1] if kc == 0 else []), marks=[(VE, 1)])
                ev_u = P.do("ve", lambda e, sl=sl: e.tensor_mul(
                    ut[:, sl], ut[:, sl], ps[5][:]),
                    waits=([eb2, eu1] if kc == 0 else [eu1]), marks=[(VE, 1)])
            return ev_x, ev_u, (e1, e3)

        def emit_ffn(x_src, dst, phase_w0, pfx):
            es = ExitStack()
            sb2 = lambda name, shape: es.enter_context(
                nc.sbuf_tensor(pfx + name, list(shape), F32))
            with es:
                xt = sb2("xt", (128, NKC * TT))
                ut = sb2("ut", (128, NKC * TT))
                acc = sb2("acc", (128, NKC * TT))
                h4b = [sb2(f"h4_{i}", (128, 4 * TT)) for i in range(2)]
                habs = sb2("habs", (128, 4 * TT))
                cmp6 = sb2("cmp6", (128, 6 * TT))
                tsc = sb2("tsc", (128, TT))
                _XSQB[0] = sb2("xsq0", (128, TT))
                _XSQB[1] = sb2("xsq1", (128, TT))
                wgb = [sb2(f"wgb{i}", (128, NKC * 128)) for i in range(2)]
                wub = [sb2(f"wub{i}", (128, NKC * 128)) for i in range(2)]
                wdb = [sb2(f"wdb{i}", (128, 2048)) for i in range(2)]
                x1t = [sb2(f"x1t{i}", (128, TT)) for i in range(2)]

                for tt in range(NTT):
                    w0 = list(phase_w0) if tt == 0 else P.snap(PE, VE, SE, *DMAS)
                    ev_x, ev_u, _ = emit_ln(x_src[tt], xt, ut, w0)
                    ev_w = {}
                    ev_wd = {}
                    ev_g_done = {}
                    ev_up_done = {}
                    ev_sil = {}
                    ev_mul = {}
                    ev_dn_done = {}
                    ev_acc_w = {}
                    ev_ps_free = {}
                    ev_x1_st = {}

                    def pf_gu(mc, w=()):
                        par = mc % 2
                        ev_w[mc] = (
                            P.do("sy", lambda e, mc=mc, par=par: e.dma_start(
                                out=wgb[par][:], in_=wgt[mc]),
                                waits=list(w) + [ev_g_done.get(mc - 2)],
                                marks=[(S_WG[par], 16)]),
                            P.do("sy", lambda e, mc=mc, par=par: e.dma_start(
                                out=wub[par][:], in_=wut[mc]),
                                waits=list(w) + [ev_up_done.get(mc - 2)],
                                marks=[(S_WU[par], 16)]))

                    def pf_dn(k, w=()):
                        mg, coq = k // 4, k % 4
                        par = k % 2
                        ev_wd[k] = P.do("sy", lambda e, mg=mg, coq=coq, par=par:
                                        e.dma_start(out=wdb[par][:], in_=wdt[mg, coq]),
                                        waits=list(w) + [ev_dn_done.get(k - 2)],
                                        marks=[(S_WD[par], 16)])

                    pf_gu(0, w0)
                    pf_gu(1, w0)
                    pf_dn(0, w0)
                    pf_dn(1, w0)
                    prev_em3 = None
                    pe_down = {}
                    ev_hs_by = {}

                    def emit_gusp(mg):
                        h4 = h4b[mg % 2]
                        ev_h = {}
                        for i in range(4):
                            mc = mg * 4 + i
                            par = mc % 2
                            last = None
                            for kc in range(NKC):
                                sl = slice(kc * TT, (kc + 1) * TT)
                                wsl = slice(kc * 128, (kc + 1) * 128)
                                last = P.do("te", lambda e, par=par, sl=sl, wsl=wsl,
                                            kc=kc: e.matmul(
                                    ps[par][:], wgb[par][:, wsl], ut[:, sl],
                                    start=(kc == 0), stop=(kc == NKC - 1)),
                                    waits=([ev_w[mc][0], ev_u,
                                            ev_sil.get(mc - 2)] if kc == 0 else []),
                                    marks=([(PE, 1)] if kc == NKC - 1 else []))
                            ev_g = last
                            ev_g_done[mc] = ev_g
                            for kc in range(NKC):
                                sl = slice(kc * TT, (kc + 1) * TT)
                                wsl = slice(kc * 128, (kc + 1) * 128)
                                last = P.do("te", lambda e, par=par, sl=sl, wsl=wsl,
                                            kc=kc: e.matmul(
                                    ps[2 + par][:], wub[par][:, wsl], ut[:, sl],
                                    start=(kc == 0), stop=(kc == NKC - 1)),
                                    waits=([ev_w[mc][1],
                                            ev_mul.get(mc - 2)] if kc == 0 else []),
                                    marks=([(PE, 1)] if kc == NKC - 1 else []))
                            ev_up = last
                            ev_up_done[mc] = ev_up
                            if mc + 2 < NMC:
                                pf_gu(mc + 2)
                            hsl = slice(i * TT, (i + 1) * TT)
                            ev_sil[mc] = P.do(
                                "se", lambda e, hsl=hsl, par=par, h4=h4: e.activation(
                                    h4[:, hsl], ps[par][:], AF.Silu),
                                waits=[ev_g] + ([pe_down.get(mg - 2), prev_em3]
                                                if i == 0 else []),
                                marks=[(SE, 1)])
                            ev_mul[mc] = P.do(
                                "ve", lambda e, hsl=hsl, par=par, h4=h4: e.tensor_tensor(
                                    h4[:, hsl], h4[:, hsl], ps[2 + par][:],
                                    op=ALU.mult),
                                waits=[ev_sil[mc], ev_up], marks=[(VE, 1)])
                            ev_h[i] = ev_mul[mc]
                        A = [h4[:, i * TT:(i + 1) * TT] for i in range(4)]
                        Ab = [habs[:, i * TT:(i + 1) * TT] for i in range(4)]
                        Cp = [cmp6[:, i * TT:(i + 1) * TT] for i in range(6)]
                        eabs = P.do("se", lambda e, h4=h4: e.activation(habs[:], h4[:], AF.Abs),
                                    waits=[ev_h[0], ev_h[1], ev_h[2], ev_h[3],
                                           prev_em3], marks=[(SE, 1)])
                        pairs = [(0, 1), (0, 2), (0, 3), (1, 2), (1, 3), (2, 3)]
                        ecmp = None
                        for ci, (a, b) in enumerate(pairs):
                            ecmp = P.do("ve", lambda e, ci=ci, a=a, b=b:
                                        e.tensor_tensor(Cp[ci], Ab[a], Ab[b],
                                                        op=ALU.is_ge),
                                        waits=([eabs] if ci == 0 else []),
                                        marks=[(VE, 1)])

                        def keep(i, terms, thr, op, w):
                            e = P.do("ve", lambda e, t=terms: e.tensor_tensor(
                                tsc[:], t[0][1], t[1][1],
                                op=(ALU.add if t[1][0] > 0 else ALU.subtract)),
                                waits=w, marks=[(VE, 1)])
                            for sgn, cc in terms[2:]:
                                e = P.do("ve", lambda e, cc=cc, sgn=sgn:
                                         e.tensor_tensor(
                                             tsc[:], tsc[:], cc,
                                             op=(ALU.add if sgn > 0 else ALU.subtract)),
                                         waits=[e], marks=[(VE, 1)])
                            ek = P.do("ve", lambda e, i=i, thr=thr, op=op:
                                      e.tensor_single_scalar(Ab[i], tsc[:], thr, op=op),
                                      waits=[e], marks=[(VE, 1)])
                            return P.do("ve", lambda e, i=i: e.tensor_mul(
                                A[i], A[i], Ab[i]), waits=[ek], marks=[(VE, 1)])

                        em0 = keep(0, [(1, Cp[0]), (1, Cp[1]), (1, Cp[2])], 2.0,
                                   ALU.is_ge, [ecmp])
                        em1 = keep(1, [(1, Cp[3]), (1, Cp[4]), (-1, Cp[0])], 1.0,
                                   ALU.is_ge, [em0])
                        em2 = keep(2, [(1, Cp[5]), (-1, Cp[1]), (-1, Cp[3])], 0.0,
                                   ALU.is_ge, [em1])
                        em3 = keep(3, [(1, Cp[2]), (1, Cp[4]), (1, Cp[5])], 1.0,
                                   ALU.is_le, [em2])
                        ev_hs_by[mg] = [em0, em1, em2, em3]
                        if debug and pfx == "f1" and tt == 0 and mg == 0:
                            P.do("ge", lambda e, h4=h4: e.dma_start(out=d_u1[:, :], in_=ut[:]),
                                 waits=[ev_u], marks=[(S_C, 16)])
                            P.do("ge", lambda e, h4=h4: e.dma_start(out=d_h4[:, :], in_=h4[:]),
                                 waits=[em3], marks=[(S_C, 16)])
                        return em3

                    def emit_down(mg):
                        h4 = h4b[mg % 2]
                        ev_hs = ev_hs_by[mg]
                        for co in range(NKC):
                            coq, col = co // 4, co % 4
                            k = mg * 4 + coq
                            par = k % 2
                            pd = 4 + (co % 2)
                            last = None
                            for ic in range(4):
                                wsl = slice((col * 4 + ic) * 128,
                                            (col * 4 + ic + 1) * 128)
                                last = P.do("te", lambda e, par=par, wsl=wsl, ic=ic,
                                            pd=pd, h4=h4: e.matmul(
                                    ps[pd][:], wdb[par][:, wsl],
                                    h4[:, ic * TT:(ic + 1) * TT],
                                    start=(ic == 0), stop=(ic == 3)),
                                    waits=(([ev_wd[k]] + ev_hs +
                                            [ev_ps_free.get(pd), ev_u])
                                           if ic == 0 else []),
                                    marks=([(PE, 1)] if ic == 3 else []))
                            ev_dn_done[k] = last
                            csl = slice(co * TT, (co + 1) * TT)
                            if mg == 0:
                                ea = P.do("ve", lambda e, csl=csl, pd=pd:
                                          e.tensor_copy(acc[:, csl], ps[pd][:]),
                                          waits=[last], marks=[(VE, 1)])
                            else:
                                ea = P.do("ve", lambda e, csl=csl, pd=pd:
                                          e.tensor_add(acc[:, csl], acc[:, csl],
                                                       ps[pd][:]),
                                          waits=[last, ev_acc_w.get(co)],
                                          marks=[(VE, 1)])
                            ev_acc_w[co] = ea
                            ev_ps_free[pd] = ea
                            if col == 3 and k + 2 < NMG * 4:
                                pf_dn(k + 2)
                        pe_down[mg] = (PE, PE.n)

                    for mg in range(NMG):
                        prev_em3 = emit_gusp(mg)
                        if mg > 0:
                            emit_down(mg - 1)
                    emit_down(NMG - 1)
                    if debug and pfx == "f1" and tt == 0:
                        P.do("ge", lambda e: e.dma_start(out=d_acc[:, :], in_=acc[:]),
                             waits=[ev_acc_w[k2] for k2 in range(NKC)],
                             marks=[(S_C, 16)])
                    for co in range(NKC):
                        csl = slice(co * TT, (co + 1) * TT)
                        par = co % 2
                        er = P.do("ve", lambda e, csl=csl, par=par:
                                  e.scalar_tensor_tensor(
                                      x1t[par][:], acc[:, csl], 0.5, xt[:, csl],
                                      op0=ALU.mult, op1=ALU.add),
                                  waits=[ev_acc_w[co], ev_x1_st.get(co - 2)],
                                  marks=[(VE, 1)])
                        ev_x1_st[co] = P.do("ge", lambda e, csl=csl, par=par, tt=tt:
                                            e.dma_start(out=dst[tt][:, csl],
                                                        in_=x1t[par][:]),
                                            waits=[er], marks=[(S_ST[par], 16)])

        emit_ffn(xin, x1m, ev_const, "f1")

        ev_p1 = P.snap(*S_ST)
        with allow_ici_all_to_all():
            ev_ag = P.do("ge", lambda e: e.collective_compute(
                "AllGather", ALU.bypass,
                replica_groups=[[0, 1], [2, 3], [4, 5], [6, 7]],
                ins=[x1m], outs=[x1f]),
                waits=ev_p1, marks=[(CC, 1)])

        es2 = ExitStack()
        sb2 = lambda name, shape: es2.enter_context(
            nc.sbuf_tensor(name, list(shape), F32))
        with es2:
            xt2 = sb2("a_xt", (128, NKC * TT))
            ut2 = sb2("a_ut", (128, NKC * TT))
            _XSQB[0] = sb2("a_xsq0", (128, TT))
            _XSQB[1] = sb2("a_xsq1", (128, TT))
            qraw = sb2("a_qraw", (128, HLOC * TT))
            kraw = sb2("a_kraw", (128, HLOC * TT))
            vraw = sb2("a_vraw", (128, HLOC * TT))
            costb = sb2("a_cost", (128, T))
            sintb = sb2("a_sint", (128, T))
            wqb = [sb2(f"a_wq{i}", (128, NKC * 128)) for i in range(2)]
            wsmb = sb2("a_wsm", (128, NKC * 32))
            wfb = sb2("a_wf", (128, NKC * 8))
            sp8 = sb2("a_sp8", (8, TT))
            g24 = sb2("a_g24", (32, TT))
            ncumt = sb2("a_ncum", (8, TT))
            zeros8 = sb2("a_zeros8", (8, TT))
            car = sb2("a_car", (128, 24))
            carcum = sb2("a_carcum", (8, 1))
            sh1 = sb2("a_sh1", (128, TT))
            rt1 = sb2("a_rt1", (128, TT))
            rt2 = sb2("a_rt2", (128, TT))
            swapb = sb2("a_swap", (128, TT))
            vkt = sb2("a_vkt", (128, 4 * 128))
            gateB = [sb2(f"a_gb{i}", (128, TT)) for i in range(2)]

            w_ph = P.snap(PE, VE, SE, *DMAS)
            ev_tbl0 = P.do("ge", lambda e: e.dma_start(out=costb[:], in_=cost_d[:, :]),
                           waits=w_ph, marks=[(S_C, 16)])
            ev_tbl1 = P.do("ge", lambda e: e.dma_start(out=sintb[:], in_=sint_d[:, :]),
                           marks=[(S_C, 16)])
            ev_tbl2 = P.do("sy", lambda e: e.dma_start(out=wsmb[:], in_=wsmt[:, :]),
                           waits=w_ph, marks=[(S_CW, 16)])
            ev_tbl3 = P.do("sy", lambda e: e.dma_start(out=wfb[:], in_=wft[:, :]),
                           marks=[(S_CW, 16)])
            ev_tbl = [ev_tbl1, ev_tbl1, ev_tbl3, ev_tbl3]
            ev_z = [P.do("ve", lambda e: e.memset(zeros8[:], 0.0), waits=w_ph,
                         marks=[(VE, 1)]),
                    P.do("ve", lambda e: e.memset(car[:], 0.0), marks=[(VE, 1)]),
                    P.do("ve", lambda e: e.memset(carcum[:], 0.0), marks=[(VE, 1)])]

            ev_cum_rd = None
            e_sp_prev = None
            ev_vk_st = None
            for g in range(2 * NTT):
                w0 = (w_ph + [ev_ag]) if g == 0 else P.snap(PE, VE, SE, *DMAS)
                ev_x, ev_u, psT_rd = emit_ln(x1f[g // NTT, g % NTT], xt2, ut2, w0)
                ev_raw = {}
                ev_wq = {}
                for im, (wsrc, rawt) in enumerate([(wqt, qraw), (wkt, kraw),
                                                   (wvt, vraw)]):
                    for mh in range(HLOC):
                        idx = im * HLOC + mh
                        par = idx % 2
                        ev_wq[idx] = P.do(
                            "sy", lambda e, wsrc=wsrc, mh=mh, par=par: e.dma_start(
                                out=wqb[par][:], in_=wsrc[mh]),
                            waits=(w0 if idx < 2 else []) +
                            [ev_raw.get(idx - 2)], marks=[(S_QV[par], 16)])
                        last = None
                        for kc in range(NKC):
                            sl = slice(kc * TT, (kc + 1) * TT)
                            wsl = slice(kc * 128, (kc + 1) * 128)
                            last = P.do("te", lambda e, par=par, sl=sl, wsl=wsl,
                                        kc=kc, q=idx % 2: e.matmul(
                                ps[q][:], wqb[par][:, wsl], ut2[:, sl],
                                start=(kc == 0), stop=(kc == NKC - 1)),
                                waits=([ev_wq[idx], ev_u, ev_raw.get(idx - 2)]
                                       if kc == 0 else []),
                                marks=([(PE, 1)] if kc == NKC - 1 else []))
                        hs = slice(mh * TT, (mh + 1) * TT)
                        ev_raw[idx] = P.do(
                            "ve", lambda e, rawt=rawt, hs=hs, q=idx % 2:
                            e.tensor_copy(rawt[:, hs], ps[q][:]),
                            waits=[last], marks=[(VE, 1)])
                last = None
                for kc in range(NKC):
                    sl = slice(kc * TT, (kc + 1) * TT)
                    wsl = slice(kc * 32, (kc + 1) * 32)
                    last = P.do("te", lambda e, sl=sl, wsl=wsl, kc=kc: e.matmul(
                        psT[0:32, :], wsmb[:, wsl], ut2[:, sl],
                        start=(kc == 0), stop=(kc == NKC - 1)),
                        waits=([ev_tbl[2], ev_u, psT_rd[0], psT_rd[1]]
                               if kc == 0 else []),
                        marks=([(PE, 1)] if kc == NKC - 1 else []))
                lastf = None
                for kc in range(NKC):
                    sl = slice(kc * TT, (kc + 1) * TT)
                    wsl = slice(kc * 8, (kc + 1) * 8)
                    lastf = P.do("te", lambda e, sl=sl, wsl=wsl, kc=kc: e.matmul(
                        ps[2][0:8, :], wfb[:, wsl], ut2[:, sl],
                        start=(kc == 0), stop=(kc == NKC - 1)),
                        waits=([ev_tbl[3], ev_u, e_sp_prev] if kc == 0 else []),
                        marks=([(PE, 1)] if kc == NKC - 1 else []))
                e_sg = P.do("se", lambda e: e.activation(
                    g24[0:32, :], psT[0:32, :], AF.Sigmoid),
                    waits=[last], marks=[(SE, 1)])
                e_spa = P.do("se", lambda e: e.activation(
                    sp8[:, :], ps[2][0:8, :], AF.Exp,
                    bias=nbf_t[0:8, 0:1], scale=-1.0),
                    waits=[lastf], marks=[(SE, 1)])
                e_spb = P.do("ve", lambda e: e.tensor_scalar_add(
                    sp8[:, :], sp8[:, :], 1.0), waits=[e_spa], marks=[(VE, 1)])
                e_sp = P.do("se", lambda e: e.activation(
                    sp8[:, :], sp8[:, :], AF.Ln), waits=[e_spb], marks=[(SE, 1)])
                e_sp_prev = e_spa
                e_scan = P.do("ve", lambda e: e.tensor_tensor_scan(
                    ncumt[:], sp8[:, :], zeros8[:], carcum[:, 0:1],
                    op0=ALU.add, op1=ALU.add),
                    waits=[e_sp, ev_z[0], ev_z[2], ev_cum_rd], marks=[(VE, 1)])
                e_carc = P.do("ve", lambda e: e.tensor_copy(
                    carcum[:, 0:1], ncumt[:, TT - 1:TT]), waits=[e_scan],
                    marks=[(VE, 1)])
                ev_cum_rd = P.do("ge", lambda e, g=g: e.dma_start(
                    out=cums[:, g * TT:(g + 1) * TT], in_=ncumt[:]),
                    waits=[e_carc], marks=[(S_CM, 16)])
                e_g2d = P.do("ge", lambda e: e.dma_start(out=gsc[:, :], in_=g24[:]),
                             waits=[e_sg, e_sp], marks=[(S_GS, 16)])
                gate_reads = []
                for mh in range(HLOC):
                    hs = slice(mh * TT, (mh + 1) * TT)
                    for im, rawt in enumerate([qraw, kraw, vraw]):
                        r = im * 8 + mh
                        ci = slice(r, r + 1)
                        ev_y = ev_raw[im * HLOC + mh]
                        ec1 = P.do("ve", lambda e, rawt=rawt, hs=hs: e.tensor_copy(
                            sh1[:, 1:TT], rawt[:, hs.start:hs.stop - 1]),
                            waits=[ev_y] +
                            ([gate_reads[-1]] if gate_reads else []),
                            marks=[(VE, 1)])
                        ec2 = P.do("ve", lambda e, ci=ci: e.tensor_copy(
                            sh1[:, 0:1], car[:, ci]),
                            waits=[ec1, ev_z[1]], marks=[(VE, 1)])
                        ec3 = P.do("ve", lambda e, ci=ci, rawt=rawt, hs=hs:
                                   e.tensor_copy(car[:, ci],
                                                 rawt[:, hs.stop - 1:hs.stop]),
                                   waits=[ev_y, ec2], marks=[(VE, 1)])
                        ec4 = P.do("ve", lambda e, rawt=rawt, hs=hs: e.tensor_sub(
                            sh1[:], sh1[:], rawt[:, hs]), waits=[ec3],
                            marks=[(VE, 1)])
                        pq = r % 2
                        eb = P.do("ge", lambda e, r=r, pq=pq: e.dma_start(
                            out=gateB[pq][:],
                            in_=bass.AP(tensor=gsc.tensor, offset=r * TT,
                                        ap=[[0, 128], [1, TT]])),
                            waits=[e_g2d] +
                            ([gate_reads[-2]] if len(gate_reads) >= 2 else []),
                            marks=[(S_GB[pq], 16)])
                        ec5 = P.do("ve", lambda e, pq=pq: e.tensor_mul(
                            sh1[:], sh1[:], gateB[pq][:]), waits=[ec4, eb],
                            marks=[(VE, 1)])
                        gate_reads.append(ec5)
                        ec6 = P.do("ve", lambda e, rawt=rawt, hs=hs: e.tensor_add(
                            rawt[:, hs], rawt[:, hs], sh1[:]), waits=[ec5],
                            marks=[(VE, 1)])
                        if im < 2:
                            es1 = P.do("ge", lambda e, rawt=rawt, hs=hs: e.dma_start(
                                out=swapb[0:64, :], in_=rawt[64:128, hs]),
                                waits=[ec6], marks=[(S_SW, 16)])
                            es2_ = P.do("ge", lambda e, rawt=rawt, hs=hs: e.dma_start(
                                out=swapb[64:128, :], in_=rawt[0:64, hs]),
                                waits=[ec6], marks=[(S_SW, 16)])
                            csl = slice(g * TT, (g + 1) * TT)
                            er1 = P.do("ve", lambda e, rawt=rawt, hs=hs, csl=csl:
                                       e.tensor_mul(rt1[:], rawt[:, hs],
                                                    costb[:, csl]),
                                       waits=[ec6] +
                                       ([ev_tbl[0]] if g == 0 else []),
                                       marks=[(VE, 1)])
                            er2 = P.do("ve", lambda e, csl=csl: e.tensor_mul(
                                rt2[:], swapb[:], sintb[:, csl]),
                                waits=[es1, es2_] +
                                ([ev_tbl[1]] if g == 0 else []),
                                marks=[(VE, 1)])
                            er3 = P.do("ve", lambda e, rawt=rawt, hs=hs:
                                       e.tensor_add(rawt[:, hs], rt1[:], rt2[:]),
                                       waits=[er1, er2], marks=[(VE, 1)])
                            dstd = qTs if im == 0 else kTs
                            P.do("ge", lambda e, d=dstd, mh=mh, hs=hs, g=g,
                                 rawt=rawt: e.dma_start(
                                out=d[mh][:, g * TT:(g + 1) * TT],
                                in_=rawt[:, hs]),
                                waits=[er3], marks=[(S_QK, 16)])
                        else:
                            evt = None
                            for blk in range(4):
                                evt = P.do("te", lambda e, hs=hs, blk=blk:
                                           e.transpose(
                                    ps[5][:, blk * 128:(blk + 1) * 128],
                                    vraw[:, hs.start + blk * 128:
                                         hs.start + (blk + 1) * 128],
                                    ident[:]),
                                    waits=([ec6, ev_u] if blk == 0 else []),
                                    marks=([(PE, 1)] if blk == 3 else []))
                            evk = P.do("ve", lambda e: e.tensor_copy(
                                vkt[:], ps[5][:]),
                                waits=[evt, ev_vk_st], marks=[(VE, 1)])
                            ev_vk_st = P.do("ge", lambda e, mh=mh, g=g: e.dma_start(
                                out=bass.AP(tensor=vKM.tensor,
                                            offset=(mh * T + g * TT) * 128,
                                            ap=[[128, 128], [128 * 128, 4],
                                                [1, 128]]),
                                in_=bass.AP(tensor=vkt.ap().tensor, offset=0,
                                            ap=[[512, 128], [128, 4], [1, 128]])),
                                waits=[evk], marks=[(S_VK, 16)])

        es3 = ExitStack()
        sb3 = lambda name, shape: es3.enter_context(
            nc.sbuf_tensor(name, list(shape), F32))
        with es3:
            qTt = sb3("b_qT", (128, T))
            kTt = sb3("b_kT", (128, T))
            vKt = sb3("b_vK", (128, T))
            cumB = sb3("b_cumB", (128, T))
            cumr = sb3("b_cumr", (1, T))
            prow2 = [sb3(f"b_prow{i}", (128, T)) for i in range(2)]
            pTg = sb3("b_pTg", (128, 32 * 512))
            mrow = sb3("b_mrow", (128, 4))
            oTt = [sb3(f"b_oT{i}", (128, 512)) for i in range(2)]

            w_ph2 = P.snap(PE, VE, SE, *DMAS)
            ev_exp_done = None
            for h in range(HLOC):
                w0 = w_ph2 if h == 0 else P.snap(PE, VE, SE, *DMAS)
                eq = P.do("sy", lambda e, h=h: e.dma_start(out=qTt[:], in_=qTs[h]),
                          waits=w0, marks=[(S_B, 16)])
                ek = P.do("sy", lambda e, h=h: e.dma_start(out=kTt[:], in_=kTs[h]),
                          marks=[(S_B, 16)])
                evv = P.do("sy", lambda e, h=h: e.dma_start(
                    out=bass.AP(tensor=vKt.ap().tensor, offset=0,
                                ap=[[T, 128], [128, 32], [1, 128]]),
                    in_=bass.AP(tensor=vKM.tensor, offset=h * T * 128,
                                ap=[[128, 128], [128 * 128, 32], [1, 128]])),
                    marks=[(S_B, 16)])
                ecr = P.do("sy", lambda e, h=h: e.dma_start(
                    out=cumr[0:1, :], in_=cums[h:h + 1, :]), marks=[(S_B, 16)])
                eq = ek = evv = ecr
                ev_cb = []
                for kci in range(8):
                    sl = slice(kci * 512, (kci + 1) * 512)
                    pq = kci % 2
                    ebm = P.do("te", lambda e, sl=sl, pq=pq: e.matmul(
                        ps[pq][:], ones1[0:1, :], cumr[0:1, sl],
                        start=True, stop=True),
                        waits=(w0 if kci == 0 else []) +
                        [ecr] + ([ev_cb[kci - 2]] if kci >= 2 else []),
                        marks=[(PE, 1)])
                    ev_cb.append(P.do("ve", lambda e, sl=sl, pq=pq: e.tensor_copy(
                        cumB[:, sl], ps[pq][:]),
                        waits=[ebm] + (w0 if kci == 0 else []), marks=[(VE, 1)]))
                ev_ps_s = {0: ev_cb[6], 1: ev_cb[7], 2: None, 3: None}
                ev_pT_rd = None
                ev_oT_st = {}
                ev_oT_ev = {}
                ev_tr_free = None
                ev_prow_free = [None, None]
                ev_exp_by = [None, None]
                ev_edv = {}
                ev_pT_w = []
                first_mm = True

                def emit_s1s2(qb):
                    nonlocal first_mm
                    par = qb % 2
                    prow = prow2[par]
                    klen = (qb + 1) * 128
                    nk5 = (klen + 511) // 512
                    ev_evac = None
                    for kci in range(nk5):
                        n = min(512, klen - kci * 512)
                        pq = kci % 4
                        sl = slice(kci * 512, kci * 512 + n)
                        em = P.do("te", lambda e, qb=qb, sl=sl, pq=pq, n=n:
                                  e.matmul(ps[pq][:, 0:n],
                                           qTt[:, qb * 128:(qb + 1) * 128],
                                           kTt[:, sl], start=True, stop=True),
                                  waits=([eq, ek] if first_mm else []) +
                                  [ev_ps_s.get(pq)],
                                  marks=[(PE, 1)])
                        first_mm = False
                        ev_evac = P.do("ve", lambda e, sl=sl, pq=pq, n=n, prow=prow:
                                       e.scalar_tensor_tensor(
                                           prow[:, sl], ps[pq][:, 0:n], RSCALE,
                                           cumB[:, sl], op0=ALU.mult,
                                           op1=ALU.add),
                                       waits=[em, ev_cb[7], ev_exp_by[par],
                                              ev_prow_free[par] if kci == 0
                                              else None],
                                       marks=[(VE, 1)])
                        ev_ps_s[pq] = ev_evac
                    edg = P.do("ve", lambda e, klen=klen, prow=prow: e.tensor_add(
                        prow[:, klen - 128:klen], prow[:, klen - 128:klen],
                        causal[:]), waits=[ev_evac], marks=[(VE, 1)])
                    enm = P.do("ve", lambda e, klen=klen, prow=prow, par=par:
                               e.reduce_max(
                        mrow[:, par:par + 1], prow[:, 0:klen], axis=AX.X,
                        negate=True), waits=[edg], marks=[(VE, 1)])
                    eex = P.do("se", lambda e, klen=klen, prow=prow, par=par:
                               e.activation(
                        prow[:, 0:klen], prow[:, 0:klen], AF.Exp,
                        bias=mrow[:, par:par + 1]), waits=[enm], marks=[(SE, 1)])
                    ev_exp_by[par] = eex
                    esm = P.do("ve", lambda e, klen=klen, prow=prow, par=par:
                               e.reduce_sum(
                        mrow[:, 2 + par:3 + par], prow[:, 0:klen], axis=AX.X),
                        waits=[eex], marks=[(VE, 1)])
                    erc = P.do("ve", lambda e, par=par: e.reciprocal(
                        mrow[:, 2 + par:3 + par], mrow[:, 2 + par:3 + par]),
                        waits=[esm], marks=[(VE, 1)])
                    ev_edv[qb] = P.do("ve", lambda e, klen=klen, prow=prow, par=par:
                                      e.tensor_single_scalar(
                        prow[:, 0:klen], prow[:, 0:klen],
                        mrow[:, 2 + par:3 + par], op=ALU.mult),
                        waits=[erc], marks=[(VE, 1)])

                def emit_s3(qb):
                    nonlocal ev_tr_free
                    par = qb % 2
                    prow = prow2[par]
                    klen = (qb + 1) * 128
                    i = qb % 4
                    nkc = klen // 128
                    ev_tr_last = None
                    for kc0 in range(0, nkc, 4):
                        nblk = min(4, nkc - kc0)
                        evt = None
                        for bb in range(nblk):
                            kc = kc0 + bb
                            evt = P.do("te", lambda e, kc=kc, bb=bb, prow=prow:
                                       e.transpose(
                                ps[4][:, bb * 128:(bb + 1) * 128],
                                prow[:, kc * 128:(kc + 1) * 128], ident[:]),
                                waits=([ev_edv[qb], ev_tr_free, ev_pT_rd]
                                       if bb == 0 else []),
                                marks=([(PE, 1)] if bb == nblk - 1 else []))
                        etv = P.do("ve", lambda e, kc0=kc0, i=i, nblk=nblk:
                                   e.tensor_copy(
                                       bass.AP(tensor=pTg.ap().tensor,
                                               offset=kc0 * 512 + i * 128,
                                               ap=[[32 * 512, 128],
                                                   [512, nblk], [1, 128]]),
                                       ps[4][:, 0:nblk * 128]),
                                   waits=[evt], marks=[(VE, 1)])
                        ev_tr_free = etv
                        ev_tr_last = evt
                        ev_pT_w.append(etv)
                    ev_prow_free[par] = ev_tr_last

                def emit_pads(qg):
                    evs = []
                    for i in range(3):
                        qb = qg * 4 + i
                        cnt = 3 - i
                        evs.append(P.do("ve", lambda e, qb=qb, i=i, cnt=cnt:
                                        e.memset(bass.AP(
                                            tensor=pTg.ap().tensor,
                                            offset=(qb + 1) * 512 + i * 128,
                                            ap=[[32 * 512, 128], [512, cnt],
                                                [1, 128]]),
                                            0.0),
                                        waits=([ev_pT_rd] if i == 0 else []),
                                        marks=[(VE, 1)]))
                    return evs

                def emit_pv(qg):
                    nonlocal ev_pT_rd, ev_pT_w
                    nkc_g = 4 * qg + 4
                    wv = [evv, ev_oT_ev.get(qg - 1)] + ev_pad_by[qg] + ev_pT_w
                    last = None
                    for kc in range(nkc_g):
                        last = P.do("te", lambda e, kc=kc, n=nkc_g: e.matmul(
                            ps[5][:], vKt[:, kc * 128:(kc + 1) * 128],
                            pTg[:, kc * 512:(kc + 1) * 512],
                            start=(kc == 0), stop=(kc == n - 1)),
                            waits=(wv if kc == 0 else []),
                            marks=([(PE, 1)] if kc == nkc_g - 1 else []))
                    ev_pT_rd = last
                    ev_pT_w = []
                    par = qg % 2
                    shard = qg // 4
                    colr = (qg * 512) % 2048
                    eo0 = P.do("ve", lambda e, par=par: e.tensor_single_scalar(
                        oTt[par][:], ps[5][:], selm[:, 0:1], op=ALU.mult),
                        waits=[last, ev_oT_st.get(qg - 2)], marks=[(VE, 1)])
                    es0 = P.do("ge", lambda e, h=h, shard=shard, colr=colr,
                               par=par: e.dma_start(
                        out=oTm[shard][h * 128:(h + 1) * 128, colr:colr + 512],
                        in_=oTt[par][:]), waits=[eo0], marks=[(S_OT[par], 16)])
                    eo1 = P.do("ve", lambda e, par=par: e.tensor_single_scalar(
                        oTt[par][:], ps[5][:], selm[:, 1:2], op=ALU.mult),
                        waits=[es0], marks=[(VE, 1)])
                    ev_oT_ev[qg] = eo1
                    ev_oT_st[qg] = P.do("ge", lambda e, h=h, shard=shard,
                                        colr=colr, par=par: e.dma_start(
                        out=oTm[shard][1024 + h * 128:1024 + (h + 1) * 128,
                                       colr:colr + 512],
                        in_=oTt[par][:]), waits=[eo1], marks=[(S_OT[par], 16)])

                ev_pad_by = {}
                for qb in range(32):
                    emit_s1s2(qb)
                    if qb >= 1:
                        emit_s3(qb - 1)
                        if (qb - 1) % 4 == 3:
                            emit_pv((qb - 1) // 4)
                    if qb % 4 == 0:
                        ev_pad_by[qb // 4] = emit_pads(qb // 4)
                emit_s3(31)
                emit_pv(7)
        ev_p2b = P.snap(*S_OT, S_VK, S_QK, S_CM)
        ev_a2a = P.do("ge", lambda e: e.collective_compute(
            "ReduceScatter", ALU.add,
            replica_groups=[[0, 1], [2, 3], [4, 5], [6, 7]],
            ins=[oTm], outs=[oTr]),
            waits=ev_p2b, marks=[(CC, 1)])

        es4 = ExitStack()
        sb4 = lambda name, shape: es4.enter_context(
            nc.sbuf_tensor(name, list(shape), F32))
        with es4:
            ot = sb4("c_ot", (128, NKC * TT))
            wob = [sb4(f"c_wo{i}", (128, NKC * 128)) for i in range(2)]
            x1c = [sb4(f"c_x1{i}", (128, TT)) for i in range(2)]
            x2t = [sb4(f"c_x2{i}", (128, TT)) for i in range(2)]

            w_ph3 = P.snap(PE, VE, SE, *DMAS)
            ev_x2w = {}
            ev_x2st = {}
            for tt in range(NTT):
                w0 = (w_ph3 + [ev_a2a]) if tt == 0 else P.snap(PE, VE, SE, *DMAS)
                ev_ot = [P.do("ge", lambda e, tt=tt: e.dma_start(
                    out=bass.AP(tensor=ot.ap().tensor, offset=0,
                                ap=[[NKC * TT, 128], [TT, 16], [1, TT]]),
                    in_=bass.AP(tensor=oTr.tensor, offset=tt * TT,
                                ap=[[T // 2, 128], [128 * (T // 2), 16],
                                    [1, TT]])),
                    waits=list(w0) + [ev_a2a], marks=[(S_XL, 16)])]
                ev_ot.append(ev_ot[0])
                ev_wo = {}
                ev_mm = {}
                for mc in range(NKC):
                    par = mc % 2
                    ev_wo[mc] = P.do("sy", lambda e, mc=mc, par=par: e.dma_start(
                        out=wob[par][:], in_=wot[mc]),
                        waits=(list(w0) if mc < 2 else []) + [ev_mm.get(mc - 2)],
                        marks=[(S_QV[par], 16)])
                    exc = P.do("ge", lambda e, mc=mc, par=par, tt=tt: e.dma_start(
                        out=x1c[par][:], in_=x1m[tt][:, mc * TT:(mc + 1) * TT]),
                        waits=(list(w0) if mc < 2 else []) + [ev_x2w.get(mc - 2)],
                        marks=[(S_XC[par], 16)])
                    last = None
                    for kc in range(NKC):
                        wsl = slice(kc * 128, (kc + 1) * 128)
                        last = P.do("te", lambda e, par=par, wsl=wsl, kc=kc, mc=mc:
                                    e.matmul(ps[mc % 2][:], wob[par][:, wsl],
                                             ot[:, kc * TT:(kc + 1) * TT],
                                             start=(kc == 0), stop=(kc == NKC - 1)),
                                    waits=([ev_wo[mc], ev_ot[0], ev_ot[1],
                                            ev_x2w.get(mc - 2)] if kc == 0 else []),
                                    marks=([(PE, 1)] if kc == NKC - 1 else []))
                    ev_mm[mc] = last
                    ev_x2w[mc] = P.do("ve", lambda e, par=par, mc=mc: e.tensor_add(
                        x2t[par][:], ps[mc % 2][:], x1c[par][:]),
                        waits=[last, exc, ev_x2st.get(mc - 2)], marks=[(VE, 1)])
                    ev_x2st[mc] = P.do("ge", lambda e, par=par, mc=mc, tt=tt:
                                       e.dma_start(
                        out=x2m[tt][:, mc * TT:(mc + 1) * TT], in_=x2t[par][:]),
                        waits=[ev_x2w[mc]], marks=[(S_ST[par], 16)])

        emit_ffn(x2m, outx, P.snap(PE, VE, SE, *DMAS), "f3")

        if debug:
            wdbg = P.snap(VE, PE, SE, *DMAS)
            for nm, (dap, src) in dbg.items():
                P.do("ge", lambda e, d=dap, s=src: e.dma_start(out=d, in_=s),
                     waits=wdbg, marks=[(S_C, 16)])

        P.do("ge", lambda e: None, waits=P.snap(CC, PE, VE, SE, *DMAS))

        with nc.Block() as block:
            for eng_name, starter in [("te", block.tensor), ("ve", block.vector),
                                      ("se", block.scalar), ("ge", block.gpsimd),
                                      ("sy", block.sync)]:
                def mk(name):
                    def f(eng):
                        done = {}
                        for o in P.ops:
                            if o.eng != name:
                                continue
                            byname = {}
                            for s, v in o.waits:
                                if v > byname.get(s.name, 0):
                                    byname[s.name] = v
                                    byname[s.name + "_s"] = s
                            for snm in list(byname):
                                if snm.endswith("_s"):
                                    continue
                                v = byname[snm]
                                if done.get(snm, -1) >= v:
                                    continue
                                eng.wait_ge(byname[snm + "_s"].h, v)
                                done[snm] = v
                            inst = o.emit(eng)
                            if inst is not None:
                                for s, inc in o.marks:
                                    inst.then_inc(s.h, inc)
                    return f
                starter(mk(eng_name))
    return nc


def _tile_feat(xt):
    ntok = xt.shape[1]
    n = ntok // TT
    return np.ascontiguousarray(
        xt.reshape(NKC, 128, n, TT).transpose(2, 1, 0, 3).reshape(n, 128, NKC * TT))


def _tile_lhsT(wT, nm):
    K, M = wT.shape
    nk = K // 128
    return np.ascontiguousarray(
        wT.reshape(nk, 128, nm, 128).transpose(2, 1, 0, 3).reshape(nm, 128, nk * 128))


def _host_inputs(L0_attn_x, L0_attn_Wq, L0_attn_Wsq, L0_attn_Wk, L0_attn_Wsk,
                 L0_attn_Wv, L0_attn_Wsv, L0_attn_Wf, L0_attn_bf, L0_attn_Wo,
                 L0_ffn_w_gate, L0_ffn_w_up, L0_ffn_w_down):
    x = np.ascontiguousarray(L0_attn_x, dtype=np.float32)
    perm = _perm_I()
    wg_p = np.asarray(L0_ffn_w_gate, np.float32)[:, perm]
    wu_p = np.asarray(L0_ffn_w_up, np.float32)[:, perm]
    wd_p = np.asarray(L0_ffn_w_down, np.float32)[perm, :]

    wgt = _tile_lhsT(wg_p, NMC)
    wut = _tile_lhsT(wu_p, NMC)
    wdt = np.ascontiguousarray(
        wd_p.reshape(NMG, 4, 128, 4, 4, 128).transpose(0, 3, 2, 4, 1, 5)
        .reshape(NMG, 4, 128, 2048))
    wot_full = _tile_lhsT(
        np.ascontiguousarray(np.asarray(L0_attn_Wo, np.float32).T), NKC)

    cos, sin = _rope_tables()
    costab = np.ascontiguousarray(np.concatenate([cos.T, cos.T], axis=0))
    sintab = np.ascontiguousarray(np.concatenate([sin.T, -sin.T], axis=0))
    qi, ki = np.meshgrid(np.arange(128), np.arange(128), indexing="ij")
    causal = np.where(ki <= qi, 0.0, NEG).astype(np.float32)
    ident = np.eye(128, dtype=np.float32)

    in_maps = []
    for c in range(8):
        b, j = c // 2, c % 2
        hsl = slice(j * HLOC, (j + 1) * HLOC)
        rsl = slice(j * HLOC * D, (j + 1) * HLOC * D)
        wq = np.ascontiguousarray(np.asarray(L0_attn_Wq, np.float32)[rsl].T)
        wk = np.ascontiguousarray(np.asarray(L0_attn_Wk, np.float32)[rsl].T)
        wv = np.ascontiguousarray(np.asarray(L0_attn_Wv, np.float32)[rsl].T)
        S = np.zeros((32, C), np.float32)
        S[0:8] = np.asarray(L0_attn_Wsq, np.float32)[hsl]
        S[8:16] = np.asarray(L0_attn_Wsk, np.float32)[hsl]
        S[16:24] = np.asarray(L0_attn_Wsv, np.float32)[hsl]
        wsmt = np.ascontiguousarray(
            S.T.reshape(NKC, 128, 32).transpose(1, 0, 2).reshape(128, NKC * 32))
        Sf = np.asarray(L0_attn_Wf, np.float32)[hsl]
        wft = np.ascontiguousarray(
            Sf.T.reshape(NKC, 128, 8).transpose(1, 0, 2).reshape(128, NKC * 8))
        nbf = -np.asarray(L0_attn_bf, np.float32)[hsl].reshape(8, 1)
        selmh = np.zeros((128, 2), np.float32)
        selmh[:, j] = 1.0
        xmine = _tile_feat(np.ascontiguousarray(x[b, j * 2048:(j + 1) * 2048, :].T))
        in_maps.append({
            "xin": xmine, "wgt": wgt, "wut": wut, "wdt": wdt,
            "wqt": _tile_lhsT(wq, HLOC), "wkt": _tile_lhsT(wk, HLOC),
            "wvt": _tile_lhsT(wv, HLOC), "wsmt": wsmt, "wft": wft, "wot": wot_full,
            "nbf": nbf, "costab": costab, "sintab": sintab,
            "causal_d": causal, "ident_d": ident, "selm": selmh,
        })
    return in_maps


def kernel(**inputs):
    in_maps = _host_inputs(**inputs)
    nc = build_nc()
    res = run_bass_kernel_spmd(nc, in_maps, core_ids=list(range(8)))
    out = np.empty((B, T, C), np.float32)
    for c in range(8):
        b, j = c // 2, c % 2
        o = np.asarray(res.results[c]["outx"])
        o = o.reshape(NTT, 128, NKC, TT).transpose(0, 3, 2, 1).reshape(2048, C)
        out[b, j * 2048:(j + 1) * 2048, :] = o
    return out


# version 11
# speedup vs baseline: 1.1324x; 1.1019x over previous
import ctypes
import numpy as np
from contextlib import ExitStack

import concourse.bass as bass
import concourse.mybir as mybir
from concourse.bass_utils import run_bass_kernel_spmd
from concourse.aot_env import allow_ici_all_to_all

F32 = mybir.dt.float32
AF = mybir.ActivationFunctionType
ALU = mybir.AluOpType
AX = mybir.AxisListType

B, T, C, H, D, II = 4, 4096, 2048, 16, 128, 8192
TT = 512
NTT = 4
HLOC = 8
NKC = C // 128
NMC = II // 128
NMG = NMC // 4
EPS = 1e-5
SCALE = float(np.float32(np.sqrt(np.float64(D))))
RSCALE = 1.0 / SCALE
NEG = -1.0e30


class Sem:
    __slots__ = ("name", "h", "n")

    def __init__(self, name):
        self.name, self.h, self.n = name, None, 0


class Op:
    __slots__ = ("eng", "waits", "emit", "marks")

    def __init__(self, eng, waits, emit, marks):
        self.eng, self.waits, self.emit, self.marks = eng, waits, emit, marks


class Prog:
    def __init__(self):
        self.ops = []
        self.sems = []

    def sem(self, name):
        s = Sem(name)
        self.sems.append(s)
        return s

    def do(self, eng, emit, waits=(), marks=()):
        evs = []
        for s, inc in marks:
            s.n += inc
            evs.append((s, s.n))
        self.ops.append(Op(eng, tuple(w for w in waits if w is not None),
                           emit, tuple(marks)))
        return evs[0] if evs else None

    def snap(self, *sems):
        return [(s, s.n) for s in sems if s.n > 0]


def _rope_tables():
    libm = ctypes.CDLL("libm.so.6")
    libm.cosf.restype = ctypes.c_float
    libm.cosf.argtypes = [ctypes.c_float]
    libm.sinf.restype = ctypes.c_float
    libm.sinf.argtypes = [ctypes.c_float]
    libm.powf.restype = ctypes.c_float
    libm.powf.argtypes = [ctypes.c_float, ctypes.c_float]
    expo = (np.arange(0, D, 2, dtype=np.float32) / np.float32(D)).astype(np.float32)
    inv = np.array([np.float32(1.0) / np.float32(libm.powf(ctypes.c_float(10000.0),
                                                           ctypes.c_float(float(e))))
                    for e in expo], dtype=np.float32)
    freqs = (np.arange(T, dtype=np.float32)[:, None] * inv[None, :]).astype(np.float32)
    cos = np.array([libm.cosf(ctypes.c_float(float(v))) for v in freqs.ravel()],
                   dtype=np.float32).reshape(freqs.shape)
    sin = np.array([libm.sinf(ctypes.c_float(float(v))) for v in freqs.ravel()],
                   dtype=np.float32).reshape(freqs.shape)
    return cos, sin


def _perm_I():
    new = np.arange(II)
    p = new % 128
    ch = new // 128
    a, k = ch // 4, ch % 4
    return 4 * (16 * p + a) + k


_XSQB = [None, None]


def build_nc(debug=False):
    nc = bass.Bass("TRN2", target_bir_lowering=False)
    P = Prog()
    PE, VE, SE, CC = (P.sem(n) for n in ["PE", "VE", "SE", "CC"])
    S_WG = [P.sem(f"sWG{i}") for i in range(2)]
    S_WU = [P.sem(f"sWU{i}") for i in range(2)]
    S_WD = [P.sem(f"sWD{i}") for i in range(2)]
    S_QV = [P.sem(f"sQV{i}") for i in range(2)]
    S_ST = [P.sem(f"sST{i}") for i in range(2)]
    S_GB = [P.sem(f"sGB{i}") for i in range(2)]
    S_OT = [P.sem(f"sOT{i}") for i in range(2)]
    S_XC = [P.sem(f"sXC{i}") for i in range(2)]
    S_XL = P.sem("sXL")
    S_SW = P.sem("sSW")
    S_GS = P.sem("sGS")
    S_CM = P.sem("sCM")
    S_QK = P.sem("sQK")
    S_VK = P.sem("sVK")
    S_B = P.sem("sB")
    S_C = P.sem("sC")
    S_CW = P.sem("sCW")
    DMAS = (S_WG + S_WU + S_WD + S_QV + S_ST + S_GB + S_OT + S_XC +
            [S_XL, S_SW, S_GS, S_CM, S_QK, S_VK, S_B, S_C, S_CW])

    def dram_in(name, shape):
        return nc.dram_tensor(name, list(shape), F32, kind="ExternalInput").ap()

    xin = dram_in("xin", (NTT, 128, NKC * TT))
    wgt = dram_in("wgt", (NMC, 128, NKC * 128))
    wut = dram_in("wut", (NMC, 128, NKC * 128))
    wdt = dram_in("wdt", (NMG, 4, 128, 2048))
    wqt = dram_in("wqt", (HLOC, 128, NKC * 128))
    wkt = dram_in("wkt", (HLOC, 128, NKC * 128))
    wvt = dram_in("wvt", (HLOC, 128, NKC * 128))
    wsmt = dram_in("wsmt", (128, NKC * 32))
    wft = dram_in("wft", (128, NKC * 8))
    wot = dram_in("wot", (NKC, 128, NKC * 128))
    nbf_d = dram_in("nbf", (8, 1))
    cost_d = dram_in("costab", (128, T))
    sint_d = dram_in("sintab", (128, T))
    caus_d = dram_in("causal_d", (128, 128))
    selm_d = dram_in("selm", (128, 2))
    iden_d = dram_in("ident_d", (128, 128))

    x1m = nc.dram_tensor("x1m", [NTT, 128, NKC * TT], F32).ap()
    x1f = nc.dram_tensor("x1f", [2, NTT, 128, NKC * TT], F32).ap()
    x2m = nc.dram_tensor("x2m", [NTT, 128, NKC * TT], F32).ap()
    qTs = nc.dram_tensor("qTs", [HLOC, 128, T], F32).ap()
    kTs = nc.dram_tensor("kTs", [HLOC, 128, T], F32).ap()
    vKM = nc.dram_tensor("vKM", [HLOC, T, 128], F32).ap()
    cums = nc.dram_tensor("cums", [HLOC, T], F32).ap()
    gsc = nc.dram_tensor("gsc", [32, TT], F32).ap()
    oTm = nc.dram_tensor("oTm", [2, C, T // 2], F32).ap()
    oTr = nc.dram_tensor("oTr", [C, T // 2], F32).ap()
    outx = nc.dram_tensor("outx", [NTT, 128, NKC * TT], F32, kind="ExternalOutput").ap()
    d_u1 = nc.dram_tensor("d_u1", [128, NKC * TT], F32, kind="ExternalOutput").ap() if debug else None
    d_h4 = nc.dram_tensor("d_h4", [128, 4 * TT], F32, kind="ExternalOutput").ap() if debug else None
    d_hpre = nc.dram_tensor("d_hpre", [128, 4 * TT], F32, kind="ExternalOutput").ap() if debug else None
    d_acc = nc.dram_tensor("d_acc", [128, NKC * TT], F32, kind="ExternalOutput").ap() if debug else None
    d_prow = nc.dram_tensor("d_prow", [4, 128, 512], F32, kind="ExternalOutput").ap() if debug else None
    d_pTg = nc.dram_tensor("d_pTg", [128, 4 * 512], F32, kind="ExternalOutput").ap() if debug else None
    dbg = {}
    if debug:
        for nm, src in [("d_x1m", x1m), ("d_qTs", qTs), ("d_kTs", kTs),
                        ("d_vKM", vKM), ("d_cums", cums), ("d_oTm", oTm),
                        ("d_x2m", x2m), ("d_x1f", x1f), ("d_oTr", oTr)]:
            dbg[nm] = (nc.dram_tensor(nm, list(src.tensor.shape), F32,
                                      kind="ExternalOutput").ap(), src)

    st = ExitStack()
    sbuf = lambda name, shape: st.enter_context(nc.sbuf_tensor(name, list(shape), F32))
    psum = lambda name, shape: st.enter_context(nc.psum_tensor(name, list(shape), F32))

    with st:
        ones1 = sbuf("ones1", (1, 128))
        onesK = sbuf("onesK", (128, 1))
        ident = sbuf("ident", (128, 128))
        causal = sbuf("causal", (128, 128))
        nbf_t = sbuf("nbf_t", (8, 1))
        selm = sbuf("selm_t", (128, 2))
        srow = sbuf("srow", (1, 5 * TT))
        epsb = sbuf("epsb", (1, 1))

        ps = [psum(f"ps{i}", (128, 512)) for i in range(6)]
        psT = psum("psT", (64, 512))

        for s in P.sems:
            s.h = st.enter_context(nc.semaphore(s.name))

        ev_ones = P.do("ve", lambda e: e.memset(ones1[:], 1.0), marks=[(VE, 1)])
        ev_eps = P.do("ve", lambda e: e.memset(epsb[:], EPS), marks=[(VE, 1)])
        ev_onesK = P.do("ve", lambda e: e.memset(onesK[:], 1.0), marks=[(VE, 1)])
        for dst, src in [(ident, iden_d), (causal, caus_d), (nbf_t, nbf_d),
                         (selm, selm_d)]:
            P.do("ge", lambda e, d=dst, s=src: e.dma_start(out=d[:], in_=s[:, :]),
                 marks=[(S_C, 16)])
        ev_const = P.snap(S_C, VE)

        def emit_ln(x_src_ap, xt, ut, w0):
            xq0, xq1 = _XSQB[0], _XSQB[1]
            xsqb = (xq0, xq1)
            ev_x = P.do("ge", lambda e: e.dma_start(out=xt[:], in_=x_src_ap),
                        waits=list(w0) + ev_const, marks=[(S_XL, 16)])
            ev_sum = None
            for kc in range(NKC):
                sl = slice(kc * TT, (kc + 1) * TT)
                ev_sum = P.do("te", lambda e, sl=sl, kc=kc: e.matmul(
                    psT[0:1, :], onesK[:, 0:1], xt[:, sl],
                    start=(kc == 0), stop=(kc == NKC - 1)),
                    waits=(list(w0) + [ev_x, ev_onesK] if kc == 0 else []),
                    marks=([(PE, 1)] if kc == NKC - 1 else []))
            prev_mm = None
            ev_sq = None
            for kc in range(NKC):
                sl = slice(kc * TT, (kc + 1) * TT)
                par = kc % 2
                es = P.do("se", lambda e, sl=sl, xq=xsqb[par]: e.activation(
                    xq[:], xt[:, sl], AF.Square),
                    waits=(list(w0) + [ev_x] if kc < 2 else [prev_mm]),
                    marks=[(SE, 1)])
                prev_mm = P.do("te", lambda e, kc=kc, xq=xsqb[par]: e.matmul(
                    psT[32:33, :], onesK[:, 0:1], xq[:],
                    start=(kc == 0), stop=(kc == NKC - 1)),
                    waits=[es], marks=[(PE, 1)])
                if kc == NKC - 1:
                    ev_sq = prev_mm
            e1 = P.do("ve", lambda e: e.tensor_scalar_mul(
                srow[0:1, 0:TT], psT[0:1, :], 1.0 / C),
                waits=list(w0) + [ev_sum], marks=[(VE, 1)])
            e2 = P.do("ve", lambda e: e.tensor_mul(
                srow[0:1, 2 * TT:3 * TT], srow[0:1, 0:TT], srow[0:1, 0:TT]),
                waits=[e1], marks=[(VE, 1)])
            e3 = P.do("ve", lambda e: e.scalar_tensor_tensor(
                srow[0:1, 3 * TT:4 * TT], psT[32:33, :], 1.0 / C,
                srow[0:1, 2 * TT:3 * TT], op0=ALU.mult, op1=ALU.subtract),
                waits=[ev_sq, e2], marks=[(VE, 1)])
            e4 = P.do("se", lambda e: e.activation(
                srow[0:1, 4 * TT:5 * TT], srow[0:1, 3 * TT:4 * TT], AF.Sqrt,
                bias=epsb[0:1, 0:1]), waits=[e3, ev_eps], marks=[(SE, 1)])
            eb1 = P.do("te", lambda e: e.matmul(
                ps[4][:], ones1[0:1, :], srow[0:1, 0:TT], start=True, stop=True),
                waits=[e1, ev_ones], marks=[(PE, 1)])
            e4r = P.do("ve", lambda e: e.reciprocal(
                srow[0:1, TT:2 * TT], srow[0:1, 4 * TT:5 * TT]),
                waits=[e4], marks=[(VE, 1)])
            eb2 = P.do("te", lambda e: e.matmul(
                ps[5][:], ones1[0:1, :], srow[0:1, TT:2 * TT],
                start=True, stop=True), waits=[e4r], marks=[(PE, 1)])
            ev_u = None
            for kc in range(NKC):
                sl = slice(kc * TT, (kc + 1) * TT)
                eu1 = P.do("ve", lambda e, sl=sl: e.tensor_sub(
                    ut[:, sl], xt[:, sl], ps[4][:]),
                    waits=([eb1] if kc == 0 else []), marks=[(VE, 1)])
                ev_u = P.do("ve", lambda e, sl=sl: e.tensor_mul(
                    ut[:, sl], ut[:, sl], ps[5][:]),
                    waits=([eb2, eu1] if kc == 0 else [eu1]), marks=[(VE, 1)])
            return ev_x, ev_u, (e1, e3)

        def emit_ffn(x_src, dst, phase_w0, pfx):
            es = ExitStack()
            sb2 = lambda name, shape: es.enter_context(
                nc.sbuf_tensor(pfx + name, list(shape), F32))
            with es:
                xt = sb2("xt", (128, NKC * TT))
                ut = sb2("ut", (128, NKC * TT))
                acc = sb2("acc", (128, NKC * TT))
                h4b = [sb2(f"h4_{i}", (128, 4 * TT)) for i in range(2)]
                habs = sb2("habs", (128, 4 * TT))
                cmp6 = sb2("cmp6", (128, 6 * TT))
                tsc = sb2("tsc", (128, TT))
                _XSQB[0] = sb2("xsq0", (128, TT))
                _XSQB[1] = sb2("xsq1", (128, TT))
                wgb = [sb2(f"wgb{i}", (128, NKC * 128)) for i in range(2)]
                wub = [sb2(f"wub{i}", (128, NKC * 128)) for i in range(2)]
                wdb = [sb2(f"wdb{i}", (128, 2048)) for i in range(2)]
                x1t = [sb2(f"x1t{i}", (128, TT)) for i in range(2)]

                for tt in range(NTT):
                    w0 = list(phase_w0) if tt == 0 else P.snap(PE, VE, SE, *DMAS)
                    ev_x, ev_u, _ = emit_ln(x_src[tt], xt, ut, w0)
                    ev_w = {}
                    ev_wd = {}
                    ev_g_done = {}
                    ev_up_done = {}
                    ev_sil = {}
                    ev_mul = {}
                    ev_dn_done = {}
                    ev_acc_w = {}
                    ev_ps_free = {}
                    ev_x1_st = {}

                    def pf_gu(mc, w=()):
                        par = mc % 2
                        ev_w[mc] = (
                            P.do("sy", lambda e, mc=mc, par=par: e.dma_start(
                                out=wgb[par][:], in_=wgt[mc]),
                                waits=list(w) + [ev_g_done.get(mc - 2)],
                                marks=[(S_WG[par], 16)]),
                            P.do("sy", lambda e, mc=mc, par=par: e.dma_start(
                                out=wub[par][:], in_=wut[mc]),
                                waits=list(w) + [ev_up_done.get(mc - 2)],
                                marks=[(S_WU[par], 16)]))

                    def pf_dn(k, w=()):
                        mg, coq = k // 4, k % 4
                        par = k % 2
                        ev_wd[k] = P.do("sy", lambda e, mg=mg, coq=coq, par=par:
                                        e.dma_start(out=wdb[par][:], in_=wdt[mg, coq]),
                                        waits=list(w) + [ev_dn_done.get(k - 2)],
                                        marks=[(S_WD[par], 16)])

                    pf_gu(0, w0)
                    pf_gu(1, w0)
                    pf_dn(0, w0)
                    pf_dn(1, w0)
                    prev_em3 = None
                    pe_down = {}
                    ev_hs_by = {}

                    def emit_gusp(mg):
                        h4 = h4b[mg % 2]
                        ev_h = {}
                        for i in range(4):
                            mc = mg * 4 + i
                            par = mc % 2
                            last = None
                            for kc in range(NKC):
                                sl = slice(kc * TT, (kc + 1) * TT)
                                wsl = slice(kc * 128, (kc + 1) * 128)
                                last = P.do("te", lambda e, par=par, sl=sl, wsl=wsl,
                                            kc=kc: e.matmul(
                                    ps[par][:], wgb[par][:, wsl], ut[:, sl],
                                    start=(kc == 0), stop=(kc == NKC - 1)),
                                    waits=([ev_w[mc][0], ev_u,
                                            ev_sil.get(mc - 2)] if kc == 0 else []),
                                    marks=([(PE, 1)] if kc == NKC - 1 else []))
                            ev_g = last
                            ev_g_done[mc] = ev_g
                            for kc in range(NKC):
                                sl = slice(kc * TT, (kc + 1) * TT)
                                wsl = slice(kc * 128, (kc + 1) * 128)
                                last = P.do("te", lambda e, par=par, sl=sl, wsl=wsl,
                                            kc=kc: e.matmul(
                                    ps[2 + par][:], wub[par][:, wsl], ut[:, sl],
                                    start=(kc == 0), stop=(kc == NKC - 1)),
                                    waits=([ev_w[mc][1],
                                            ev_mul.get(mc - 2)] if kc == 0 else []),
                                    marks=([(PE, 1)] if kc == NKC - 1 else []))
                            ev_up = last
                            ev_up_done[mc] = ev_up
                            if mc + 2 < NMC:
                                pf_gu(mc + 2)
                            hsl = slice(i * TT, (i + 1) * TT)
                            ev_sil[mc] = P.do(
                                "se", lambda e, hsl=hsl, par=par, h4=h4: e.activation(
                                    h4[:, hsl], ps[par][:], AF.Silu),
                                waits=[ev_g] + ([pe_down.get(mg - 2), prev_em3]
                                                if i == 0 else []),
                                marks=[(SE, 1)])
                            ev_mul[mc] = P.do(
                                "ve", lambda e, hsl=hsl, par=par, h4=h4: e.tensor_tensor(
                                    h4[:, hsl], h4[:, hsl], ps[2 + par][:],
                                    op=ALU.mult),
                                waits=[ev_sil[mc], ev_up], marks=[(VE, 1)])
                            ev_h[i] = ev_mul[mc]
                        A = [h4[:, i * TT:(i + 1) * TT] for i in range(4)]
                        Ab = [habs[:, i * TT:(i + 1) * TT] for i in range(4)]
                        Cp = [cmp6[:, i * TT:(i + 1) * TT] for i in range(6)]
                        eabs = P.do("se", lambda e, h4=h4: e.activation(habs[:], h4[:], AF.Abs),
                                    waits=[ev_h[0], ev_h[1], ev_h[2], ev_h[3],
                                           prev_em3], marks=[(SE, 1)])
                        pairs = [(0, 1), (0, 2), (0, 3), (1, 2), (1, 3), (2, 3)]
                        ecmp = None
                        for ci, (a, b) in enumerate(pairs):
                            ecmp = P.do("ve", lambda e, ci=ci, a=a, b=b:
                                        e.tensor_tensor(Cp[ci], Ab[a], Ab[b],
                                                        op=ALU.is_ge),
                                        waits=([eabs] if ci == 0 else []),
                                        marks=[(VE, 1)])

                        def keep(i, terms, thr, op, w):
                            e = P.do("ve", lambda e, t=terms: e.tensor_tensor(
                                tsc[:], t[0][1], t[1][1],
                                op=(ALU.add if t[1][0] > 0 else ALU.subtract)),
                                waits=w, marks=[(VE, 1)])
                            for sgn, cc in terms[2:]:
                                e = P.do("ve", lambda e, cc=cc, sgn=sgn:
                                         e.tensor_tensor(
                                             tsc[:], tsc[:], cc,
                                             op=(ALU.add if sgn > 0 else ALU.subtract)),
                                         waits=[e], marks=[(VE, 1)])
                            ek = P.do("ve", lambda e, i=i, thr=thr, op=op:
                                      e.tensor_single_scalar(Ab[i], tsc[:], thr, op=op),
                                      waits=[e], marks=[(VE, 1)])
                            return P.do("ve", lambda e, i=i: e.tensor_mul(
                                A[i], A[i], Ab[i]), waits=[ek], marks=[(VE, 1)])

                        em0 = keep(0, [(1, Cp[0]), (1, Cp[1]), (1, Cp[2])], 2.0,
                                   ALU.is_ge, [ecmp])
                        em1 = keep(1, [(1, Cp[3]), (1, Cp[4]), (-1, Cp[0])], 1.0,
                                   ALU.is_ge, [em0])
                        em2 = keep(2, [(1, Cp[5]), (-1, Cp[1]), (-1, Cp[3])], 0.0,
                                   ALU.is_ge, [em1])
                        em3 = keep(3, [(1, Cp[2]), (1, Cp[4]), (1, Cp[5])], 1.0,
                                   ALU.is_le, [em2])
                        ev_hs_by[mg] = [em0, em1, em2, em3]
                        if debug and pfx == "f1" and tt == 0 and mg == 0:
                            P.do("ge", lambda e, h4=h4: e.dma_start(out=d_u1[:, :], in_=ut[:]),
                                 waits=[ev_u], marks=[(S_C, 16)])
                            P.do("ge", lambda e, h4=h4: e.dma_start(out=d_h4[:, :], in_=h4[:]),
                                 waits=[em3], marks=[(S_C, 16)])
                        return em3

                    def emit_down(mg):
                        h4 = h4b[mg % 2]
                        ev_hs = ev_hs_by[mg]
                        for co in range(NKC):
                            coq, col = co // 4, co % 4
                            k = mg * 4 + coq
                            par = k % 2
                            pd = 4 + (co % 2)
                            last = None
                            for ic in range(4):
                                wsl = slice((col * 4 + ic) * 128,
                                            (col * 4 + ic + 1) * 128)
                                last = P.do("te", lambda e, par=par, wsl=wsl, ic=ic,
                                            pd=pd, h4=h4: e.matmul(
                                    ps[pd][:], wdb[par][:, wsl],
                                    h4[:, ic * TT:(ic + 1) * TT],
                                    start=(ic == 0), stop=(ic == 3)),
                                    waits=(([ev_wd[k]] + ev_hs +
                                            [ev_ps_free.get(pd), ev_u])
                                           if ic == 0 else []),
                                    marks=([(PE, 1)] if ic == 3 else []))
                            ev_dn_done[k] = last
                            csl = slice(co * TT, (co + 1) * TT)
                            if mg == 0:
                                ea = P.do("ve", lambda e, csl=csl, pd=pd:
                                          e.tensor_copy(acc[:, csl], ps[pd][:]),
                                          waits=[last], marks=[(VE, 1)])
                            else:
                                ea = P.do("ve", lambda e, csl=csl, pd=pd:
                                          e.tensor_add(acc[:, csl], acc[:, csl],
                                                       ps[pd][:]),
                                          waits=[last, ev_acc_w.get(co)],
                                          marks=[(VE, 1)])
                            ev_acc_w[co] = ea
                            ev_ps_free[pd] = ea
                            if col == 3 and k + 2 < NMG * 4:
                                pf_dn(k + 2)
                        pe_down[mg] = (PE, PE.n)

                    for mg in range(NMG):
                        prev_em3 = emit_gusp(mg)
                        if mg > 0:
                            emit_down(mg - 1)
                    emit_down(NMG - 1)
                    if debug and pfx == "f1" and tt == 0:
                        P.do("ge", lambda e: e.dma_start(out=d_acc[:, :], in_=acc[:]),
                             waits=[ev_acc_w[k2] for k2 in range(NKC)],
                             marks=[(S_C, 16)])
                    for co in range(NKC):
                        csl = slice(co * TT, (co + 1) * TT)
                        par = co % 2
                        er = P.do("ve", lambda e, csl=csl, par=par:
                                  e.scalar_tensor_tensor(
                                      x1t[par][:], acc[:, csl], 0.5, xt[:, csl],
                                      op0=ALU.mult, op1=ALU.add),
                                  waits=[ev_acc_w[co], ev_x1_st.get(co - 2)],
                                  marks=[(VE, 1)])
                        ev_x1_st[co] = P.do("ge", lambda e, csl=csl, par=par, tt=tt:
                                            e.dma_start(out=dst[tt][:, csl],
                                                        in_=x1t[par][:]),
                                            waits=[er], marks=[(S_ST[par], 16)])

        emit_ffn(xin, x1m, ev_const, "f1")

        ev_p1 = P.snap(*S_ST)
        with allow_ici_all_to_all():
            ev_ag = P.do("ge", lambda e: e.collective_compute(
                "AllGather", ALU.bypass,
                replica_groups=[[0, 1], [2, 3], [4, 5], [6, 7]],
                ins=[x1m], outs=[x1f]),
                waits=ev_p1, marks=[(CC, 1)])

        es2 = ExitStack()
        sb2 = lambda name, shape: es2.enter_context(
            nc.sbuf_tensor(name, list(shape), F32))
        with es2:
            xt2 = sb2("a_xt", (128, NKC * TT))
            ut2 = sb2("a_ut", (128, NKC * TT))
            _XSQB[0] = sb2("a_xsq0", (128, TT))
            _XSQB[1] = sb2("a_xsq1", (128, TT))
            qraw = sb2("a_qraw", (128, HLOC * TT))
            kraw = sb2("a_kraw", (128, HLOC * TT))
            vraw = sb2("a_vraw", (128, HLOC * TT))
            costb = sb2("a_cost", (128, T))
            sintb = sb2("a_sint", (128, T))
            wqb = [sb2(f"a_wq{i}", (128, NKC * 128)) for i in range(2)]
            wsmb = sb2("a_wsm", (128, NKC * 32))
            wfb = sb2("a_wf", (128, NKC * 8))
            sp8 = sb2("a_sp8", (8, TT))
            g24 = sb2("a_g24", (32, TT))
            ncumt = sb2("a_ncum", (8, TT))
            zeros8 = sb2("a_zeros8", (8, TT))
            car = sb2("a_car", (128, 24))
            carcum = sb2("a_carcum", (8, 1))
            sh1 = sb2("a_sh1", (128, TT))
            rt1 = sb2("a_rt1", (128, TT))
            rt2 = sb2("a_rt2", (128, TT))
            swapb = sb2("a_swap", (128, TT))
            vkt = sb2("a_vkt", (128, 4 * 128))
            gateB = [sb2(f"a_gb{i}", (128, TT)) for i in range(2)]

            w_ph = P.snap(PE, VE, SE, *DMAS)
            ev_tbl0 = P.do("ge", lambda e: e.dma_start(out=costb[:], in_=cost_d[:, :]),
                           waits=w_ph, marks=[(S_C, 16)])
            ev_tbl1 = P.do("ge", lambda e: e.dma_start(out=sintb[:], in_=sint_d[:, :]),
                           marks=[(S_C, 16)])
            ev_tbl2 = P.do("sy", lambda e: e.dma_start(out=wsmb[:], in_=wsmt[:, :]),
                           waits=w_ph, marks=[(S_CW, 16)])
            ev_tbl3 = P.do("sy", lambda e: e.dma_start(out=wfb[:], in_=wft[:, :]),
                           marks=[(S_CW, 16)])
            ev_tbl = [ev_tbl1, ev_tbl1, ev_tbl3, ev_tbl3]
            ev_z = [P.do("ve", lambda e: e.memset(zeros8[:], 0.0), waits=w_ph,
                         marks=[(VE, 1)]),
                    P.do("ve", lambda e: e.memset(car[:], 0.0), marks=[(VE, 1)]),
                    P.do("ve", lambda e: e.memset(carcum[:], 0.0), marks=[(VE, 1)])]

            ev_cum_rd = None
            e_sp_prev = None
            ev_vk_st = None
            for g in range(2 * NTT):
                w0 = (w_ph + [ev_ag]) if g == 0 else P.snap(PE, VE, SE, *DMAS)
                ev_x, ev_u, psT_rd = emit_ln(x1f[g // NTT, g % NTT], xt2, ut2, w0)
                ev_raw = {}
                ev_wq = {}
                for im, (wsrc, rawt) in enumerate([(wqt, qraw), (wkt, kraw),
                                                   (wvt, vraw)]):
                    for mh in range(HLOC):
                        idx = im * HLOC + mh
                        par = idx % 2
                        ev_wq[idx] = P.do(
                            "sy", lambda e, wsrc=wsrc, mh=mh, par=par: e.dma_start(
                                out=wqb[par][:], in_=wsrc[mh]),
                            waits=(w0 if idx < 2 else []) +
                            [ev_raw.get(idx - 2)], marks=[(S_QV[par], 16)])
                        last = None
                        for kc in range(NKC):
                            sl = slice(kc * TT, (kc + 1) * TT)
                            wsl = slice(kc * 128, (kc + 1) * 128)
                            last = P.do("te", lambda e, par=par, sl=sl, wsl=wsl,
                                        kc=kc, q=idx % 2: e.matmul(
                                ps[q][:], wqb[par][:, wsl], ut2[:, sl],
                                start=(kc == 0), stop=(kc == NKC - 1)),
                                waits=([ev_wq[idx], ev_u, ev_raw.get(idx - 2)]
                                       if kc == 0 else []),
                                marks=([(PE, 1)] if kc == NKC - 1 else []))
                        hs = slice(mh * TT, (mh + 1) * TT)
                        ev_raw[idx] = P.do(
                            "ve", lambda e, rawt=rawt, hs=hs, q=idx % 2:
                            e.tensor_copy(rawt[:, hs], ps[q][:]),
                            waits=[last], marks=[(VE, 1)])
                last = None
                for kc in range(NKC):
                    sl = slice(kc * TT, (kc + 1) * TT)
                    wsl = slice(kc * 32, (kc + 1) * 32)
                    last = P.do("te", lambda e, sl=sl, wsl=wsl, kc=kc: e.matmul(
                        psT[0:32, :], wsmb[:, wsl], ut2[:, sl],
                        start=(kc == 0), stop=(kc == NKC - 1)),
                        waits=([ev_tbl[2], ev_u, psT_rd[0], psT_rd[1]]
                               if kc == 0 else []),
                        marks=([(PE, 1)] if kc == NKC - 1 else []))
                lastf = None
                for kc in range(NKC):
                    sl = slice(kc * TT, (kc + 1) * TT)
                    wsl = slice(kc * 8, (kc + 1) * 8)
                    lastf = P.do("te", lambda e, sl=sl, wsl=wsl, kc=kc: e.matmul(
                        ps[2][0:8, :], wfb[:, wsl], ut2[:, sl],
                        start=(kc == 0), stop=(kc == NKC - 1)),
                        waits=([ev_tbl[3], ev_u, e_sp_prev] if kc == 0 else []),
                        marks=([(PE, 1)] if kc == NKC - 1 else []))
                e_sg = P.do("se", lambda e: e.activation(
                    g24[0:32, :], psT[0:32, :], AF.Sigmoid),
                    waits=[last], marks=[(SE, 1)])
                e_spa = P.do("se", lambda e: e.activation(
                    sp8[:, :], ps[2][0:8, :], AF.Exp,
                    bias=nbf_t[0:8, 0:1], scale=-1.0),
                    waits=[lastf], marks=[(SE, 1)])
                e_spb = P.do("ve", lambda e: e.tensor_scalar_add(
                    sp8[:, :], sp8[:, :], 1.0), waits=[e_spa], marks=[(VE, 1)])
                e_sp = P.do("se", lambda e: e.activation(
                    sp8[:, :], sp8[:, :], AF.Ln), waits=[e_spb], marks=[(SE, 1)])
                e_sp_prev = e_spa
                e_scan = P.do("ve", lambda e: e.tensor_tensor_scan(
                    ncumt[:], sp8[:, :], zeros8[:], carcum[:, 0:1],
                    op0=ALU.add, op1=ALU.add),
                    waits=[e_sp, ev_z[0], ev_z[2], ev_cum_rd], marks=[(VE, 1)])
                e_carc = P.do("ve", lambda e: e.tensor_copy(
                    carcum[:, 0:1], ncumt[:, TT - 1:TT]), waits=[e_scan],
                    marks=[(VE, 1)])
                ev_cum_rd = P.do("ge", lambda e, g=g: e.dma_start(
                    out=cums[:, g * TT:(g + 1) * TT], in_=ncumt[:]),
                    waits=[e_carc], marks=[(S_CM, 16)])
                e_g2d = P.do("ge", lambda e: e.dma_start(out=gsc[:, :], in_=g24[:]),
                             waits=[e_sg, e_sp], marks=[(S_GS, 16)])
                gate_reads = []
                for mh in range(HLOC):
                    hs = slice(mh * TT, (mh + 1) * TT)
                    for im, rawt in enumerate([qraw, kraw, vraw]):
                        r = im * 8 + mh
                        ci = slice(r, r + 1)
                        ev_y = ev_raw[im * HLOC + mh]
                        ec1 = P.do("ve", lambda e, rawt=rawt, hs=hs: e.tensor_copy(
                            sh1[:, 1:TT], rawt[:, hs.start:hs.stop - 1]),
                            waits=[ev_y] +
                            ([gate_reads[-1]] if gate_reads else []),
                            marks=[(VE, 1)])
                        ec2 = P.do("ve", lambda e, ci=ci: e.tensor_copy(
                            sh1[:, 0:1], car[:, ci]),
                            waits=[ec1, ev_z[1]], marks=[(VE, 1)])
                        ec3 = P.do("ve", lambda e, ci=ci, rawt=rawt, hs=hs:
                                   e.tensor_copy(car[:, ci],
                                                 rawt[:, hs.stop - 1:hs.stop]),
                                   waits=[ev_y, ec2], marks=[(VE, 1)])
                        ec4 = P.do("ve", lambda e, rawt=rawt, hs=hs: e.tensor_sub(
                            sh1[:], sh1[:], rawt[:, hs]), waits=[ec3],
                            marks=[(VE, 1)])
                        pq = r % 2
                        eb = P.do("ge", lambda e, r=r, pq=pq: e.dma_start(
                            out=gateB[pq][:],
                            in_=bass.AP(tensor=gsc.tensor, offset=r * TT,
                                        ap=[[0, 128], [1, TT]])),
                            waits=[e_g2d] +
                            ([gate_reads[-2]] if len(gate_reads) >= 2 else []),
                            marks=[(S_GB[pq], 16)])
                        ec5 = P.do("ve", lambda e, pq=pq: e.tensor_mul(
                            sh1[:], sh1[:], gateB[pq][:]), waits=[ec4, eb],
                            marks=[(VE, 1)])
                        gate_reads.append(ec5)
                        ec6 = P.do("ve", lambda e, rawt=rawt, hs=hs: e.tensor_add(
                            rawt[:, hs], rawt[:, hs], sh1[:]), waits=[ec5],
                            marks=[(VE, 1)])
                        if im < 2:
                            es1 = P.do("ge", lambda e, rawt=rawt, hs=hs: e.dma_start(
                                out=swapb[0:64, :], in_=rawt[64:128, hs]),
                                waits=[ec6], marks=[(S_SW, 16)])
                            es2_ = P.do("ge", lambda e, rawt=rawt, hs=hs: e.dma_start(
                                out=swapb[64:128, :], in_=rawt[0:64, hs]),
                                waits=[ec6], marks=[(S_SW, 16)])
                            csl = slice(g * TT, (g + 1) * TT)
                            er1 = P.do("ve", lambda e, rawt=rawt, hs=hs, csl=csl:
                                       e.tensor_mul(rt1[:], rawt[:, hs],
                                                    costb[:, csl]),
                                       waits=[ec6] +
                                       ([ev_tbl[0]] if g == 0 else []),
                                       marks=[(VE, 1)])
                            er2 = P.do("ve", lambda e, csl=csl: e.tensor_mul(
                                rt2[:], swapb[:], sintb[:, csl]),
                                waits=[es1, es2_] +
                                ([ev_tbl[1]] if g == 0 else []),
                                marks=[(VE, 1)])
                            er3 = P.do("ve", lambda e, rawt=rawt, hs=hs:
                                       e.tensor_add(rawt[:, hs], rt1[:], rt2[:]),
                                       waits=[er1, er2], marks=[(VE, 1)])
                            dstd = qTs if im == 0 else kTs
                            P.do("ge", lambda e, d=dstd, mh=mh, hs=hs, g=g,
                                 rawt=rawt: e.dma_start(
                                out=d[mh][:, g * TT:(g + 1) * TT],
                                in_=rawt[:, hs]),
                                waits=[er3], marks=[(S_QK, 16)])
                        else:
                            evt = None
                            for blk in range(4):
                                evt = P.do("te", lambda e, hs=hs, blk=blk:
                                           e.transpose(
                                    ps[5][:, blk * 128:(blk + 1) * 128],
                                    vraw[:, hs.start + blk * 128:
                                         hs.start + (blk + 1) * 128],
                                    ident[:]),
                                    waits=([ec6, ev_u] if blk == 0 else []),
                                    marks=([(PE, 1)] if blk == 3 else []))
                            evk = P.do("ve", lambda e: e.tensor_copy(
                                vkt[:], ps[5][:]),
                                waits=[evt, ev_vk_st], marks=[(VE, 1)])
                            ev_vk_st = P.do("ge", lambda e, mh=mh, g=g: e.dma_start(
                                out=bass.AP(tensor=vKM.tensor,
                                            offset=(mh * T + g * TT) * 128,
                                            ap=[[128, 128], [128 * 128, 4],
                                                [1, 128]]),
                                in_=bass.AP(tensor=vkt.ap().tensor, offset=0,
                                            ap=[[512, 128], [128, 4], [1, 128]])),
                                waits=[evk], marks=[(S_VK, 16)])

        es3 = ExitStack()
        sb3 = lambda name, shape: es3.enter_context(
            nc.sbuf_tensor(name, list(shape), F32))
        with es3:
            qTt = sb3("b_qT", (128, T))
            kTt = sb3("b_kT", (128, T))
            vKt = sb3("b_vK", (128, T))
            cumB = sb3("b_cumB", (128, T))
            cumr = sb3("b_cumr", (1, T))
            prow2 = [sb3(f"b_prow{i}", (128, T)) for i in range(2)]
            pTg = sb3("b_pTg", (128, 32 * 512))
            mrow = sb3("b_mrow", (128, 4))
            oTt = [sb3(f"b_oT{i}", (128, 512)) for i in range(2)]

            w_ph2 = P.snap(PE, VE, SE, *DMAS)
            ev_exp_done = None
            for h in range(HLOC):
                w0 = w_ph2 if h == 0 else P.snap(PE, VE, SE, *DMAS)
                eq = P.do("sy", lambda e, h=h: e.dma_start(out=qTt[:], in_=qTs[h]),
                          waits=w0, marks=[(S_B, 16)])
                ek = P.do("sy", lambda e, h=h: e.dma_start(out=kTt[:], in_=kTs[h]),
                          marks=[(S_B, 16)])
                evv = P.do("sy", lambda e, h=h: e.dma_start(
                    out=bass.AP(tensor=vKt.ap().tensor, offset=0,
                                ap=[[T, 128], [128, 32], [1, 128]]),
                    in_=bass.AP(tensor=vKM.tensor, offset=h * T * 128,
                                ap=[[128, 128], [128 * 128, 32], [1, 128]])),
                    marks=[(S_B, 16)])
                ecr = P.do("sy", lambda e, h=h: e.dma_start(
                    out=cumr[0:1, :], in_=cums[h:h + 1, :]), marks=[(S_B, 16)])
                eq = ek = evv = ecr
                ev_cb = []
                for kci in range(8):
                    sl = slice(kci * 512, (kci + 1) * 512)
                    pq = kci % 2
                    ebm = P.do("te", lambda e, sl=sl, pq=pq: e.matmul(
                        ps[pq][:], ones1[0:1, :], cumr[0:1, sl],
                        start=True, stop=True),
                        waits=(w0 if kci == 0 else []) +
                        [ecr] + ([ev_cb[kci - 2]] if kci >= 2 else []),
                        marks=[(PE, 1)])
                    ev_cb.append(P.do("ve", lambda e, sl=sl, pq=pq: e.tensor_copy(
                        cumB[:, sl], ps[pq][:]),
                        waits=[ebm] + (w0 if kci == 0 else []), marks=[(VE, 1)]))
                ev_ps_s = {0: ev_cb[6], 1: ev_cb[7], 2: None, 3: None}
                ev_pT_rd = None
                ev_oT_st = {}
                ev_oT_ev = {}
                ev_tr_free = None
                ev_prow_free = [None, None]
                ev_exp_by = [None, None]
                ev_edv = {}
                ev_pT_w = []
                first_mm = True

                def emit_s1s2(qb):
                    nonlocal first_mm
                    par = qb % 2
                    prow = prow2[par]
                    klen = (qb + 1) * 128
                    nk5 = (klen + 511) // 512
                    ev_evac = None
                    for kci in range(nk5):
                        n = min(512, klen - kci * 512)
                        pq = kci % 4
                        sl = slice(kci * 512, kci * 512 + n)
                        em = P.do("te", lambda e, qb=qb, sl=sl, pq=pq, n=n:
                                  e.matmul(ps[pq][:, 0:n],
                                           qTt[:, qb * 128:(qb + 1) * 128],
                                           kTt[:, sl], start=True, stop=True),
                                  waits=([eq, ek] if first_mm else []) +
                                  [ev_ps_s.get(pq)],
                                  marks=[(PE, 1)])
                        first_mm = False
                        ev_evac = P.do("ve", lambda e, sl=sl, pq=pq, n=n, prow=prow:
                                       e.scalar_tensor_tensor(
                                           prow[:, sl], ps[pq][:, 0:n], RSCALE,
                                           cumB[:, sl], op0=ALU.mult,
                                           op1=ALU.add),
                                       waits=[em, ev_cb[7], ev_exp_by[par],
                                              ev_prow_free[par] if kci == 0
                                              else None],
                                       marks=[(VE, 1)])
                        ev_ps_s[pq] = ev_evac
                    edg = P.do("ve", lambda e, klen=klen, prow=prow: e.tensor_add(
                        prow[:, klen - 128:klen], prow[:, klen - 128:klen],
                        causal[:]), waits=[ev_evac], marks=[(VE, 1)])
                    enm = P.do("ve", lambda e, klen=klen, prow=prow, par=par:
                               e.reduce_max(
                        mrow[:, par:par + 1], prow[:, 0:klen], axis=AX.X,
                        negate=True), waits=[edg], marks=[(VE, 1)])
                    eex = P.do("se", lambda e, klen=klen, prow=prow, par=par:
                               e.activation(
                        prow[:, 0:klen], prow[:, 0:klen], AF.Exp,
                        bias=mrow[:, par:par + 1],
                        accum_out=mrow[:, 2 + par:3 + par]),
                        waits=[enm], marks=[(SE, 1)])
                    erc = P.do("ve", lambda e, par=par: e.reciprocal(
                        mrow[:, 2 + par:3 + par], mrow[:, 2 + par:3 + par]),
                        waits=[eex], marks=[(VE, 1)])
                    ev_edv[qb] = P.do("se", lambda e, klen=klen, prow=prow, par=par:
                                      e.mul(
                        prow[:, 0:klen], prow[:, 0:klen],
                        mrow[:, 2 + par:3 + par]),
                        waits=[erc], marks=[(SE, 1)])
                    ev_exp_by[par] = ev_edv[qb]

                def emit_s3(qb):
                    nonlocal ev_tr_free
                    par = qb % 2
                    prow = prow2[par]
                    klen = (qb + 1) * 128
                    i = qb % 4
                    nkc = klen // 128
                    ev_tr_last = None
                    for kc0 in range(0, nkc, 4):
                        nblk = min(4, nkc - kc0)
                        evt = None
                        for bb in range(nblk):
                            kc = kc0 + bb
                            evt = P.do("te", lambda e, kc=kc, bb=bb, prow=prow:
                                       e.transpose(
                                ps[4][:, bb * 128:(bb + 1) * 128],
                                prow[:, kc * 128:(kc + 1) * 128], ident[:]),
                                waits=([ev_edv[qb], ev_tr_free, ev_pT_rd]
                                       if bb == 0 else []),
                                marks=([(PE, 1)] if bb == nblk - 1 else []))
                        etv = P.do("ve", lambda e, kc0=kc0, i=i, nblk=nblk:
                                   e.tensor_copy(
                                       bass.AP(tensor=pTg.ap().tensor,
                                               offset=kc0 * 512 + i * 128,
                                               ap=[[32 * 512, 128],
                                                   [512, nblk], [1, 128]]),
                                       ps[4][:, 0:nblk * 128]),
                                   waits=[evt], marks=[(VE, 1)])
                        ev_tr_free = etv
                        ev_tr_last = evt
                        ev_pT_w.append(etv)
                    ev_prow_free[par] = ev_tr_last

                def emit_pads(qg):
                    evs = []
                    for i in range(3):
                        qb = qg * 4 + i
                        cnt = 3 - i
                        evs.append(P.do("ve", lambda e, qb=qb, i=i, cnt=cnt:
                                        e.memset(bass.AP(
                                            tensor=pTg.ap().tensor,
                                            offset=(qb + 1) * 512 + i * 128,
                                            ap=[[32 * 512, 128], [512, cnt],
                                                [1, 128]]),
                                            0.0),
                                        waits=([ev_pT_rd] if i == 0 else []),
                                        marks=[(VE, 1)]))
                    return evs

                def emit_pv(qg):
                    nonlocal ev_pT_rd, ev_pT_w
                    nkc_g = 4 * qg + 4
                    wv = [evv, ev_oT_ev.get(qg - 1)] + ev_pad_by[qg] + ev_pT_w
                    last = None
                    for kc in range(nkc_g):
                        last = P.do("te", lambda e, kc=kc, n=nkc_g: e.matmul(
                            ps[5][:], vKt[:, kc * 128:(kc + 1) * 128],
                            pTg[:, kc * 512:(kc + 1) * 512],
                            start=(kc == 0), stop=(kc == n - 1)),
                            waits=(wv if kc == 0 else []),
                            marks=([(PE, 1)] if kc == nkc_g - 1 else []))
                    ev_pT_rd = last
                    ev_pT_w = []
                    par = qg % 2
                    shard = qg // 4
                    colr = (qg * 512) % 2048
                    eo0 = P.do("ve", lambda e, par=par: e.tensor_single_scalar(
                        oTt[par][:], ps[5][:], selm[:, 0:1], op=ALU.mult),
                        waits=[last, ev_oT_st.get(qg - 2)], marks=[(VE, 1)])
                    es0 = P.do("ge", lambda e, h=h, shard=shard, colr=colr,
                               par=par: e.dma_start(
                        out=oTm[shard][h * 128:(h + 1) * 128, colr:colr + 512],
                        in_=oTt[par][:]), waits=[eo0], marks=[(S_OT[par], 16)])
                    eo1 = P.do("ve", lambda e, par=par: e.tensor_single_scalar(
                        oTt[par][:], ps[5][:], selm[:, 1:2], op=ALU.mult),
                        waits=[es0], marks=[(VE, 1)])
                    ev_oT_ev[qg] = eo1
                    ev_oT_st[qg] = P.do("ge", lambda e, h=h, shard=shard,
                                        colr=colr, par=par: e.dma_start(
                        out=oTm[shard][1024 + h * 128:1024 + (h + 1) * 128,
                                       colr:colr + 512],
                        in_=oTt[par][:]), waits=[eo1], marks=[(S_OT[par], 16)])

                ev_pad_by = {}
                for qb in range(32):
                    emit_s1s2(qb)
                    if qb >= 1:
                        emit_s3(qb - 1)
                        if (qb - 1) % 4 == 3:
                            emit_pv((qb - 1) // 4)
                    if qb % 4 == 0:
                        ev_pad_by[qb // 4] = emit_pads(qb // 4)
                emit_s3(31)
                emit_pv(7)
        ev_p2b = P.snap(*S_OT, S_VK, S_QK, S_CM)
        ev_a2a = P.do("ge", lambda e: e.collective_compute(
            "ReduceScatter", ALU.add,
            replica_groups=[[0, 1], [2, 3], [4, 5], [6, 7]],
            ins=[oTm], outs=[oTr]),
            waits=ev_p2b, marks=[(CC, 1)])

        es4 = ExitStack()
        sb4 = lambda name, shape: es4.enter_context(
            nc.sbuf_tensor(name, list(shape), F32))
        with es4:
            ot = sb4("c_ot", (128, NKC * TT))
            wob = [sb4(f"c_wo{i}", (128, NKC * 128)) for i in range(2)]
            x1c = [sb4(f"c_x1{i}", (128, TT)) for i in range(2)]
            x2t = [sb4(f"c_x2{i}", (128, TT)) for i in range(2)]

            w_ph3 = P.snap(PE, VE, SE, *DMAS)
            ev_x2w = {}
            ev_x2st = {}
            for tt in range(NTT):
                w0 = (w_ph3 + [ev_a2a]) if tt == 0 else P.snap(PE, VE, SE, *DMAS)
                ev_ot = [P.do("ge", lambda e, tt=tt: e.dma_start(
                    out=bass.AP(tensor=ot.ap().tensor, offset=0,
                                ap=[[NKC * TT, 128], [TT, 16], [1, TT]]),
                    in_=bass.AP(tensor=oTr.tensor, offset=tt * TT,
                                ap=[[T // 2, 128], [128 * (T // 2), 16],
                                    [1, TT]])),
                    waits=list(w0) + [ev_a2a], marks=[(S_XL, 16)])]
                ev_ot.append(ev_ot[0])
                ev_wo = {}
                ev_mm = {}
                for mc in range(NKC):
                    par = mc % 2
                    ev_wo[mc] = P.do("sy", lambda e, mc=mc, par=par: e.dma_start(
                        out=wob[par][:], in_=wot[mc]),
                        waits=(list(w0) if mc < 2 else []) + [ev_mm.get(mc - 2)],
                        marks=[(S_QV[par], 16)])
                    exc = P.do("ge", lambda e, mc=mc, par=par, tt=tt: e.dma_start(
                        out=x1c[par][:], in_=x1m[tt][:, mc * TT:(mc + 1) * TT]),
                        waits=(list(w0) if mc < 2 else []) + [ev_x2w.get(mc - 2)],
                        marks=[(S_XC[par], 16)])
                    last = None
                    for kc in range(NKC):
                        wsl = slice(kc * 128, (kc + 1) * 128)
                        last = P.do("te", lambda e, par=par, wsl=wsl, kc=kc, mc=mc:
                                    e.matmul(ps[mc % 2][:], wob[par][:, wsl],
                                             ot[:, kc * TT:(kc + 1) * TT],
                                             start=(kc == 0), stop=(kc == NKC - 1)),
                                    waits=([ev_wo[mc], ev_ot[0], ev_ot[1],
                                            ev_x2w.get(mc - 2)] if kc == 0 else []),
                                    marks=([(PE, 1)] if kc == NKC - 1 else []))
                    ev_mm[mc] = last
                    ev_x2w[mc] = P.do("ve", lambda e, par=par, mc=mc: e.tensor_add(
                        x2t[par][:], ps[mc % 2][:], x1c[par][:]),
                        waits=[last, exc, ev_x2st.get(mc - 2)], marks=[(VE, 1)])
                    ev_x2st[mc] = P.do("ge", lambda e, par=par, mc=mc, tt=tt:
                                       e.dma_start(
                        out=x2m[tt][:, mc * TT:(mc + 1) * TT], in_=x2t[par][:]),
                        waits=[ev_x2w[mc]], marks=[(S_ST[par], 16)])

        emit_ffn(x2m, outx, P.snap(PE, VE, SE, *DMAS), "f3")

        if debug:
            wdbg = P.snap(VE, PE, SE, *DMAS)
            for nm, (dap, src) in dbg.items():
                P.do("ge", lambda e, d=dap, s=src: e.dma_start(out=d, in_=s),
                     waits=wdbg, marks=[(S_C, 16)])

        P.do("ge", lambda e: None, waits=P.snap(CC, PE, VE, SE, *DMAS))

        with nc.Block() as block:
            for eng_name, starter in [("te", block.tensor), ("ve", block.vector),
                                      ("se", block.scalar), ("ge", block.gpsimd),
                                      ("sy", block.sync)]:
                def mk(name):
                    def f(eng):
                        done = {}
                        for o in P.ops:
                            if o.eng != name:
                                continue
                            byname = {}
                            for s, v in o.waits:
                                if v > byname.get(s.name, 0):
                                    byname[s.name] = v
                                    byname[s.name + "_s"] = s
                            for snm in list(byname):
                                if snm.endswith("_s"):
                                    continue
                                v = byname[snm]
                                if done.get(snm, -1) >= v:
                                    continue
                                eng.wait_ge(byname[snm + "_s"].h, v)
                                done[snm] = v
                            inst = o.emit(eng)
                            if inst is not None:
                                for s, inc in o.marks:
                                    inst.then_inc(s.h, inc)
                    return f
                starter(mk(eng_name))
    return nc


def _tile_feat(xt):
    ntok = xt.shape[1]
    n = ntok // TT
    return np.ascontiguousarray(
        xt.reshape(NKC, 128, n, TT).transpose(2, 1, 0, 3).reshape(n, 128, NKC * TT))


def _tile_lhsT(wT, nm):
    K, M = wT.shape
    nk = K // 128
    return np.ascontiguousarray(
        wT.reshape(nk, 128, nm, 128).transpose(2, 1, 0, 3).reshape(nm, 128, nk * 128))


def _host_inputs(L0_attn_x, L0_attn_Wq, L0_attn_Wsq, L0_attn_Wk, L0_attn_Wsk,
                 L0_attn_Wv, L0_attn_Wsv, L0_attn_Wf, L0_attn_bf, L0_attn_Wo,
                 L0_ffn_w_gate, L0_ffn_w_up, L0_ffn_w_down):
    x = np.ascontiguousarray(L0_attn_x, dtype=np.float32)
    perm = _perm_I()
    wg_p = np.asarray(L0_ffn_w_gate, np.float32)[:, perm]
    wu_p = np.asarray(L0_ffn_w_up, np.float32)[:, perm]
    wd_p = np.asarray(L0_ffn_w_down, np.float32)[perm, :]

    wgt = _tile_lhsT(wg_p, NMC)
    wut = _tile_lhsT(wu_p, NMC)
    wdt = np.ascontiguousarray(
        wd_p.reshape(NMG, 4, 128, 4, 4, 128).transpose(0, 3, 2, 4, 1, 5)
        .reshape(NMG, 4, 128, 2048))
    wot_full = _tile_lhsT(
        np.ascontiguousarray(np.asarray(L0_attn_Wo, np.float32).T), NKC)

    cos, sin = _rope_tables()
    costab = np.ascontiguousarray(np.concatenate([cos.T, cos.T], axis=0))
    sintab = np.ascontiguousarray(np.concatenate([sin.T, -sin.T], axis=0))
    qi, ki = np.meshgrid(np.arange(128), np.arange(128), indexing="ij")
    causal = np.where(ki <= qi, 0.0, NEG).astype(np.float32)
    ident = np.eye(128, dtype=np.float32)

    in_maps = []
    for c in range(8):
        b, j = c // 2, c % 2
        hsl = slice(j * HLOC, (j + 1) * HLOC)
        rsl = slice(j * HLOC * D, (j + 1) * HLOC * D)
        wq = np.ascontiguousarray(np.asarray(L0_attn_Wq, np.float32)[rsl].T)
        wk = np.ascontiguousarray(np.asarray(L0_attn_Wk, np.float32)[rsl].T)
        wv = np.ascontiguousarray(np.asarray(L0_attn_Wv, np.float32)[rsl].T)
        S = np.zeros((32, C), np.float32)
        S[0:8] = np.asarray(L0_attn_Wsq, np.float32)[hsl]
        S[8:16] = np.asarray(L0_attn_Wsk, np.float32)[hsl]
        S[16:24] = np.asarray(L0_attn_Wsv, np.float32)[hsl]
        wsmt = np.ascontiguousarray(
            S.T.reshape(NKC, 128, 32).transpose(1, 0, 2).reshape(128, NKC * 32))
        Sf = np.asarray(L0_attn_Wf, np.float32)[hsl]
        wft = np.ascontiguousarray(
            Sf.T.reshape(NKC, 128, 8).transpose(1, 0, 2).reshape(128, NKC * 8))
        nbf = -np.asarray(L0_attn_bf, np.float32)[hsl].reshape(8, 1)
        selmh = np.zeros((128, 2), np.float32)
        selmh[:, j] = 1.0
        xmine = _tile_feat(np.ascontiguousarray(x[b, j * 2048:(j + 1) * 2048, :].T))
        in_maps.append({
            "xin": xmine, "wgt": wgt, "wut": wut, "wdt": wdt,
            "wqt": _tile_lhsT(wq, HLOC), "wkt": _tile_lhsT(wk, HLOC),
            "wvt": _tile_lhsT(wv, HLOC), "wsmt": wsmt, "wft": wft, "wot": wot_full,
            "nbf": nbf, "costab": costab, "sintab": sintab,
            "causal_d": causal, "ident_d": ident, "selm": selmh,
        })
    return in_maps


def kernel(**inputs):
    in_maps = _host_inputs(**inputs)
    nc = build_nc()
    res = run_bass_kernel_spmd(nc, in_maps, core_ids=list(range(8)))
    out = np.empty((B, T, C), np.float32)
    for c in range(8):
        b, j = c // 2, c % 2
        o = np.asarray(res.results[c]["outx"])
        o = o.reshape(NTT, 128, NKC, TT).transpose(0, 3, 2, 1).reshape(2048, C)
        out[b, j * 2048:(j + 1) * 2048, :] = o
    return out
